# version 04
# baseline (speedup 1.0000x reference)
import sys
sys.path.insert(0, "/opt/trn_rl_repo")
import numpy as np
import concourse.bass as bass
import concourse.mybir as mybir
from contextlib import ExitStack

F32 = mybir.dt.float32
BF16 = mybir.dt.bfloat16
I32 = mybir.dt.int32
I8 = mybir.dt.int8
AF = mybir.ActivationFunctionType
ALU = mybir.AluOpType
AX = mybir.AxisListType

H, KVH, D, CHUNK = 32, 8, 128, 128
DBG_HN = None
DBG_OB = None
PARANOID_S4 = False
HID = 4096
FFH = 11008
T = 1024
NCH = 8
NOB = FFH // 128

ENGS = ("pe", "dv", "ac", "gp", "sy")
ENG_ATTR = {"pe": "tensor", "dv": "vector", "ac": "scalar", "gp": "gpsimd", "sy": "sync"}


class Bld:

    def __init__(self, nc):
        self.nc = nc
        self.ops = {e: [] for e in ENGS}
        self.cnt = {}
        self.waited = {}
        self.sems = {}

    def _dedup(self, eng, evs):
        out = []
        for ev in evs:
            if ev is None:
                continue
            s, v = ev
            if v <= 0:
                continue
            key = (eng, s)
            if self.waited.get(key, 0) >= v:
                continue
            self.waited[key] = v
            out.append((s, v))
        return out

    def op(self, eng, fn, waits=(), inc=None, amt=1):
        w = self._dedup(eng, list(waits))
        ev = None
        if inc is not None:
            self.cnt[inc] = self.cnt.get(inc, 0) + amt
            ev = (inc, self.cnt[inc])
        self.ops[eng].append((w, fn, inc, amt))
        return ev

    def dma(self, queue, out, in_, waits=(), sem=None, **kw):
        assert sem is not None, "stream semaphore required"
        if sem not in self.cnt:
            self.sems[sem] = self.mksem(sem)
            self.cnt[sem] = 0
        w = list(waits) + [(sem, self.cnt[sem])]
        return self.op(queue, lambda e: e.dma_start(out=out, in_=in_, **kw),
                       waits=w, inc=sem, amt=16)

    def dma_sems_now(self):
        return [self.now(s) for s in self.cnt if s.startswith("d_") or s == "sto"]

    def now(self, sem):
        return (sem, self.cnt.get(sem, 0))

    def barrier(self):
        evs = [self.now(s) for s in self.cnt]
        for e in ENGS:
            w = self._dedup(e, evs)
            if w:
                self.ops[e].append((w, None, None, 0))

    def play(self, blk):
        handles = self.sems
        for eng in ENGS:
            ops = self.ops[eng]
            if not ops:
                continue

            def body(h, ops=ops):
                for (waits, fn, inc, amt) in ops:
                    for (s, v) in waits:
                        h.wait_ge(handles[s], v)
                    if fn is None:
                        continue
                    inst = fn(h)
                    if inc is not None:
                        inst.then_inc(handles[inc], amt)
            getattr(blk, ENG_ATTR[eng])(body)


def ap2(t, part0, nparts, free_ap, off_elems=0):
    a = t.ap()
    return bass.AP(a.tensor, a.offset + part0 * a.ap[0][0] + off_elems,
                   [[a.ap[0][0], nparts]] + [list(x) for x in free_ap])


def apview(ap, part0, nparts, off_elems, free_ap):
    return bass.AP(ap.tensor, ap.offset + part0 * ap.ap[0][0] + off_elems,
                   [[ap.ap[0][0], nparts]] + [list(x) for x in free_ap])


def build(debug_names=(), stages=99):
    nc = bass.Bass("TRN2", target_bir_lowering=False)
    bld = Bld(nc)
    dbg_tensors = {}

    def din(name, shape, dt=F32):
        return nc.dram_tensor(name, shape, dt, kind="ExternalInput").ap()

    xmain_h = din("xmain", [HID, T])
    xpre_h = din("xpre", [HID, T])
    wqT_h = din("wqT", [HID, HID])
    wkT_h = din("wkT", [HID, KVH * D])
    wvT_h = din("wvT", [HID, KVH * D])
    wgT_h = din("wgT", [HID, HID])
    woT_h = din("woT", [HID, HID])
    winT_h = din("winT", [HID, H])
    dtb_h = din("dtb", [H, 1])
    alog_h = din("alog", [H, 1])
    gnorm_h = din("gnorm", [D, 1])
    cosm_h = din("cosm", [D, T])
    sinm_h = din("sinm", [64, T])
    cosp_h = din("cosp", [D, T])
    sinp_h = din("sinp", [64, T])
    maskneg_h = din("maskneg", [CHUNK, CHUNK])
    ident_h = din("ident", [128, 128])
    sel2_h = din("sel2", [2, 128])
    w1d_h = din("w1d", [NOB * 128, 4096], I8)
    w3d_h = din("w3d", [NOB * 128, 4096], I8)
    w2d_h = din("w2d", [32 * 128, FFH], I8)
    s1re_h = din("s1re", [NOB * 2, 4096])
    s3re_h = din("s3re", [NOB * 2, 4096])
    s2re_h = din("s2re", [32 * 2, FFH])

    outT_h = nc.dram_tensor("outT", [HID, T], F32, kind="ExternalOutput").ap()

    y_s = nc.dram_tensor("y_s", [H * D, T], BF16, kind="Internal").ap()
    h_s = nc.dram_tensor("h_s", [NOB * 128, T], BF16, kind="Internal").ap()
    xm_s = nc.dram_tensor("xm_s", [HID, T], BF16, kind="Internal").ap()
    r2_s = nc.dram_tensor("r2_s", [HID, T], BF16, kind="Internal").ap()

    def dbg(name, src_ap, shape, dt=F32):
        if name not in debug_names:
            return
        dt = src_ap.tensor.dtype
        hh = nc.dram_tensor("dbg_" + name, shape, dt, kind="ExternalOutput").ap()
        dbg_tensors[name] = "dbg_" + name
        bld.dma("sy", hh, src_ap, sem="d_dbg",
                waits=[bld.now("dv"), bld.now("ac"), bld.now("pe"), bld.now("gp")]
                + bld.dma_sems_now())

    def dbg_dram(name, src_ap, shape, dt=F32):
        if name not in debug_names:
            return
        hh = nc.dram_tensor("dbg_" + name, shape, dt, kind="ExternalOutput").ap()
        dbg_tensors[name] = "dbg_" + name
        bld.dma("sy", hh, src_ap, sem="d_dbg",
                waits=[bld.now("dv"), bld.now("ac"), bld.now("pe"), bld.now("gp")]
                + bld.dma_sems_now())

    es = ExitStack()

    def sb(name, shape, dt=F32, side=None):
        return es.enter_context(nc.sbuf_tensor(name, shape, dt, side=side))

    psf = [es.enter_context(nc.psum_tensor(f"psf{i}", [128, 512], F32)) for i in range(7)]
    psb = es.enter_context(nc.psum_tensor("psb", [128, 1024], BF16))

    bld.mksem = lambda name: es.enter_context(nc.semaphore(name))
    for s in ("pe", "dv", "ac", "gp"):
        bld.sems[s] = es.enter_context(nc.semaphore(s))
        bld.cnt[s] = 0

    consts = sb("consts", [128, 8], F32)
    C_ONES, C_EPS, C_DTB, C_A, C_GNORM, C_OFF = 0, 1, 2, 3, 4, 5
    NCOL = 8

    def ccol(c, nparts=128, p0=0):
        return ap2(consts, p0, nparts, [[NCOL, 1]], c)

    onesrow = sb("onesrow", [1, 128], F32)
    sel2 = sb("sel2_sb", [2, 128], F32)
    ident = sb("ident_sb", [128, 128], F32)
    identb = sb("identb_sb", [128, 128], BF16)
    maskneg = sb("maskneg_sb", [128, 128], F32)
    onescol_bf = sb("onescol_bf", [128, 1], BF16)
    sel2b = sb("sel2b_sb", [2, 128], BF16)
    maski = sb("maski", [128, 1], I32)
    xori = sb("xori", [128, 1], I32)

    bld.dma("sy", ccol(C_DTB, H), dtb_h, allow_slow_non_contiguous=True, sem="d_c")
    bld.dma("sy", ccol(C_A, H), alog_h, allow_slow_non_contiguous=True, sem="d_c")
    bld.dma("sy", ccol(C_GNORM, D), gnorm_h, allow_slow_non_contiguous=True, sem="d_c")
    bld.dma("sy", maskneg.ap(), maskneg_h, sem="d_c")
    bld.dma("sy", ident.ap(), ident_h, sem="d_c")
    bld.dma("sy", sel2.ap(), sel2_h, sem="d_c")
    bld.op("gp", lambda e: e.memset(ccol(C_ONES), 1.0))
    bld.op("gp", lambda e: e.memset(ccol(C_EPS), 1e-5))
    bld.op("gp", lambda e: e.memset(ccol(C_OFF, 64, 0), 0.0))
    bld.op("gp", lambda e: e.memset(ccol(C_OFF, 64, 64), 8.0))
    bld.op("gp", lambda e: e.memset(onesrow.ap(), 1.0))
    bld.op("gp", lambda e: e.memset(onescol_bf.ap(), 1.0))
    bld.op("gp", lambda e: e.memset(maski[0:64, :], -16))
    bld.op("gp", lambda e: e.memset(maski[64:128, :], 15))
    bld.op("gp", lambda e: e.memset(xori[0:64, :], 0))
    bld.op("gp", lambda e: e.memset(xori[64:128, :], 8), inc="gp")
    bld.barrier()
    bld.op("ac", lambda e: e.activation(ccol(C_A, H), ccol(C_A, H), AF.Exp))
    bld.op("ac", lambda e: e.mul(ccol(C_A, H), ccol(C_A, H), -1.0), inc="ac")
    bld.op("dv", lambda e: e.tensor_copy(identb[:], ident[:]))
    bld.op("dv", lambda e: e.tensor_copy(sel2b[:], sel2[:]), inc="dv")

    ones_col = ccol(C_ONES)
    eps_col = ccol(C_EPS)
    dtb_col = ccol(C_DTB, H)
    A_col = ccol(C_A, H)
    gnorm_col = ccol(C_GNORM, D)
    off_col = ccol(C_OFF)

    scope_m = ExitStack()

    def smb(name, shape, dt=F32):
        return scope_m.enter_context(nc.sbuf_tensor(name, shape, dt))

    kT = smb("kT", [128, KVH * T], BF16)
    ktok = smb("ktok", [128, NCH * KVH * D], BF16)
    vtok = smb("vtok", [128, NCH * KVH * D], BF16)
    G_all = smb("G_all", [H, 2 * T], F32)
    Gcol = smb("Gcol", [128, 16 * H], F32)
    dt_tok = smb("dt_tok", [128, 16 * H], F32)
    states = smb("states", [128, H * D], F32)
    cosm = smb("cosm_sb", [128, T], F32)
    sinm = smb("sinm_sb", [64, T], F32)
    grow = smb("grow", [1, T], F32)
    gl_cols = smb("gl_cols", [128, 2 * NCH], F32)
    xcb = smb("xcb", [128, 128], F32)
    kdb = smb("kdb", [128, 128], F32)

    pre_scope = ExitStack()
    ktok_p = pre_scope.enter_context(nc.sbuf_tensor("ktok_p", [128, NCH * KVH * D], BF16))
    vtok_p = pre_scope.enter_context(nc.sbuf_tensor("vtok_p", [128, NCH * KVH * D], BF16))
    cosp = pre_scope.enter_context(nc.sbuf_tensor("cosp_sb", [128, T], F32))
    sinp = pre_scope.enter_context(nc.sbuf_tensor("sinp_sb", [64, T], F32))

    xT = None

    def xT_sl(kc, t0, nt):
        return ap2(xT, 0, 128, [[1, nt]], kc * T + t0)

    bld.dma("sy", cosm.ap(), cosm_h, sem="d_c")
    bld.dma("sy", sinm.ap(), sinm_h, sem="d_c")
    bld.dma("sy", cosp.ap(), cosp_h, sem="d_c")
    bld.dma("sy", sinp.ap(), sinp_h, sem="d_c")
    bld.op("gp", lambda e: e.memset(states.ap(), 0.0), inc="gp")
    bld.barrier()

    def emit_wcol_load(dst_t, w_h, mb, ncols=128, waits=(), par=0):
        dst = ap2(dst_t, 0, 128, [[ncols, 32], [1, ncols]], 0)
        src = w_h.rearrange("(k p) m -> p k m", p=128)[:, :, mb * ncols:(mb + 1) * ncols]
        return bld.dma("gp", dst, src, waits=list(waits), sem=f"d_w{par}")

    def emit_proj(w_h, nmb, xsl_fn, tlen, bank_fn, evac_fn, wbuf, ncols=128,
                  pre_waits=()):
        ev_mm = {}
        ev_free = {}
        for mb in range(nmb):
            w = [ev_mm[mb - 2]] if mb >= 2 else list(pre_waits)
            evw = emit_wcol_load(wbuf[mb % 2], w_h, mb, ncols, waits=w, par=mb % 2)
            bank = bank_fn(mb)
            for kc in range(32):
                ev_mm[mb] = bld.op("pe", lambda e, mb=mb, kc=kc, bank=bank: e.matmul(
                    bank,
                    ap2(wbuf[mb % 2], 0, 128, [[1, ncols]], kc * ncols),
                    xsl_fn(kc, 0, tlen),
                    start=(kc == 0), stop=(kc == 31)),
                    waits=([evw] + ([ev_free[mb - 2]] if mb >= 2 else []))
                    if kc == 0 else [], inc="pe" if kc == 31 else None)
            evac_fn(mb, bank, ev_mm[mb])
            ev_free[mb] = bld.now("dv")

    def emit_s1_sb(sbi):
        pre = sbi < 2
        t0 = (sbi % 2) * 512
        x_h_src = xpre_h if pre else xmain_h
        cosT, sinT = (cosp, sinp) if pre else (cosm, sinm)
        ktok_dst, vtok_dst = (ktok_p, vtok_p) if pre else (ktok, vtok)
        goff = 0 if pre else T
        nn0 = (0 if pre else 8) + (sbi % 2) * 4
        nl0 = (sbi % 2) * 4

        scope = ExitStack()

        def lsb(name, shape, dt=F32):
            return scope.enter_context(nc.sbuf_tensor(f"{name}_{sbi}", shape, dt))

        if pre:
            xpre_bf = lsb("xpre_bf", [128, 32 * 512], BF16)

            def xsl(kc, tt0, nt):
                return ap2(xpre_bf, 0, 128, [[1, nt]], kc * 512 + tt0)
            dstx = ap2(xpre_bf, 0, 128, [[512, 32], [1, 512]], 0)
        else:
            def xsl(kc, tt0, nt):
                return xT_sl(kc, t0 + tt0, nt)
            dstx = ap2(xT, 0, 128, [[T, 32], [1, 512]], t0)
        srcx = x_h_src.rearrange("(k p) t -> p k t", p=128)[:, :, t0:t0 + 512]
        evx = bld.dma("gp", dstx, srcx, waits=[bld.now("pe"), bld.now("dv")], sem="d_x")

        kT_sb = lsb("kT_sb", [128, KVH * 512], BF16) if pre else None
        vT_sb = lsb("vT_sb", [128, 512], BF16)
        wbuf = [lsb("wbuf0", [128, 32 * 128], BF16), lsb("wbuf1", [128, 32 * 128], BF16)]
        ropeA = lsb("ropeA", [128, 512], F32)
        ropeB = lsb("ropeB", [128, 512], F32)

        def krope_dst(mb, tt0, nt, parts=(0, 128)):
            if pre:
                return apview(kT_sb.ap(), parts[0], parts[1], mb * 512 + tt0, [[1, nt]])
            return apview(kT.ap(), parts[0], parts[1], mb * T + t0 + tt0, [[1, nt]])

        cssl = cosT[:, t0:t0 + 512]
        snsl = sinT[:, t0:t0 + 512]

        def k_evac(mb, bank, ev_mm):
            bld.op("dv", lambda e, bank=bank: e.tensor_mul(ropeA[0:64, :], bank[0:64, :], cssl[0:64, :]),
                   waits=[ev_mm])
            bld.op("dv", lambda e, bank=bank: e.tensor_mul(ropeA[64:128, :], bank[64:128, :], cssl[64:128, :]))
            bld.op("dv", lambda e, bank=bank: e.tensor_mul(ropeB[0:64, :], bank[64:128, :], snsl[:]))
            bld.op("dv", lambda e, bank=bank: e.tensor_mul(ropeB[64:128, :], bank[0:64, :], snsl[:]))
            bld.op("dv", lambda e, mb=mb: e.tensor_sub(krope_dst(mb, 0, 512, (0, 64)),
                                                       ropeA[0:64, :], ropeB[0:64, :]))
            bld.op("dv", lambda e, mb=mb: e.tensor_add(krope_dst(mb, 0, 512, (64, 64)),
                                                       ropeA[64:128, :], ropeB[64:128, :]),
                   inc="dv")
            ev_rope = bld.now("dv")
            for n4 in range(4):
                nl = nl0 + n4
                srck = krope_dst(mb, n4 * 128, 128)
                half = (n4 % 2) * 512
                evt = bld.op("pe", lambda e, srck=srck, half=half: e.transpose(
                    apview(psb.ap(), 0, 128, half, [[1, 128]]), srck, identb[:]),
                    waits=[ev_rope, bld.now("dv")], inc="pe")
                dstk = ap2(ktok_dst, 0, 128, [[1, 128]], (nl * KVH + mb) * D)
                bld.op("dv", lambda e, dstk=dstk, half=half: e.tensor_copy(
                    dstk, apview(psb.ap(), 0, 128, half, [[1, 128]])),
                    waits=[evt], inc="dv")

        emit_proj(wkT_h, KVH, xsl, 512, lambda mb: psf[mb % 2][:],
                  k_evac, wbuf, pre_waits=[evx])

        def v_evac(mb, bank, ev_mm):
            bld.op("dv", lambda e, bank=bank: e.tensor_copy(vT_sb[:], bank), waits=[ev_mm], inc="dv")
            ev_c = bld.now("dv")
            for n4 in range(4):
                nl = nl0 + n4
                half = (n4 % 2) * 512
                evt = bld.op("pe", lambda e, n4=n4, half=half: e.transpose(
                    apview(psb.ap(), 0, 128, half, [[1, 128]]),
                    vT_sb[:, n4 * 128:(n4 + 1) * 128], identb[:]),
                    waits=[ev_c, bld.now("dv")], inc="pe")
                dstv = ap2(vtok_dst, 0, 128, [[1, 128]], (nl * KVH + mb) * D)
                bld.op("dv", lambda e, dstv=dstv, half=half: e.tensor_copy(
                    dstv, apview(psb.ap(), 0, 128, half, [[1, 128]])),
                    waits=[evt], inc="dv")

        emit_proj(wvT_h, KVH, xsl, 512, lambda mb: psf[2 + mb % 2][:],
                  v_evac, wbuf, pre_waits=[bld.now("pe")])

        dt_sl = lsb("dt_sl", [H, 512], F32)
        sp1 = lsb("sp1", [H, 512], F32)
        sp2 = lsb("sp2", [H, 512], F32)
        gsl = lsb("gsl", [H, 512], F32)
        gon = lsb("gon", [H, 128], F32)
        dts = dt_sl[:, :]

        def dt_evac(mb, bank, ev_mm):
            bld.op("dv", lambda e, bank=bank: e.tensor_scalar(sp1[:], bank, dtb_col, None, ALU.add),
                   waits=[ev_mm])
            bld.op("dv", lambda e: e.tensor_relu(sp2[:], sp1[:]), inc="dv")
            bld.op("ac", lambda e: e.activation(sp1[:], sp1[:], AF.Abs), waits=[bld.now("dv")])
            bld.op("ac", lambda e: e.activation(sp1[:], sp1[:], AF.Exp, scale=-1.0))
            bld.op("ac", lambda e: e.activation(sp1[:], sp1[:], AF.Ln, bias=1.0), inc="ac")
            bld.op("dv", lambda e: e.tensor_add(dts, sp1[:], sp2[:]),
                   waits=[bld.now("ac")], inc="dv")

        emit_proj(winT_h, 1, xsl, 512, lambda mb: psf[4][0:H, :],
                  dt_evac, wbuf, ncols=H, pre_waits=[bld.now("pe")])

        ev_sp = bld.now("dv")
        bld.op("gp", lambda e: e.memset(gon[:], 1.0), inc="gp")
        bld.op("dv", lambda e: e.tensor_scalar(gsl[:], dts, A_col, None, ALU.mult),
               waits=[ev_sp, bld.now("gp"), bld.now("ac")])
        Gs = ap2(G_all, 0, H, [[1, 512]], goff + t0)
        for n4 in range(4):
            bld.op("dv", lambda e, n4=n4, Gs=Gs: e.tensor_tensor_scan(
                apview(Gs, 0, H, n4 * 128, [[1, 128]]),
                gon[:], gsl[:, n4 * 128:(n4 + 1) * 128], 0.0, ALU.mult, ALU.add),
                inc="dv" if n4 == 3 else None)
        ev_G = bld.now("dv")

        for n4 in range(4):
            nn = nn0 + n4
            srcd = dt_sl[:, n4 * 128:(n4 + 1) * 128]
            srcg = ap2(G_all, 0, H, [[1, 128]], goff + t0 + n4 * 128)
            ev1 = bld.op("pe", lambda e, srcd=srcd: e.transpose(psf[1][:, 0:H], srcd,
                                                                ident[0:H, 0:H]),
                         waits=[ev_G, bld.now("ac")], inc="pe")
            bld.op("ac", lambda e, nn=nn: e.copy(ap2(dt_tok, 0, 128, [[1, H]], nn * H),
                                                 psf[1][:, 0:H]),
                   waits=[ev1], inc="ac")
            ev2 = bld.op("pe", lambda e, srcg=srcg: e.transpose(psf[2][:, 0:H], srcg,
                                                                ident[0:H, 0:H]),
                         waits=[ev_G, bld.now("ac")], inc="pe")
            bld.op("ac", lambda e, nn=nn: e.copy(ap2(Gcol, 0, 128, [[1, H]], nn * H),
                                                 psf[2][:, 0:H]),
                   waits=[ev2], inc="ac")

        bld.barrier()
        scope.close()

    emit_s1_sb(0)
    emit_s1_sb(1)

    dbg("vtok_p", vtok_p.ap(), [128, NCH * KVH * D])
    dbg("ktok_p", ktok_p.ap(), [128, NCH * KVH * D])

    def emit_head_cols(h, pre):
        goff = 0 if pre else T
        nnb = 0 if pre else 8
        evr = bld.dma("sy", grow.ap(), ap2(G_all, h, 1, [[1, T]], goff),
                      waits=[bld.now("pe"), bld.now("dv"), bld.now("ac")], sem="d_gr")
        gl_row = apview(grow.ap(), 0, 1, 127, [[128, NCH]])
        ev1 = bld.op("pe", lambda e, gl_row=gl_row: e.matmul(
            psf[3][:, 0:NCH], onesrow[:], gl_row, start=True, stop=True),
            waits=[evr, bld.now("dv")], inc="pe")
        gc_sl = ap2(Gcol, 0, 128, [[H, NCH]], nnb * H + h)
        bld.op("dv", lambda e, gc_sl=gc_sl: e.tensor_sub(gl_cols[:, 0:NCH], psf[3][:, 0:NCH], gc_sl),
               waits=[ev1], inc="dv")
        bld.op("ac", lambda e: e.activation(gl_cols[:, 0:NCH], gl_cols[:, 0:NCH], AF.Exp),
               waits=[bld.now("dv")])
        bld.op("ac", lambda e: e.activation(gl_cols[:, NCH:2 * NCH], psf[3][:, 0:NCH], AF.Exp),
               inc="ac")
        return bld.now("ac")

    for h in range(H):
        ev_cols = emit_head_cols(h, pre=True)
        kv = h // 4
        for n in range(NCH):
            bld.op("dv", lambda e, n=n, kv=kv, h=h: e.tensor_scalar(
                xcb[:], ap2(vtok_p, 0, 128, [[1, 128]], (n * KVH + kv) * D),
                ap2(dt_tok, 0, 128, [[H, 1]], n * H + h), None, ALU.mult),
                waits=[ev_cols])
            bld.op("dv", lambda e, n=n, kv=kv: e.tensor_scalar(
                kdb[:], ap2(ktok_p, 0, 128, [[1, 128]], (n * KVH + kv) * D),
                gl_cols[:, n:n + 1], None, ALU.mult), inc="dv")
            evk = bld.now("dv")
            evm = bld.op("pe", lambda e: e.matmul(psf[4][:, 0:128], kdb[:], xcb[:],
                                                  start=True, stop=True),
                         waits=[evk], inc="pe")
            bld.op("dv", lambda e, h=h, n=n: e.scalar_tensor_tensor(
                ap2(states, 0, 128, [[1, 128]], h * D),
                ap2(states, 0, 128, [[1, 128]], h * D),
                gl_cols[:, NCH + n:NCH + n + 1],
                psf[4][:, 0:128], ALU.mult, ALU.add),
                waits=[evm], inc="dv")
    bld.barrier()
    dbg("states", states.ap(), [128, H * D])
    pre_scope.close()

    xT = nc.alloc_sbuf_tensor("xT", [128, 32 * T], BF16, side="right")
    emit_s1_sb(2)
    emit_s1_sb(3)
    dbg("kT", kT.ap(), [128, KVH * T])
    dbg("vtok", vtok.ap(), [128, NCH * KVH * D])
    dbg("ktok", ktok.ap(), [128, NCH * KVH * D])
    dbg("G_all", G_all.ap(), [H, 2 * T])
    dbg("Gcol", Gcol.ap(), [128, 16 * H])
    dbg("dt_tok", dt_tok.ap(), [128, 16 * H])

    if stages < 3:
        return finish(nc, bld, es, dbg_tensors)

    s2 = ExitStack()

    def s2b(name, shape, dt=F32):
        return s2.enter_context(nc.sbuf_tensor(name, shape, dt))

    qT_h = s2b("qT_h", [128, T], BF16)
    gsil_h = s2b("gsil_h", [128, T], BF16)
    Y_h = s2b("Y_h", [128, T], BF16)
    wqbuf = [s2b("wqbuf0", [128, 32 * 128], BF16), s2b("wqbuf1", [128, 32 * 128], BF16)]
    ropeA2 = s2b("ropeA2", [128, 512], F32)
    ropeB2 = s2b("ropeB2", [128, 512], F32)
    dfT = s2b("dfT", [128, 128], F32)
    dbgyA = s2b("dbgyA", [128, 128], F32)
    dbgyB = s2b("dbgyB", [128, 128], F32)
    egb = s2b("egb", [128, 128], F32)
    STDb = s2b("STDb", [128, 128], F32)
    qscb = s2b("qscb", [128, 128], F32)
    prb = s2b("prb", [128, 128], F32)
    sqb = s2b("sqb", [128, 128], F32)
    t5b = s2b("t5b", [128, 128], F32)
    rmsr = s2b("rmsr", [1, 128], F32)
    rstdr = s2b("rstdr", [1, 128], F32)

    ev_w_prev = {}

    def emit_qg_proj(w_h, h, dst_rope, silu_dst, free_evs=()):
        par = emit_qg_proj.parity
        emit_qg_proj.parity ^= 1
        w = [ev_w_prev[par]] if par in ev_w_prev else [bld.now("pe")]
        evw = emit_wcol_load(wqbuf[par], w_h, h, 128, waits=w, par=par)
        evs = []
        for tb in range(2):
            for kc in range(32):
                ev = bld.op("pe", lambda e, kc=kc, tb=tb, par=par: e.matmul(
                    psf[tb][:],
                    ap2(wqbuf[par], 0, 128, [[1, 128]], kc * 128),
                    xT_sl(kc, tb * 512, 512),
                    start=(kc == 0), stop=(kc == 31)),
                    waits=([evw] + list(free_evs)) if kc == 0 else [],
                    inc="pe" if kc == 31 else None)
            evs.append(ev)
        ev_w_prev[par] = evs[-1]
        for tb in range(2):
            bank = psf[tb]
            if dst_rope is not None:
                cssl = cosm[:, tb * 512:(tb + 1) * 512]
                snsl = sinm[:, tb * 512:(tb + 1) * 512]
                bld.op("dv", lambda e, bank=bank, cssl=cssl: e.tensor_mul(
                    ropeA2[0:64, :], bank[0:64, :], cssl[0:64, :]), waits=[evs[tb]])
                bld.op("dv", lambda e, bank=bank, cssl=cssl: e.tensor_mul(
                    ropeA2[64:128, :], bank[64:128, :], cssl[64:128, :]))
                bld.op("dv", lambda e, bank=bank, snsl=snsl: e.tensor_mul(
                    ropeB2[0:64, :], bank[64:128, :], snsl[:]))
                bld.op("dv", lambda e, bank=bank, snsl=snsl: e.tensor_mul(
                    ropeB2[64:128, :], bank[0:64, :], snsl[:]))
                bld.op("dv", lambda e, tb=tb, dst_rope=dst_rope: e.tensor_sub(
                    apview(dst_rope, 0, 64, tb * 512, [[1, 512]]),
                    ropeA2[0:64, :], ropeB2[0:64, :]))
                bld.op("dv", lambda e, tb=tb, dst_rope=dst_rope: e.tensor_add(
                    apview(dst_rope, 64, 64, tb * 512, [[1, 512]]),
                    ropeA2[64:128, :], ropeB2[64:128, :]), inc="dv")
            else:
                bld.op("ac", lambda e, bank=bank, tb=tb: e.activation(
                    silu_dst[:, tb * 512:(tb + 1) * 512], bank[:], AF.Silu),
                    waits=[evs[tb]], inc="ac")
        return bld.now("dv") if dst_rope is not None else bld.now("ac")
    emit_qg_proj.parity = 0

    pc = {"STD": None, "egb": None, "qsc": None, "sq": None, "rms": None, "Y": None,
          "rstd": None}

    for h in range(H):
        kv = h // 4
        ev_cols = emit_head_cols(h, pre=False)
        ev_q = emit_qg_proj(wqT_h, h, qT_h.ap(), None,
                            free_evs=[pc["STD"], pc["egb"], pc.get("gsilu")])
        ev_g = emit_qg_proj(wgT_h, h, None, gsil_h, free_evs=[ev_q])
        pc["gsilu"] = ev_g
        for n in range(NCH):
            col = (8 + n) * H + h
            ev_st = bld.op("pe", lambda e, kv=kv, n=n: e.matmul(
                psf[0][:, 0:128],
                ap2(kT, 0, 128, [[1, 128]], kv * T + n * 128),
                ap2(qT_h, 0, 128, [[1, 128]], n * 128),
                start=True, stop=True),
                waits=[ev_q, ev_g, pc["STD"]], inc="pe")
            ev_gi = bld.op("pe", lambda e, n=n: e.matmul(
                psf[1][:, 0:128], onesrow[:],
                apview(grow.ap(), 0, 1, n * 128, [[1, 128]]),
                start=True, stop=True),
                waits=[ev_cols, ev_g, pc["STD"], pc["egb"]], inc="pe")
            ev3 = bld.op("dv", lambda e, col=col: e.scalar_tensor_tensor(
                dfT[:], psf[1][:, 0:128], ap2(Gcol, 0, 128, [[H, 1]], col),
                maskneg[:], ALU.subtract, ALU.add),
                waits=[ev_gi], inc="dv")
            ev4 = bld.op("ac", lambda e: e.activation(dfT[:], dfT[:], AF.Exp),
                         waits=[ev3], inc="ac")
            ev5 = bld.op("ac", lambda e: e.activation(egb[:], psf[1][:, 0:128], AF.Exp),
                         waits=[ev_gi, pc["qsc"]], inc="ac")
            ev6 = bld.op("dv", lambda e: e.tensor_mul(STDb[:], dfT[:], psf[0][:, 0:128]),
                         waits=[ev4, ev_st], inc="dv")
            pc["STD"] = ev6
            ev7 = bld.op("dv", lambda e, n=n: e.tensor_mul(
                qscb[:], ap2(qT_h, 0, 128, [[1, 128]], n * 128), egb[:]),
                waits=[ev5], inc="dv")
            pc["qsc"] = ev7
            pc["egb"] = ev5
            bld.op("dv", lambda e, n=n, kv=kv, col=col: e.tensor_scalar(
                xcb[:], ap2(vtok, 0, 128, [[1, 128]], (n * KVH + kv) * D),
                ap2(dt_tok, 0, 128, [[H, 1]], col), None, ALU.mult))
            bld.op("dv", lambda e, n=n, kv=kv: e.tensor_scalar(
                kdb[:], ap2(ktok, 0, 128, [[1, 128]], (n * KVH + kv) * D),
                gl_cols[:, n:n + 1], None, ALU.mult))
            ev10 = bld.op("dv", lambda e, h=h: e.tensor_copy(
                prb[:], ap2(states, 0, 128, [[1, 128]], h * D)), inc="dv")
            bld.op("pe", lambda e: e.matmul(psf[2][:, 0:128], xcb[:], STDb[:],
                                            start=True, stop=False),
                   waits=[ev10, pc["sq"], ev6], inc=None)
            ev_y = bld.op("pe", lambda e: e.matmul(psf[2][:, 0:128], prb[:], qscb[:],
                                                   start=False, stop=True),
                          waits=[ev7], inc="pe")
            ev_cs = bld.op("pe", lambda e: e.matmul(psf[3][:, 0:128], kdb[:], xcb[:],
                                                    start=True, stop=True),
                           waits=[ev10], inc="pe")
            bld.op("dv", lambda e, h=h, n=n: e.scalar_tensor_tensor(
                ap2(states, 0, 128, [[1, 128]], h * D),
                ap2(states, 0, 128, [[1, 128]], h * D),
                gl_cols[:, NCH + n:NCH + n + 1],
                psf[3][:, 0:128], ALU.mult, ALU.add),
                waits=[ev_cs], inc="dv")
            ev_sq = bld.op("ac", lambda e: e.activation(sqb[:], psf[2][:, 0:128], AF.Square),
                           waits=[ev_y], inc="ac")
            pc["sq"] = ev_sq
            if DBG_HN == (h, n):
                bld.op("ac", lambda e: e.copy(dbgyA[:], psf[2][:, 0:128]),
                       waits=[ev_y], inc="ac")
            ev_ssq = bld.op("pe", lambda e: e.matmul(psf[4][0:1, 0:128], ones_col, sqb[:],
                                                     start=True, stop=True),
                            waits=[ev_sq, pc["rms"]], inc="pe")
            ev_rms = bld.op("ac", lambda e: e.activation(rmsr[:], psf[4][0:1, 0:128], AF.Sqrt,
                                                         scale=1.0 / 128.0,
                                                         bias=ccol(C_EPS, 1)),
                            waits=[ev_ssq, pc["rstd"]], inc="ac")
            pc["rms"] = ev_rms
            ev_rstd = bld.op("dv", lambda e: e.reciprocal(rstdr[:], rmsr[:]),
                             waits=[ev_rms], inc="dv")
            pc["rstd"] = ev_rstd
            ev_rb = bld.op("pe", lambda e: e.matmul(psf[5][:, 0:128], onesrow[:], rstdr[:],
                                                    start=True, stop=True),
                           waits=[ev_rstd, pc["Y"]], inc="pe")
            ev_t5 = bld.op("dv", lambda e, n=n: e.scalar_tensor_tensor(
                t5b[:], psf[2][:, 0:128], gnorm_col,
                ap2(gsil_h, 0, 128, [[1, 128]], n * 128), ALU.mult, ALU.mult),
                waits=[ev_y, ev_g], inc="dv")
            if DBG_HN == (h, n):
                bld.op("ac", lambda e: e.copy(dbgyB[:], psf[2][:, 0:128]),
                       waits=[bld.now("dv")], inc="ac")
            ev_Y = bld.op("dv", lambda e, n=n: e.tensor_mul(
                ap2(Y_h, 0, 128, [[1, 128]], n * 128), t5b[:], psf[5][:, 0:128]),
                waits=[ev_rb], inc="dv")
            pc["Y"] = ev_Y
            if DBG_HN == (h, n):
                for nm, tt in (("dfT", dfT), ("STDb", STDb), ("qscb", qscb),
                               ("egb", egb), ("prb", prb), ("xcb", xcb),
                               ("kdb", kdb), ("t5b", t5b), ("rmsr", rmsr),
                               ("dbgyA", dbgyA), ("dbgyB", dbgyB)):
                    dbg(nm, tt.ap(), list(tt.ap().shape))
                dbg("Yn", ap2(Y_h, 0, 128, [[1, 128]], n * 128), [128, 128])
                bld.barrier()
        ev_ys = bld.dma("sy", y_s[h * 128:(h + 1) * 128, :], Y_h.ap(), waits=[pc["Y"]],
                        sem="d_y")
        pc["Y"] = ev_ys

    bld.barrier()
    s2.close()
    scope_m.close()
    dbg_dram("y_s", y_s, [H * D, T], BF16)

    if stages < 4:
        return finish(nc, bld, es, dbg_tensors)

    s3 = ExitStack()

    def s3b(name, shape, dt=F32):
        return s3.enter_context(nc.sbuf_tensor(name, shape, dt))

    Y_all = s3b("Y_all", [128, H * T], BF16)
    wobuf = [s3b("wobuf0", [128, 32 * 128], BF16), s3b("wobuf1", [128, 32 * 128], BF16)]
    sqt = s3b("sqt", [128, T], BF16)
    rows = s3b("rows", [1, 4 * T], F32)
    ut = s3b("ut", [128, 512], F32)

    ev_yl = bld.dma("sy", ap2(Y_all, 0, 128, [[T, H], [1, T]], 0),
                    y_s.rearrange("(h p) t -> p h t", p=128), sem="d_y")

    def emit_ln(xsrc_sl, dma_out, out_dt_bf=True):
        pass

    ev_w_prev2 = {}
    ev_evac2 = {}
    for mc in range(32):
        par = mc % 2
        w = [ev_w_prev2[par]] if par in ev_w_prev2 else [bld.now("pe")]
        evw = emit_wcol_load(wobuf[par], woT_h, mc, 128, waits=w, par=par)
        for tb in range(2):
            bank = psf[tb]
            for hc in range(32):
                ev = bld.op("pe", lambda e, par=par, hc=hc, tb=tb, bank=bank: e.matmul(
                    bank[:], ap2(wobuf[par], 0, 128, [[1, 128]], hc * 128),
                    ap2(Y_all, 0, 128, [[1, 512]], hc * T + tb * 512),
                    start=(hc == 0), stop=(hc == 31)),
                    waits=([evw, ev_yl] + ([ev_evac2.get(mc - 2)] if mc >= 2 else []))
                    if hc == 0 else [],
                    inc="pe" if hc == 31 else None)
            ev_w_prev2[par] = ev
            xsl = ap2(xT, 0, 128, [[1, 512]], mc * T + tb * 512)
            bld.op("dv", lambda e, xsl=xsl, bank=bank: e.tensor_add(xsl, xsl, bank[:]),
                   waits=[ev], inc="dv")
        ev_evac2[mc] = bld.now("dv")
    bld.barrier()

    for mc in range(32):
        xsl = ap2(xT, 0, 128, [[1, T]], mc * T)
        ev_s = bld.op("ac", lambda e, xsl=xsl: e.activation(sqt[:], xsl, AF.Square),
                      waits=[bld.now("pe")], inc="ac")
        for tb in range(2):
            bld.op("pe", lambda e, mc=mc, tb=tb: e.matmul(
                psf[2 + tb][0:1, :], onescol_bf[:],
                ap2(xT, 0, 128, [[1, 512]], mc * T + tb * 512),
                start=(mc == 0), stop=(mc == 31)),
                waits=[], inc="pe" if mc == 31 else None)
            bld.op("pe", lambda e, mc=mc, tb=tb: e.matmul(
                psf[4 + tb][0:1, :], onescol_bf[:], sqt[:, tb * 512:(tb + 1) * 512],
                start=(mc == 0), stop=(mc == 31)),
                waits=[ev_s], inc="pe" if mc == 31 else None)
    ev_stats = bld.now("pe")

    def emit_ln_rows(ev_stats):
        for tb in range(2):
            bld.op("dv", lambda e, tb=tb: e.tensor_scalar(
                rows[0:1, tb * 512:(tb + 1) * 512], psf[2 + tb][0:1, :],
                1.0 / HID, None, ALU.mult), waits=[ev_stats])
            bld.op("dv", lambda e, tb=tb: e.tensor_scalar(
                rows[0:1, T + tb * 512:T + (tb + 1) * 512], psf[4 + tb][0:1, :],
                1.0 / HID, None, ALU.mult), inc="dv")
        ev1 = bld.now("dv")
        bld.op("ac", lambda e: e.activation(rows[0:1, 2 * T:3 * T], rows[0:1, 0:T],
                                            AF.Square), waits=[ev1], inc="ac")
        ev2 = bld.now("ac")
        bld.op("dv", lambda e: e.tensor_sub(rows[0:1, T:2 * T], rows[0:1, T:2 * T],
                                            rows[0:1, 2 * T:3 * T]), waits=[ev2], inc="dv")
        ev3 = bld.now("dv")
        bld.op("ac", lambda e: e.activation(rows[0:1, 2 * T:3 * T], rows[0:1, T:2 * T],
                                            AF.Sqrt, bias=ccol(C_EPS, 1)),
               waits=[ev3], inc="ac")
        ev4 = bld.now("ac")
        bld.op("dv", lambda e: e.reciprocal(rows[0:1, 3 * T:4 * T], rows[0:1, 2 * T:3 * T]),
               waits=[ev4], inc="dv")
        return bld.now("dv")

    ev_rows = emit_ln_rows(ev_stats)

    ev_bc = []
    for tb in range(2):
        bld.op("pe", lambda e, tb=tb: e.matmul(
            psf[2 + tb][:], onesrow[:], rows[0:1, tb * 512:(tb + 1) * 512],
            start=True, stop=True), waits=[ev_rows])
        ev_bc.append(bld.op("pe", lambda e, tb=tb: e.matmul(
            psf[4 + tb][:], onesrow[:], rows[0:1, 3 * T + tb * 512:3 * T + (tb + 1) * 512],
            start=True, stop=True), inc="pe"))

    for mc in range(32):
        for tb in range(2):
            xsl = ap2(xT, 0, 128, [[1, 512]], mc * T + tb * 512)
            bld.op("dv", lambda e, xsl=xsl, tb=tb: e.tensor_sub(ut[:], xsl, psf[2 + tb][:]),
                   waits=[ev_bc[tb]])
            bld.op("dv", lambda e, xsl=xsl, tb=tb: e.tensor_mul(xsl, ut[:], psf[4 + tb][:]),
                   inc="dv")
        bld.dma("sy", xm_s[mc * 128:(mc + 1) * 128, :],
                ap2(xT, 0, 128, [[1, T]], mc * T), waits=[bld.now("dv")], sem="d_xm")
    bld.barrier()
    s3.close()
    dbg_dram("xm_s", xm_s, [HID, T], BF16)

    if stages < 5:
        return finish(nc, bld, es, dbg_tensors)

    s4 = ExitStack()

    def s4b(name, shape, dt=F32):
        return s4.enter_context(nc.sbuf_tensor(name, shape, dt))

    pk1b = [s4b("pk1b0", [128, 4096], I8), s4b("pk1b1", [128, 4096], I8)]
    pk3b = [s4b("pk3b0", [128, 4096], I8), s4b("pk3b1", [128, 4096], I8)]
    sc1b = [s4b("sc1b0", [2, 4096], BF16), s4b("sc1b1", [2, 4096], BF16)]
    sc3b = [s4b("sc3b0", [2, 4096], BF16), s4b("sc3b1", [2, 4096], BF16)]
    W1c = [s4b("W1c0", [128, 4096], BF16), s4b("W1c1", [128, 4096], BF16)]
    W3c = [s4b("W3c0", [128, 4096], BF16), s4b("W3c1", [128, 4096], BF16)]
    asb = s4b("asb", [128, 512], BF16)
    hst = s4b("hst", [128, T], BF16)
    fence4 = s4b("fence4", [1, 4], F32)

    ev_mm_ob = {}
    ev_stt_ob = {}
    ev_hdma = {}
    ev_ext_prev = {None: None}

    def emit_dequant(ob, par, pk, sc, Wc, pk_h, sc_h, scale_bank, waits_pk):
        evp = bld.dma("sy", pk[par].ap(), pk_h[ob * 128:(ob + 1) * 128, :],
                      waits=waits_pk, sem=f"d_s4_{par}")
        evs = bld.dma("gp", sc[par].ap(), sc_h[ob * 2:(ob + 1) * 2, :],
                      waits=[bld.now("pe")], sem=f"d_s4s_{par}")
        eve = bld.op("dv", lambda e, par=par, pk=pk: e.tensor_scalar(
            pk[par][:], pk[par][:], maski[:], xori[:],
            ALU.bitwise_and, ALU.bitwise_xor), waits=[evp], inc="dv")
        out = []
        for sl in range(8):
            evm = bld.op("pe", lambda e, sl=sl, par=par, sc=sc, scale_bank=scale_bank:
                         e.matmul(scale_bank[:], sel2b[:],
                                  sc[par][0:2, sl * 512:(sl + 1) * 512],
                                  start=True, stop=True),
                         waits=[evs, bld.now("dv")], inc="pe")
            ev = bld.op("dv", lambda e, sl=sl, par=par, pk=pk, Wc=Wc, scale_bank=scale_bank:
                        e.scalar_tensor_tensor(
                            Wc[par][:, sl * 512:(sl + 1) * 512],
                            pk[par][:, sl * 512:(sl + 1) * 512],
                            off_col, scale_bank[:], ALU.subtract, ALU.mult),
                        waits=[eve, evm] + (list(waits_pk) if sl == 0 else []),
                        inc="dv")
            out.append(ev)
        evf = bld.op("dv", lambda e, Wc=Wc, par=par: e.tensor_copy(
            fence4[0:1, 0:4], Wc[par][0:1, 0:4]), inc="dv")
        out.append(evf)
        return out

    for ob in range(NOB):
        if PARANOID_S4:
            bld.barrier()
        par = ob % 2
        wfree = [ev_mm_ob.get(ob - 2)]
        e1 = emit_dequant(ob, par, pk1b, sc1b, W1c, w1d_h, s1re_h, psf[4], wfree)
        e3 = emit_dequant(ob, par, pk3b, sc3b, W3c, w3d_h, s3re_h, psf[5], wfree)
        evab = {}
        for tb in range(2):
            for j in range(32):
                w = []
                if j == 0:
                    w = [e1[-1], e3[-1], bld.now("ac"), ev_hdma.get(ob - 1)]
                evab[("a", tb)] = bld.op("pe", lambda e, tb=tb, j=j, par=par: e.matmul(
                    psf[tb][:], ap2(W1c[par], 0, 128, [[1, 128]], j * 128),
                    xT_sl(j, tb * 512, 512),
                    start=(j == 0), stop=(j == 31)),
                    waits=w + ([e1[j // 4]] if j % 4 == 0 else []),
                    inc="pe" if j == 31 else None)
            for j in range(32):
                evab[("b", tb)] = bld.op("pe", lambda e, tb=tb, j=j, par=par: e.matmul(
                    psf[2 + tb][:], ap2(W3c[par], 0, 128, [[1, 128]], j * 128),
                    xT_sl(j, tb * 512, 512),
                    start=(j == 0), stop=(j == 31)),
                    waits=[e3[-1]] if j == 0 else [],
                    inc="pe" if j == 31 else None)
        ev_mm_ob[ob] = evab[("b", 1)]
        for tb in range(2):
            eva = bld.op("ac", lambda e, tb=tb: e.activation(asb[:], psf[tb][:], AF.Silu),
                         waits=[evab[("a", tb)], bld.now("dv")], inc="ac")
            bld.op("dv", lambda e, tb=tb: e.tensor_mul(
                hst[:, tb * 512:(tb + 1) * 512], asb[:], psf[2 + tb][:]),
                waits=[eva, evab[("b", tb)], ev_hdma.get(ob - 1)], inc="dv")
        ev_hdma[ob] = bld.dma("sy", h_s[ob * 128:(ob + 1) * 128, :], hst.ap(),
                              waits=[bld.now("dv")], sem="d_hs")
        if DBG_OB == ob:
            dbg("W1c", W1c[par].ap(), [128, 4096])
            dbg("W3c", W3c[par].ap(), [128, 4096])
            bld.barrier()
    bld.barrier()
    s4.close()
    dbg_dram("h_s", h_s, [NOB * 128, T], BF16)

    if stages < 6:
        return finish(nc, bld, es, dbg_tensors)

    s5 = ExitStack()

    def s5b(name, shape, dt=F32):
        return s5.enter_context(nc.sbuf_tensor(name, shape, dt))

    MG = [3] * 10 + [2]
    W2c = [s5b(f"W2c{i}", [128, FFH], BF16) for i in range(3)]
    pk2b = s5b("pk2b", [128, FFH], I8)
    ss2 = s5b("ss2", [2, 2560], BF16)
    hb = [s5b("hb0", [128, T], BF16), s5b("hb1", [128, T], BF16)]
    fence5 = s5b("fence5", [1, 4], F32)
    xmt = [s5b("xmt0", [128, 512], BF16), s5b("xmt1", [128, 512], BF16)]
    r2st = [s5b("r2st0", [128, 512], BF16), s5b("r2st1", [128, 512], BF16)]

    mg_base = 0
    ev_mm_done = None
    ev_r2st = {}
    for gi, gsz in enumerate(MG):
        ev_stts = []
        for mb4 in range(gsz):
            mb = mg_base + mb4
            evp = bld.dma("sy", pk2b.ap(), w2d_h[mb * 128:(mb + 1) * 128, :],
                          waits=[bld.now("gp"), bld.now("dv"), ev_mm_done], sem="d_s5")
            eve = bld.op("dv", lambda e: e.tensor_scalar(
                pk2b[:], pk2b[:], maski[:], xori[:],
                ALU.bitwise_and, ALU.bitwise_xor), waits=[evp], inc="dv")
            sl0 = 0
            for ch in range(5):
                cw = 2560 if ch < 4 else FFH - 4 * 2560
                evs = bld.dma("gp", ss2[0:2, 0:cw],
                              s2re_h[mb * 2:(mb + 1) * 2, sl0:sl0 + cw],
                              waits=[bld.now("dv")], sem="d_s5s")
                nmm = (cw + 511) // 512
                for q2 in range(nmm):
                    qw = min(512, cw - q2 * 512)
                    evm = bld.op("pe", lambda e, q2=q2, qw=qw: e.matmul(
                        psf[6][:, 0:qw], sel2b[:], ss2[0:2, q2 * 512:q2 * 512 + qw],
                        start=True, stop=True),
                        waits=[evs, bld.now("dv")], inc="pe")
                    ev = bld.op("dv", lambda e, mb4=mb4, sl0=sl0, q2=q2, qw=qw: e.scalar_tensor_tensor(
                        W2c[mb4][:, sl0 + q2 * 512:sl0 + q2 * 512 + qw],
                        pk2b[:, sl0 + q2 * 512:sl0 + q2 * 512 + qw],
                        off_col, psf[6][:, 0:qw], ALU.subtract, ALU.mult),
                        waits=[eve, evm], inc="dv")
                ev_stts.append(ev)
                sl0 += cw
            bld.op("dv", lambda e, mb4=mb4: e.tensor_copy(
                fence5[0:1, 0:4], W2c[mb4][0:1, 0:4]), inc="dv")
        ev_deq = bld.now("dv")
        ev_h = {}
        ev_last = None
        for j in range(NOB):
            ev_h[j] = bld.dma("sy", hb[j % 2].ap(), h_s[j * 128:(j + 1) * 128, :],
                              waits=([ev_last] if j >= 2 else [bld.now("pe")])
                              + ([("d_hs", 16 * NOB)] if j < 2 else []),
                              sem=f"d_h{j % 2}")
            for mb4 in range(gsz):
                for tb in range(2):
                    ev_last = bld.op("pe", lambda e, j=j, mb4=mb4, tb=tb: e.matmul(
                        psf[mb4 * 2 + tb][:],
                        ap2(W2c[mb4], 0, 128, [[1, 128]], j * 128),
                        apview(hb[j % 2].ap(), 0, 128, tb * 512, [[1, 512]]),
                        start=(j == 0), stop=(j == NOB - 1)),
                        waits=[ev_h[j], ev_deq] if (mb4 == 0 and tb == 0) else [],
                        inc="pe" if (mb4 == gsz - 1 and tb == 1) else None)
        ev_mm_done = ev_last
        for mb4 in range(gsz):
            mb = mg_base + mb4
            for tb in range(2):
                evx = bld.dma("sy", xmt[tb].ap(),
                              xm_s[mb * 128:(mb + 1) * 128, tb * 512:(tb + 1) * 512],
                              waits=[bld.now("dv")]
                              + ([("d_xm", 16 * 32)] if gi == 0 and mb4 == 0 else []),
                              sem=f"d_xt{tb}")
                bld.op("dv", lambda e, mb4=mb4, tb=tb: e.tensor_add(
                    r2st[tb][:], xmt[tb][:], psf[mb4 * 2 + tb][:]),
                    waits=[evx, ev_mm_done, ev_r2st.get(tb)], inc="dv")
                ev_r2st[tb] = bld.dma("sy",
                        r2_s[mb * 128:(mb + 1) * 128, tb * 512:(tb + 1) * 512],
                        r2st[tb].ap(), waits=[bld.now("dv")], sem="d_r2")
        mg_base += gsz
    bld.barrier()
    s5.close()
    dbg_dram("r2_s", r2_s, [HID, T], BF16)

    s6 = ExitStack()

    def s6b(name, shape, dt=F32):
        return s6.enter_context(nc.sbuf_tensor(name, shape, dt))

    rt = [s6b("rt0", [128, T], BF16), s6b("rt1", [128, T], BF16)]
    sq6 = s6b("sq6", [128, T], BF16)
    ut6 = s6b("ut6", [128, 512], F32)
    ost = s6b("ost", [128, T], F32)
    rows6 = s6b("rows6", [1, 4 * T], F32)

    ev_rt = {}
    ev_cons = {}
    for mc in range(32):
        w = [bld.now("pe"), bld.now("ac")] if mc < 2 else [ev_cons[mc - 2]]
        ev_rt[mc] = bld.dma("sy", rt[mc % 2].ap(), r2_s[mc * 128:(mc + 1) * 128, :],
                            waits=w + ([("d_r2", 16 * 64)] if mc < 2 else []),
                            sem=f"d_rt{mc % 2}")
        ev_s = bld.op("ac", lambda e, mc=mc: e.activation(sq6[:], rt[mc % 2][:], AF.Square),
                      waits=[ev_rt[mc], bld.now("pe")], inc="ac")
        for tb in range(2):
            bld.op("pe", lambda e, mc=mc, tb=tb: e.matmul(
                psf[2 + tb][0:1, :], onescol_bf[:],
                apview(rt[mc % 2].ap(), 0, 128, tb * 512, [[1, 512]]),
                start=(mc == 0), stop=(mc == 31)),
                waits=[ev_rt[mc]], inc=None)
            ev_cons[mc] = bld.op("pe", lambda e, mc=mc, tb=tb: e.matmul(
                psf[4 + tb][0:1, :], onescol_bf[:], sq6[:, tb * 512:(tb + 1) * 512],
                start=(mc == 0), stop=(mc == 31)),
                waits=[ev_s], inc="pe")
    ev_stats6 = bld.now("pe")

    for tb in range(2):
        bld.op("dv", lambda e, tb=tb: e.tensor_scalar(
            rows6[0:1, tb * 512:(tb + 1) * 512], psf[2 + tb][0:1, :],
            1.0 / HID, None, ALU.mult), waits=[ev_stats6])
        bld.op("dv", lambda e, tb=tb: e.tensor_scalar(
            rows6[0:1, T + tb * 512:T + (tb + 1) * 512], psf[4 + tb][0:1, :],
            1.0 / HID, None, ALU.mult), inc="dv")
    bld.op("ac", lambda e: e.activation(rows6[0:1, 2 * T:3 * T], rows6[0:1, 0:T],
                                        AF.Square), waits=[bld.now("dv")], inc="ac")
    bld.op("dv", lambda e: e.tensor_sub(rows6[0:1, T:2 * T], rows6[0:1, T:2 * T],
                                        rows6[0:1, 2 * T:3 * T]),
           waits=[bld.now("ac")], inc="dv")
    bld.op("ac", lambda e: e.activation(rows6[0:1, 2 * T:3 * T], rows6[0:1, T:2 * T],
                                        AF.Sqrt, bias=ccol(C_EPS, 1)),
           waits=[bld.now("dv")], inc="ac")
    bld.op("dv", lambda e: e.reciprocal(rows6[0:1, 3 * T:4 * T], rows6[0:1, 2 * T:3 * T]),
           waits=[bld.now("ac")], inc="dv")
    ev_rows6 = bld.now("dv")

    ev_out = None
    ev_bc6 = []
    for tb in range(2):
        bld.op("pe", lambda e, tb=tb: e.matmul(
            psf[2 + tb][:], onesrow[:], rows6[0:1, tb * 512:(tb + 1) * 512],
            start=True, stop=True), waits=[ev_rows6])
        ev_bc6.append(bld.op("pe", lambda e, tb=tb: e.matmul(
            psf[4 + tb][:], onesrow[:], rows6[0:1, 3 * T + tb * 512:3 * T + (tb + 1) * 512],
            start=True, stop=True), inc="pe"))

    for mc in range(32):
        evr = bld.dma("sy", rt[mc % 2].ap(), r2_s[mc * 128:(mc + 1) * 128, :],
                      waits=[bld.now("dv"), ev_bc6[1]], sem=f"d_rt{mc % 2}")
        for tb in range(2):
            bld.op("dv", lambda e, mc=mc, tb=tb: e.tensor_sub(
                ut6[:], apview(rt[mc % 2].ap(), 0, 128, tb * 512, [[1, 512]]),
                psf[2 + tb][:]), waits=[evr, ev_out])
            bld.op("dv", lambda e, mc=mc, tb=tb: e.tensor_mul(
                ost[:, tb * 512:(tb + 1) * 512], ut6[:], psf[4 + tb][:]), inc="dv")
        ev_out = bld.dma("sy", outT_h[mc * 128:(mc + 1) * 128, :], ost[:],
                         waits=[bld.now("dv")], sem="sto")

    dbg("rows6", rows6.ap(), [1, 4 * T])
    s6.close()
    return finish(nc, bld, es, dbg_tensors)


def finish(nc, bld, es, dbg_tensors):
    bld.op("sy", lambda e: e.nop(), waits=bld.dma_sems_now())
    blk = es.enter_context(nc.Block())
    bld.play(blk)
    es.close()
    return nc, dbg_tensors


def rope_tables():
    inv = 1.0 / (10000.0 ** (np.arange(0, D, 2, dtype=np.float64) / D))
    ang = np.arange(2048, dtype=np.float64)[:, None] * inv[None, :]
    cos = np.cos(ang)
    cos_full = np.concatenate([cos, cos], axis=1).T.astype(np.float32)
    sin64 = np.sin(ang).T.astype(np.float32)
    return np.ascontiguousarray(cos_full), np.ascontiguousarray(sin64)


def pack_w(w_packed_i32, out_f, in_f):
    b = w_packed_i32.view(np.int8).reshape(-1, 64, 4)[:, :, 0]
    nob = out_f // 128
    nj = in_f // 128
    t = b.reshape(out_f, nj, 64).reshape(nob, 128, nj, 64)
    t = np.transpose(t, (0, 3, 2, 1))
    dup = np.concatenate([t, t], axis=1)
    return np.ascontiguousarray(dup.reshape(nob * 128, nj * 128))


def pack_s(s_flat, out_f, in_f):
    nob = out_f // 128
    nj = in_f // 128
    t = s_flat.reshape(out_f, nj, 2).reshape(nob, 128, nj, 2)
    t = np.transpose(t, (0, 3, 2, 1))
    return np.ascontiguousarray(t.reshape(nob * 2, nj * 128).astype(np.float32))


def make_shared(inp):
    cos_full, sin64 = rope_tables()
    maskneg = np.where(np.arange(128)[None, :] >= np.arange(128)[:, None],
                       0.0, -1e30).astype(np.float32)
    sel2 = np.zeros((2, 128), dtype=np.float32)
    sel2[0, :64] = 1.0 / 16.0
    sel2[1, 64:] = 1.0
    sh = {
        "wqT": np.ascontiguousarray(inp["L0_attn_Wq"].T),
        "wkT": np.ascontiguousarray(inp["L0_attn_Wk"].T),
        "wvT": np.ascontiguousarray(inp["L0_attn_Wv"].T),
        "wgT": np.ascontiguousarray(inp["L0_attn_Wg"].T),
        "woT": np.ascontiguousarray(inp["L0_attn_Wo"].T),
        "winT": np.ascontiguousarray(inp["L0_attn_Win"].T),
        "dtb": np.ascontiguousarray(inp["L0_attn_dt_bias"].reshape(H, 1)),
        "alog": np.ascontiguousarray(inp["L0_attn_A_log_bias"].reshape(H, 1)),
        "gnorm": np.ascontiguousarray(inp["L0_attn_gnorm_w"].reshape(D, 1)),
        "maskneg": maskneg,
        "ident": np.eye(128, dtype=np.float32),
        "sel2": sel2,
        "w1d": pack_w(inp["L0_ffn_w1"], FFH, HID),
        "w3d": pack_w(inp["L0_ffn_w3"], FFH, HID),
        "w2d": pack_w(inp["L0_ffn_w2"], HID, FFH),
        "s1re": pack_s(inp["L0_ffn_s1"], FFH, HID),
        "s3re": pack_s(inp["L0_ffn_s3"], FFH, HID),
        "s2re": pack_s(inp["L0_ffn_s2"], HID, FFH),
        "_cos": cos_full, "_sin": sin64,
    }
    return sh


def make_core_inputs(inp, sh, core):
    b, half = core // 2, core % 2
    x = inp["L0_attn_hidden_states"]
    xm = np.ascontiguousarray(x[b, half * T:(half + 1) * T, :].T.astype(np.float32))
    if half == 1:
        xp = np.ascontiguousarray(x[b, 0:T, :].T.astype(np.float32))
    else:
        xp = np.zeros((HID, T), dtype=np.float32)
    off = half * T
    d = {k: v for k, v in sh.items() if not k.startswith("_")}
    d.update({
        "xmain": xm, "xpre": xp,
        "cosm": np.ascontiguousarray(sh["_cos"][:, off:off + T]),
        "sinm": np.ascontiguousarray(sh["_sin"][:, off:off + T]),
        "cosp": np.ascontiguousarray(sh["_cos"][:, 0:T]),
        "sinp": np.ascontiguousarray(sh["_sin"][:, 0:T]),
    })
    return d

import concourse.bass_utils as _bass_utils


def kernel(**inputs):
    inputs = {k: np.asarray(v) for k, v in inputs.items()}
    sh = make_shared(inputs)
    in_maps = [make_core_inputs(inputs, sh, c) for c in range(8)]
    nc, _ = build()
    res = _bass_utils.run_bass_kernel_spmd(nc, in_maps, core_ids=list(range(8)))
    out = np.zeros((4, 2048, 4096), np.float32)
    for c in range(8):
        b, half = c // 2, c % 2
        out[b, half * T:(half + 1) * T, :] = np.asarray(res.results[c]["outT"]).T
    return out


# version 05
# speedup vs baseline: 1.0032x; 1.0032x over previous
import sys
sys.path.insert(0, "/opt/trn_rl_repo")
import numpy as np
import concourse.bass as bass
import concourse.mybir as mybir
from contextlib import ExitStack

F32 = mybir.dt.float32
BF16 = mybir.dt.bfloat16
I32 = mybir.dt.int32
I8 = mybir.dt.int8
AF = mybir.ActivationFunctionType
ALU = mybir.AluOpType
AX = mybir.AxisListType

H, KVH, D, CHUNK = 32, 8, 128, 128
DBG_HN = None
DBG_OB = None
PARANOID_S4 = False
HID = 4096
FFH = 11008
T = 1024
NCH = 8
NOB = FFH // 128

ENGS = ("pe", "dv", "ac", "gp", "sy")
ENG_ATTR = {"pe": "tensor", "dv": "vector", "ac": "scalar", "gp": "gpsimd", "sy": "sync"}


class Bld:

    def __init__(self, nc):
        self.nc = nc
        self.ops = {e: [] for e in ENGS}
        self.cnt = {}
        self.waited = {}
        self.sems = {}

    def _dedup(self, eng, evs):
        out = []
        for ev in evs:
            if ev is None:
                continue
            s, v = ev
            if v <= 0:
                continue
            key = (eng, s)
            if self.waited.get(key, 0) >= v:
                continue
            self.waited[key] = v
            out.append((s, v))
        return out

    def op(self, eng, fn, waits=(), inc=None, amt=1):
        w = self._dedup(eng, list(waits))
        ev = None
        if inc is not None:
            self.cnt[inc] = self.cnt.get(inc, 0) + amt
            ev = (inc, self.cnt[inc])
        self.ops[eng].append((w, fn, inc, amt))
        return ev

    def dma(self, queue, out, in_, waits=(), sem=None, **kw):
        assert sem is not None, "stream semaphore required"
        if sem not in self.cnt:
            self.sems[sem] = self.mksem(sem)
            self.cnt[sem] = 0
        w = list(waits) + [(sem, self.cnt[sem])]
        return self.op(queue, lambda e: e.dma_start(out=out, in_=in_, **kw),
                       waits=w, inc=sem, amt=16)

    def dma_sems_now(self):
        return [self.now(s) for s in self.cnt if s.startswith("d_") or s == "sto"]

    def now(self, sem):
        return (sem, self.cnt.get(sem, 0))

    def barrier(self):
        evs = [self.now(s) for s in self.cnt]
        for e in ENGS:
            w = self._dedup(e, evs)
            if w:
                self.ops[e].append((w, None, None, 0))

    def play(self, blk):
        handles = self.sems
        for eng in ENGS:
            ops = self.ops[eng]
            if not ops:
                continue

            def body(h, ops=ops):
                for (waits, fn, inc, amt) in ops:
                    for (s, v) in waits:
                        h.wait_ge(handles[s], v)
                    if fn is None:
                        continue
                    inst = fn(h)
                    if inc is not None:
                        inst.then_inc(handles[inc], amt)
            getattr(blk, ENG_ATTR[eng])(body)


def ap2(t, part0, nparts, free_ap, off_elems=0):
    a = t.ap()
    return bass.AP(a.tensor, a.offset + part0 * a.ap[0][0] + off_elems,
                   [[a.ap[0][0], nparts]] + [list(x) for x in free_ap])


def apview(ap, part0, nparts, off_elems, free_ap):
    return bass.AP(ap.tensor, ap.offset + part0 * ap.ap[0][0] + off_elems,
                   [[ap.ap[0][0], nparts]] + [list(x) for x in free_ap])


def build(debug_names=(), stages=99):
    nc = bass.Bass("TRN2", target_bir_lowering=False)
    bld = Bld(nc)
    dbg_tensors = {}

    def din(name, shape, dt=F32):
        return nc.dram_tensor(name, shape, dt, kind="ExternalInput").ap()

    xmain_h = din("xmain", [HID, T])
    xpre_h = din("xpre", [HID, T])
    wqT_h = din("wqT", [HID, HID])
    wkT_h = din("wkT", [HID, KVH * D])
    wvT_h = din("wvT", [HID, KVH * D])
    wgT_h = din("wgT", [HID, HID])
    woT_h = din("woT", [HID, HID])
    winT_h = din("winT", [HID, H])
    dtb_h = din("dtb", [H, 1])
    alog_h = din("alog", [H, 1])
    gnorm_h = din("gnorm", [D, 1])
    cosm_h = din("cosm", [D, T])
    sinm_h = din("sinm", [64, T])
    cosp_h = din("cosp", [D, T])
    sinp_h = din("sinp", [64, T])
    maskneg_h = din("maskneg", [CHUNK, CHUNK])
    ident_h = din("ident", [128, 128])
    sel2_h = din("sel2", [2, 128])
    w1d_h = din("w1d", [NOB * 128, 4096], I8)
    w3d_h = din("w3d", [NOB * 128, 4096], I8)
    w2d_h = din("w2d", [32 * 128, FFH], I8)
    s1re_h = din("s1re", [NOB * 2, 4096])
    s3re_h = din("s3re", [NOB * 2, 4096])
    s2re_h = din("s2re", [32 * 2, FFH])

    outT_h = nc.dram_tensor("outT", [HID, T], F32, kind="ExternalOutput").ap()

    y_s = nc.dram_tensor("y_s", [H * D, T], BF16, kind="Internal").ap()
    h_s = nc.dram_tensor("h_s", [NOB * 128, T], BF16, kind="Internal").ap()
    xm_s = nc.dram_tensor("xm_s", [HID, T], BF16, kind="Internal").ap()
    r2_s = nc.dram_tensor("r2_s", [HID, T], BF16, kind="Internal").ap()

    def dbg(name, src_ap, shape, dt=F32):
        if name not in debug_names:
            return
        dt = src_ap.tensor.dtype
        hh = nc.dram_tensor("dbg_" + name, shape, dt, kind="ExternalOutput").ap()
        dbg_tensors[name] = "dbg_" + name
        bld.dma("sy", hh, src_ap, sem="d_dbg",
                waits=[bld.now("dv"), bld.now("ac"), bld.now("pe"), bld.now("gp")]
                + bld.dma_sems_now())

    def dbg_dram(name, src_ap, shape, dt=F32):
        if name not in debug_names:
            return
        hh = nc.dram_tensor("dbg_" + name, shape, dt, kind="ExternalOutput").ap()
        dbg_tensors[name] = "dbg_" + name
        bld.dma("sy", hh, src_ap, sem="d_dbg",
                waits=[bld.now("dv"), bld.now("ac"), bld.now("pe"), bld.now("gp")]
                + bld.dma_sems_now())

    es = ExitStack()

    def sb(name, shape, dt=F32, side=None):
        return es.enter_context(nc.sbuf_tensor(name, shape, dt, side=side))

    psf = [es.enter_context(nc.psum_tensor(f"psf{i}", [128, 512], F32)) for i in range(7)]
    psb = es.enter_context(nc.psum_tensor("psb", [128, 1024], BF16))

    bld.mksem = lambda name: es.enter_context(nc.semaphore(name))
    for s in ("pe", "dv", "ac", "gp"):
        bld.sems[s] = es.enter_context(nc.semaphore(s))
        bld.cnt[s] = 0

    consts = sb("consts", [128, 8], F32)
    C_ONES, C_EPS, C_DTB, C_A, C_GNORM, C_OFF = 0, 1, 2, 3, 4, 5
    NCOL = 8

    def ccol(c, nparts=128, p0=0):
        return ap2(consts, p0, nparts, [[NCOL, 1]], c)

    onesrow = sb("onesrow", [1, 128], F32)
    sel2 = sb("sel2_sb", [2, 128], F32)
    ident = sb("ident_sb", [128, 128], F32)
    identb = sb("identb_sb", [128, 128], BF16)
    maskneg = sb("maskneg_sb", [128, 128], F32)
    onescol_bf = sb("onescol_bf", [128, 1], BF16)
    sel2b = sb("sel2b_sb", [2, 128], BF16)
    maski = sb("maski", [128, 1], I32)
    xori = sb("xori", [128, 1], I32)

    bld.dma("sy", ccol(C_DTB, H), dtb_h, allow_slow_non_contiguous=True, sem="d_c")
    bld.dma("sy", ccol(C_A, H), alog_h, allow_slow_non_contiguous=True, sem="d_c")
    bld.dma("sy", ccol(C_GNORM, D), gnorm_h, allow_slow_non_contiguous=True, sem="d_c")
    bld.dma("sy", maskneg.ap(), maskneg_h, sem="d_c")
    bld.dma("sy", ident.ap(), ident_h, sem="d_c")
    bld.dma("sy", sel2.ap(), sel2_h, sem="d_c")
    bld.op("gp", lambda e: e.memset(ccol(C_ONES), 1.0))
    bld.op("gp", lambda e: e.memset(ccol(C_EPS), 1e-5))
    bld.op("gp", lambda e: e.memset(ccol(C_OFF, 64, 0), 0.0))
    bld.op("gp", lambda e: e.memset(ccol(C_OFF, 64, 64), 8.0))
    bld.op("gp", lambda e: e.memset(onesrow.ap(), 1.0))
    bld.op("gp", lambda e: e.memset(onescol_bf.ap(), 1.0))
    bld.op("gp", lambda e: e.memset(maski[0:64, :], -16))
    bld.op("gp", lambda e: e.memset(maski[64:128, :], 15))
    bld.op("gp", lambda e: e.memset(xori[0:64, :], 0))
    bld.op("gp", lambda e: e.memset(xori[64:128, :], 8), inc="gp")
    bld.barrier()
    bld.op("ac", lambda e: e.activation(ccol(C_A, H), ccol(C_A, H), AF.Exp))
    bld.op("ac", lambda e: e.mul(ccol(C_A, H), ccol(C_A, H), -1.0), inc="ac")
    bld.op("dv", lambda e: e.tensor_copy(identb[:], ident[:]))
    bld.op("dv", lambda e: e.tensor_copy(sel2b[:], sel2[:]), inc="dv")

    ones_col = ccol(C_ONES)
    eps_col = ccol(C_EPS)
    dtb_col = ccol(C_DTB, H)
    A_col = ccol(C_A, H)
    gnorm_col = ccol(C_GNORM, D)
    off_col = ccol(C_OFF)

    scope_m = ExitStack()

    def smb(name, shape, dt=F32):
        return scope_m.enter_context(nc.sbuf_tensor(name, shape, dt))

    kT = smb("kT", [128, KVH * T], BF16)
    ktok = smb("ktok", [128, NCH * KVH * D], BF16)
    vtok = smb("vtok", [128, NCH * KVH * D], BF16)
    G_all = smb("G_all", [H, 2 * T], F32)
    Gcol = smb("Gcol", [128, 16 * H], F32)
    dt_tok = smb("dt_tok", [128, 16 * H], F32)
    states = smb("states", [128, H * D], F32)
    cosm = smb("cosm_sb", [128, T], F32)
    sinm = smb("sinm_sb", [64, T], F32)
    grow = smb("grow", [1, T], F32)
    gl_cols = smb("gl_cols", [128, 2 * NCH], F32)
    xcb = smb("xcb", [128, 128], F32)
    kdb = smb("kdb", [128, 128], F32)

    pre_scope = ExitStack()
    ktok_p = pre_scope.enter_context(nc.sbuf_tensor("ktok_p", [128, NCH * KVH * D], BF16))
    vtok_p = pre_scope.enter_context(nc.sbuf_tensor("vtok_p", [128, NCH * KVH * D], BF16))
    cosp = pre_scope.enter_context(nc.sbuf_tensor("cosp_sb", [128, T], F32))
    sinp = pre_scope.enter_context(nc.sbuf_tensor("sinp_sb", [64, T], F32))

    xT = None

    def xT_sl(kc, t0, nt):
        return ap2(xT, 0, 128, [[1, nt]], kc * T + t0)

    bld.dma("sy", cosm.ap(), cosm_h, sem="d_c")
    bld.dma("sy", sinm.ap(), sinm_h, sem="d_c")
    bld.dma("sy", cosp.ap(), cosp_h, sem="d_c")
    bld.dma("sy", sinp.ap(), sinp_h, sem="d_c")
    bld.op("gp", lambda e: e.memset(states.ap(), 0.0), inc="gp")
    bld.barrier()

    def emit_wcol_load(dst_t, w_h, mb, ncols=128, waits=(), par=0):
        dst = ap2(dst_t, 0, 128, [[ncols, 32], [1, ncols]], 0)
        src = w_h.rearrange("(k p) m -> p k m", p=128)[:, :, mb * ncols:(mb + 1) * ncols]
        return bld.dma("gp", dst, src, waits=list(waits), sem=f"d_w{par}")

    def emit_proj(w_h, nmb, xsl_fn, tlen, bank_fn, evac_fn, wbuf, ncols=128,
                  pre_waits=()):
        ev_mm = {}
        ev_free = {}
        for mb in range(nmb):
            w = [ev_mm[mb - 2]] if mb >= 2 else list(pre_waits)
            evw = emit_wcol_load(wbuf[mb % 2], w_h, mb, ncols, waits=w, par=mb % 2)
            bank = bank_fn(mb)
            for kc in range(32):
                ev_mm[mb] = bld.op("pe", lambda e, mb=mb, kc=kc, bank=bank: e.matmul(
                    bank,
                    ap2(wbuf[mb % 2], 0, 128, [[1, ncols]], kc * ncols),
                    xsl_fn(kc, 0, tlen),
                    start=(kc == 0), stop=(kc == 31)),
                    waits=([evw] + ([ev_free[mb - 2]] if mb >= 2 else []))
                    if kc == 0 else [], inc="pe" if kc == 31 else None)
            evac_fn(mb, bank, ev_mm[mb])
            ev_free[mb] = bld.now("dv")

    def emit_s1_sb(sbi):
        pre = sbi < 2
        t0 = (sbi % 2) * 512
        x_h_src = xpre_h if pre else xmain_h
        cosT, sinT = (cosp, sinp) if pre else (cosm, sinm)
        ktok_dst, vtok_dst = (ktok_p, vtok_p) if pre else (ktok, vtok)
        goff = 0 if pre else T
        nn0 = (0 if pre else 8) + (sbi % 2) * 4
        nl0 = (sbi % 2) * 4

        scope = ExitStack()

        def lsb(name, shape, dt=F32):
            return scope.enter_context(nc.sbuf_tensor(f"{name}_{sbi}", shape, dt))

        if pre:
            xpre_bf = lsb("xpre_bf", [128, 32 * 512], BF16)

            def xsl(kc, tt0, nt):
                return ap2(xpre_bf, 0, 128, [[1, nt]], kc * 512 + tt0)
            dstx = ap2(xpre_bf, 0, 128, [[512, 32], [1, 512]], 0)
        else:
            def xsl(kc, tt0, nt):
                return xT_sl(kc, t0 + tt0, nt)
            dstx = ap2(xT, 0, 128, [[T, 32], [1, 512]], t0)
        srcx = x_h_src.rearrange("(k p) t -> p k t", p=128)[:, :, t0:t0 + 512]
        evx = bld.dma("gp", dstx, srcx, waits=[bld.now("pe"), bld.now("dv")], sem="d_x")

        kT_sb = lsb("kT_sb", [128, KVH * 512], BF16) if pre else None
        vT_sb = lsb("vT_sb", [128, 512], BF16)
        wbuf = [lsb("wbuf0", [128, 32 * 128], BF16), lsb("wbuf1", [128, 32 * 128], BF16)]
        ropeA = lsb("ropeA", [128, 512], F32)
        ropeB = lsb("ropeB", [128, 512], F32)

        def krope_dst(mb, tt0, nt, parts=(0, 128)):
            if pre:
                return apview(kT_sb.ap(), parts[0], parts[1], mb * 512 + tt0, [[1, nt]])
            return apview(kT.ap(), parts[0], parts[1], mb * T + t0 + tt0, [[1, nt]])

        cssl = cosT[:, t0:t0 + 512]
        snsl = sinT[:, t0:t0 + 512]

        def k_evac(mb, bank, ev_mm):
            bld.op("dv", lambda e, bank=bank: e.tensor_mul(ropeA[0:64, :], bank[0:64, :], cssl[0:64, :]),
                   waits=[ev_mm])
            bld.op("dv", lambda e, bank=bank: e.tensor_mul(ropeA[64:128, :], bank[64:128, :], cssl[64:128, :]))
            bld.op("dv", lambda e, bank=bank: e.tensor_mul(ropeB[0:64, :], bank[64:128, :], snsl[:]))
            bld.op("dv", lambda e, bank=bank: e.tensor_mul(ropeB[64:128, :], bank[0:64, :], snsl[:]))
            bld.op("dv", lambda e, mb=mb: e.tensor_sub(krope_dst(mb, 0, 512, (0, 64)),
                                                       ropeA[0:64, :], ropeB[0:64, :]))
            bld.op("dv", lambda e, mb=mb: e.tensor_add(krope_dst(mb, 0, 512, (64, 64)),
                                                       ropeA[64:128, :], ropeB[64:128, :]),
                   inc="dv")
            ev_rope = bld.now("dv")
            for n4 in range(4):
                nl = nl0 + n4
                srck = krope_dst(mb, n4 * 128, 128)
                half = (n4 % 2) * 512
                evt = bld.op("pe", lambda e, srck=srck, half=half: e.transpose(
                    apview(psb.ap(), 0, 128, half, [[1, 128]]), srck, identb[:]),
                    waits=[ev_rope, bld.now("dv")], inc="pe")
                dstk = ap2(ktok_dst, 0, 128, [[1, 128]], (nl * KVH + mb) * D)
                bld.op("dv", lambda e, dstk=dstk, half=half: e.tensor_copy(
                    dstk, apview(psb.ap(), 0, 128, half, [[1, 128]])),
                    waits=[evt], inc="dv")

        emit_proj(wkT_h, KVH, xsl, 512, lambda mb: psf[mb % 2][:],
                  k_evac, wbuf, pre_waits=[evx])

        def v_evac(mb, bank, ev_mm):
            bld.op("dv", lambda e, bank=bank: e.tensor_copy(vT_sb[:], bank), waits=[ev_mm], inc="dv")
            ev_c = bld.now("dv")
            for n4 in range(4):
                nl = nl0 + n4
                half = (n4 % 2) * 512
                evt = bld.op("pe", lambda e, n4=n4, half=half: e.transpose(
                    apview(psb.ap(), 0, 128, half, [[1, 128]]),
                    vT_sb[:, n4 * 128:(n4 + 1) * 128], identb[:]),
                    waits=[ev_c, bld.now("dv")], inc="pe")
                dstv = ap2(vtok_dst, 0, 128, [[1, 128]], (nl * KVH + mb) * D)
                bld.op("dv", lambda e, dstv=dstv, half=half: e.tensor_copy(
                    dstv, apview(psb.ap(), 0, 128, half, [[1, 128]])),
                    waits=[evt], inc="dv")

        emit_proj(wvT_h, KVH, xsl, 512, lambda mb: psf[2 + mb % 2][:],
                  v_evac, wbuf, pre_waits=[bld.now("pe")])

        dt_sl = lsb("dt_sl", [H, 512], F32)
        sp1 = lsb("sp1", [H, 512], F32)
        sp2 = lsb("sp2", [H, 512], F32)
        gsl = lsb("gsl", [H, 512], F32)
        gon = lsb("gon", [H, 128], F32)
        dts = dt_sl[:, :]

        def dt_evac(mb, bank, ev_mm):
            bld.op("dv", lambda e, bank=bank: e.tensor_scalar(sp1[:], bank, dtb_col, None, ALU.add),
                   waits=[ev_mm])
            bld.op("dv", lambda e: e.tensor_relu(sp2[:], sp1[:]), inc="dv")
            bld.op("ac", lambda e: e.activation(sp1[:], sp1[:], AF.Abs), waits=[bld.now("dv")])
            bld.op("ac", lambda e: e.activation(sp1[:], sp1[:], AF.Exp, scale=-1.0))
            bld.op("ac", lambda e: e.activation(sp1[:], sp1[:], AF.Ln, bias=1.0), inc="ac")
            bld.op("dv", lambda e: e.tensor_add(dts, sp1[:], sp2[:]),
                   waits=[bld.now("ac")], inc="dv")

        emit_proj(winT_h, 1, xsl, 512, lambda mb: psf[4][0:H, :],
                  dt_evac, wbuf, ncols=H, pre_waits=[bld.now("pe")])

        ev_sp = bld.now("dv")
        bld.op("gp", lambda e: e.memset(gon[:], 1.0), inc="gp")
        bld.op("dv", lambda e: e.tensor_scalar(gsl[:], dts, A_col, None, ALU.mult),
               waits=[ev_sp, bld.now("gp"), bld.now("ac")])
        Gs = ap2(G_all, 0, H, [[1, 512]], goff + t0)
        for n4 in range(4):
            bld.op("dv", lambda e, n4=n4, Gs=Gs: e.tensor_tensor_scan(
                apview(Gs, 0, H, n4 * 128, [[1, 128]]),
                gon[:], gsl[:, n4 * 128:(n4 + 1) * 128], 0.0, ALU.mult, ALU.add),
                inc="dv" if n4 == 3 else None)
        ev_G = bld.now("dv")

        for n4 in range(4):
            nn = nn0 + n4
            srcd = dt_sl[:, n4 * 128:(n4 + 1) * 128]
            srcg = ap2(G_all, 0, H, [[1, 128]], goff + t0 + n4 * 128)
            ev1 = bld.op("pe", lambda e, srcd=srcd: e.transpose(psf[1][:, 0:H], srcd,
                                                                ident[0:H, 0:H]),
                         waits=[ev_G, bld.now("ac")], inc="pe")
            bld.op("ac", lambda e, nn=nn: e.copy(ap2(dt_tok, 0, 128, [[1, H]], nn * H),
                                                 psf[1][:, 0:H]),
                   waits=[ev1], inc="ac")
            ev2 = bld.op("pe", lambda e, srcg=srcg: e.transpose(psf[2][:, 0:H], srcg,
                                                                ident[0:H, 0:H]),
                         waits=[ev_G, bld.now("ac")], inc="pe")
            bld.op("ac", lambda e, nn=nn: e.copy(ap2(Gcol, 0, 128, [[1, H]], nn * H),
                                                 psf[2][:, 0:H]),
                   waits=[ev2], inc="ac")

        bld.barrier()
        scope.close()

    emit_s1_sb(0)
    emit_s1_sb(1)

    dbg("vtok_p", vtok_p.ap(), [128, NCH * KVH * D])
    dbg("ktok_p", ktok_p.ap(), [128, NCH * KVH * D])

    def emit_head_cols(h, pre):
        goff = 0 if pre else T
        nnb = 0 if pre else 8
        evr = bld.dma("sy", grow.ap(), ap2(G_all, h, 1, [[1, T]], goff),
                      waits=[bld.now("pe"), bld.now("dv"), bld.now("ac")], sem="d_gr")
        gl_row = apview(grow.ap(), 0, 1, 127, [[128, NCH]])
        ev1 = bld.op("pe", lambda e, gl_row=gl_row: e.matmul(
            psf[3][:, 0:NCH], onesrow[:], gl_row, start=True, stop=True),
            waits=[evr, bld.now("dv")], inc="pe")
        gc_sl = ap2(Gcol, 0, 128, [[H, NCH]], nnb * H + h)
        bld.op("dv", lambda e, gc_sl=gc_sl: e.tensor_sub(gl_cols[:, 0:NCH], psf[3][:, 0:NCH], gc_sl),
               waits=[ev1], inc="dv")
        bld.op("ac", lambda e: e.activation(gl_cols[:, 0:NCH], gl_cols[:, 0:NCH], AF.Exp),
               waits=[bld.now("dv")])
        bld.op("ac", lambda e: e.activation(gl_cols[:, NCH:2 * NCH], psf[3][:, 0:NCH], AF.Exp),
               inc="ac")
        return bld.now("ac")

    for h in range(H):
        ev_cols = emit_head_cols(h, pre=True)
        kv = h // 4
        for n in range(NCH):
            bld.op("dv", lambda e, n=n, kv=kv, h=h: e.tensor_scalar(
                xcb[:], ap2(vtok_p, 0, 128, [[1, 128]], (n * KVH + kv) * D),
                ap2(dt_tok, 0, 128, [[H, 1]], n * H + h), None, ALU.mult),
                waits=[ev_cols])
            bld.op("dv", lambda e, n=n, kv=kv: e.tensor_scalar(
                kdb[:], ap2(ktok_p, 0, 128, [[1, 128]], (n * KVH + kv) * D),
                gl_cols[:, n:n + 1], None, ALU.mult), inc="dv")
            evk = bld.now("dv")
            evm = bld.op("pe", lambda e: e.matmul(psf[4][:, 0:128], kdb[:], xcb[:],
                                                  start=True, stop=True),
                         waits=[evk], inc="pe")
            bld.op("dv", lambda e, h=h, n=n: e.scalar_tensor_tensor(
                ap2(states, 0, 128, [[1, 128]], h * D),
                ap2(states, 0, 128, [[1, 128]], h * D),
                gl_cols[:, NCH + n:NCH + n + 1],
                psf[4][:, 0:128], ALU.mult, ALU.add),
                waits=[evm], inc="dv")
    bld.barrier()
    dbg("states", states.ap(), [128, H * D])
    pre_scope.close()

    xT = nc.alloc_sbuf_tensor("xT", [128, 32 * T], BF16, side="right")
    emit_s1_sb(2)
    emit_s1_sb(3)
    dbg("kT", kT.ap(), [128, KVH * T])
    dbg("vtok", vtok.ap(), [128, NCH * KVH * D])
    dbg("ktok", ktok.ap(), [128, NCH * KVH * D])
    dbg("G_all", G_all.ap(), [H, 2 * T])
    dbg("Gcol", Gcol.ap(), [128, 16 * H])
    dbg("dt_tok", dt_tok.ap(), [128, 16 * H])

    if stages < 3:
        return finish(nc, bld, es, dbg_tensors)

    s2 = ExitStack()

    def s2b(name, shape, dt=F32):
        return s2.enter_context(nc.sbuf_tensor(name, shape, dt))

    qT_h = s2b("qT_h", [128, T], BF16)
    gsil_h = s2b("gsil_h", [128, T], BF16)
    Y_h = s2b("Y_h", [128, T], BF16)
    wqbuf = [s2b("wqbuf0", [128, 32 * 128], BF16), s2b("wqbuf1", [128, 32 * 128], BF16)]
    ropeA2 = s2b("ropeA2", [128, 512], F32)
    ropeB2 = s2b("ropeB2", [128, 512], F32)
    dfT = s2b("dfT", [128, 128], F32)
    dbgyA = s2b("dbgyA", [128, 128], F32)
    dbgyB = s2b("dbgyB", [128, 128], F32)
    egb = s2b("egb", [128, 128], F32)
    STDb = s2b("STDb", [128, 128], F32)
    qscb = s2b("qscb", [128, 128], F32)
    prb = s2b("prb", [128, 128], F32)
    sqb = s2b("sqb", [128, 128], F32)
    t5b = s2b("t5b", [128, 128], F32)
    rmsr = s2b("rmsr", [1, 128], F32)
    rstdr = s2b("rstdr", [1, 128], F32)

    ev_w_prev = {}

    def emit_qg_proj(w_h, h, dst_rope, silu_dst, free_evs=()):
        par = emit_qg_proj.parity
        emit_qg_proj.parity ^= 1
        w = [ev_w_prev[par]] if par in ev_w_prev else [bld.now("pe")]
        evw = emit_wcol_load(wqbuf[par], w_h, h, 128, waits=w, par=par)
        evs = []
        for tb in range(2):
            for kc in range(32):
                ev = bld.op("pe", lambda e, kc=kc, tb=tb, par=par: e.matmul(
                    psf[tb][:],
                    ap2(wqbuf[par], 0, 128, [[1, 128]], kc * 128),
                    xT_sl(kc, tb * 512, 512),
                    start=(kc == 0), stop=(kc == 31)),
                    waits=([evw] + list(free_evs)) if kc == 0 else [],
                    inc="pe" if kc == 31 else None)
            evs.append(ev)
        ev_w_prev[par] = evs[-1]
        for tb in range(2):
            bank = psf[tb]
            if dst_rope is not None:
                cssl = cosm[:, tb * 512:(tb + 1) * 512]
                snsl = sinm[:, tb * 512:(tb + 1) * 512]
                bld.op("dv", lambda e, bank=bank, cssl=cssl: e.tensor_mul(
                    ropeA2[0:64, :], bank[0:64, :], cssl[0:64, :]), waits=[evs[tb]])
                bld.op("dv", lambda e, bank=bank, cssl=cssl: e.tensor_mul(
                    ropeA2[64:128, :], bank[64:128, :], cssl[64:128, :]))
                bld.op("dv", lambda e, bank=bank, snsl=snsl: e.tensor_mul(
                    ropeB2[0:64, :], bank[64:128, :], snsl[:]))
                bld.op("dv", lambda e, bank=bank, snsl=snsl: e.tensor_mul(
                    ropeB2[64:128, :], bank[0:64, :], snsl[:]))
                bld.op("dv", lambda e, tb=tb, dst_rope=dst_rope: e.tensor_sub(
                    apview(dst_rope, 0, 64, tb * 512, [[1, 512]]),
                    ropeA2[0:64, :], ropeB2[0:64, :]))
                bld.op("dv", lambda e, tb=tb, dst_rope=dst_rope: e.tensor_add(
                    apview(dst_rope, 64, 64, tb * 512, [[1, 512]]),
                    ropeA2[64:128, :], ropeB2[64:128, :]), inc="dv")
            else:
                bld.op("ac", lambda e, bank=bank, tb=tb: e.activation(
                    silu_dst[:, tb * 512:(tb + 1) * 512], bank[:], AF.Silu),
                    waits=[evs[tb]], inc="ac")
        return bld.now("dv") if dst_rope is not None else bld.now("ac")
    emit_qg_proj.parity = 0

    pc = {"STD": None, "egb": None, "qsc": None, "sq": None, "rms": None, "Y": None,
          "rstd": None}

    for h in range(H):
        kv = h // 4
        ev_cols = emit_head_cols(h, pre=False)
        ev_q = emit_qg_proj(wqT_h, h, qT_h.ap(), None,
                            free_evs=[pc["STD"], pc["egb"], pc.get("gsilu")])
        ev_g = emit_qg_proj(wgT_h, h, None, gsil_h, free_evs=[ev_q])
        pc["gsilu"] = ev_g
        for n in range(NCH):
            col = (8 + n) * H + h
            ev_st = bld.op("pe", lambda e, kv=kv, n=n: e.matmul(
                psf[0][:, 0:128],
                ap2(kT, 0, 128, [[1, 128]], kv * T + n * 128),
                ap2(qT_h, 0, 128, [[1, 128]], n * 128),
                start=True, stop=True),
                waits=[ev_q, ev_g, pc["STD"]], inc="pe")
            ev_gi = bld.op("pe", lambda e, n=n: e.matmul(
                psf[1][:, 0:128], onesrow[:],
                apview(grow.ap(), 0, 1, n * 128, [[1, 128]]),
                start=True, stop=True),
                waits=[ev_cols, ev_g, pc["STD"], pc["egb"]], inc="pe")
            ev3 = bld.op("dv", lambda e, col=col: e.scalar_tensor_tensor(
                dfT[:], psf[1][:, 0:128], ap2(Gcol, 0, 128, [[H, 1]], col),
                maskneg[:], ALU.subtract, ALU.add),
                waits=[ev_gi], inc="dv")
            ev4 = bld.op("ac", lambda e: e.activation(dfT[:], dfT[:], AF.Exp),
                         waits=[ev3], inc="ac")
            ev5 = bld.op("ac", lambda e: e.activation(egb[:], psf[1][:, 0:128], AF.Exp),
                         waits=[ev_gi, pc["qsc"]], inc="ac")
            ev6 = bld.op("dv", lambda e: e.tensor_mul(STDb[:], dfT[:], psf[0][:, 0:128]),
                         waits=[ev4, ev_st], inc="dv")
            pc["STD"] = ev6
            ev7 = bld.op("dv", lambda e, n=n: e.tensor_mul(
                qscb[:], ap2(qT_h, 0, 128, [[1, 128]], n * 128), egb[:]),
                waits=[ev5], inc="dv")
            pc["qsc"] = ev7
            pc["egb"] = ev5
            bld.op("dv", lambda e, n=n, kv=kv, col=col: e.tensor_scalar(
                xcb[:], ap2(vtok, 0, 128, [[1, 128]], (n * KVH + kv) * D),
                ap2(dt_tok, 0, 128, [[H, 1]], col), None, ALU.mult))
            bld.op("dv", lambda e, n=n, kv=kv: e.tensor_scalar(
                kdb[:], ap2(ktok, 0, 128, [[1, 128]], (n * KVH + kv) * D),
                gl_cols[:, n:n + 1], None, ALU.mult))
            ev10 = bld.op("dv", lambda e, h=h: e.tensor_copy(
                prb[:], ap2(states, 0, 128, [[1, 128]], h * D)), inc="dv")
            bld.op("pe", lambda e: e.matmul(psf[2][:, 0:128], xcb[:], STDb[:],
                                            start=True, stop=False),
                   waits=[ev10, pc["sq"], ev6], inc=None)
            ev_y = bld.op("pe", lambda e: e.matmul(psf[2][:, 0:128], prb[:], qscb[:],
                                                   start=False, stop=True),
                          waits=[ev7], inc="pe")
            ev_cs = bld.op("pe", lambda e: e.matmul(psf[3][:, 0:128], kdb[:], xcb[:],
                                                    start=True, stop=True),
                           waits=[ev10], inc="pe")
            bld.op("dv", lambda e, h=h, n=n: e.scalar_tensor_tensor(
                ap2(states, 0, 128, [[1, 128]], h * D),
                ap2(states, 0, 128, [[1, 128]], h * D),
                gl_cols[:, NCH + n:NCH + n + 1],
                psf[3][:, 0:128], ALU.mult, ALU.add),
                waits=[ev_cs], inc="dv")
            ev_sq = bld.op("ac", lambda e: e.activation(sqb[:], psf[2][:, 0:128], AF.Square),
                           waits=[ev_y], inc="ac")
            pc["sq"] = ev_sq
            if DBG_HN == (h, n):
                bld.op("ac", lambda e: e.copy(dbgyA[:], psf[2][:, 0:128]),
                       waits=[ev_y], inc="ac")
            ev_ssq = bld.op("pe", lambda e: e.matmul(psf[4][0:1, 0:128], ones_col, sqb[:],
                                                     start=True, stop=True),
                            waits=[ev_sq, pc["rms"]], inc="pe")
            ev_rms = bld.op("ac", lambda e: e.activation(rmsr[:], psf[4][0:1, 0:128], AF.Sqrt,
                                                         scale=1.0 / 128.0,
                                                         bias=ccol(C_EPS, 1)),
                            waits=[ev_ssq, pc["rstd"]], inc="ac")
            pc["rms"] = ev_rms
            ev_rstd = bld.op("dv", lambda e: e.reciprocal(rstdr[:], rmsr[:]),
                             waits=[ev_rms], inc="dv")
            pc["rstd"] = ev_rstd
            ev_rb = bld.op("pe", lambda e: e.matmul(psf[5][:, 0:128], onesrow[:], rstdr[:],
                                                    start=True, stop=True),
                           waits=[ev_rstd, pc["Y"]], inc="pe")
            ev_t5 = bld.op("dv", lambda e, n=n: e.scalar_tensor_tensor(
                t5b[:], psf[2][:, 0:128], gnorm_col,
                ap2(gsil_h, 0, 128, [[1, 128]], n * 128), ALU.mult, ALU.mult),
                waits=[ev_y, ev_g], inc="dv")
            if DBG_HN == (h, n):
                bld.op("ac", lambda e: e.copy(dbgyB[:], psf[2][:, 0:128]),
                       waits=[bld.now("dv")], inc="ac")
            ev_Y = bld.op("dv", lambda e, n=n: e.tensor_mul(
                ap2(Y_h, 0, 128, [[1, 128]], n * 128), t5b[:], psf[5][:, 0:128]),
                waits=[ev_rb], inc="dv")
            pc["Y"] = ev_Y
            if DBG_HN == (h, n):
                for nm, tt in (("dfT", dfT), ("STDb", STDb), ("qscb", qscb),
                               ("egb", egb), ("prb", prb), ("xcb", xcb),
                               ("kdb", kdb), ("t5b", t5b), ("rmsr", rmsr),
                               ("dbgyA", dbgyA), ("dbgyB", dbgyB)):
                    dbg(nm, tt.ap(), list(tt.ap().shape))
                dbg("Yn", ap2(Y_h, 0, 128, [[1, 128]], n * 128), [128, 128])
                bld.barrier()
        ev_ys = bld.dma("sy", y_s[h * 128:(h + 1) * 128, :], Y_h.ap(), waits=[pc["Y"]],
                        sem="d_y")
        pc["Y"] = ev_ys

    bld.barrier()
    s2.close()
    scope_m.close()
    dbg_dram("y_s", y_s, [H * D, T], BF16)

    if stages < 4:
        return finish(nc, bld, es, dbg_tensors)

    s3 = ExitStack()

    def s3b(name, shape, dt=F32):
        return s3.enter_context(nc.sbuf_tensor(name, shape, dt))

    Y_all = s3b("Y_all", [128, H * T], BF16)
    wobuf = [s3b("wobuf0", [128, 32 * 128], BF16), s3b("wobuf1", [128, 32 * 128], BF16)]
    sqt = s3b("sqt", [128, T], BF16)
    rows = s3b("rows", [1, 4 * T], F32)
    ut = s3b("ut", [128, 512], F32)

    ev_yl = bld.dma("sy", ap2(Y_all, 0, 128, [[T, H], [1, T]], 0),
                    y_s.rearrange("(h p) t -> p h t", p=128), sem="d_y")

    def emit_ln(xsrc_sl, dma_out, out_dt_bf=True):
        pass

    ev_w_prev2 = {}
    ev_evac2 = {}
    for mc in range(32):
        par = mc % 2
        w = [ev_w_prev2[par]] if par in ev_w_prev2 else [bld.now("pe")]
        evw = emit_wcol_load(wobuf[par], woT_h, mc, 128, waits=w, par=par)
        for tb in range(2):
            bank = psf[tb]
            for hc in range(32):
                ev = bld.op("pe", lambda e, par=par, hc=hc, tb=tb, bank=bank: e.matmul(
                    bank[:], ap2(wobuf[par], 0, 128, [[1, 128]], hc * 128),
                    ap2(Y_all, 0, 128, [[1, 512]], hc * T + tb * 512),
                    start=(hc == 0), stop=(hc == 31)),
                    waits=([evw, ev_yl] + ([ev_evac2.get(mc - 2)] if mc >= 2 else []))
                    if hc == 0 else [],
                    inc="pe" if hc == 31 else None)
            ev_w_prev2[par] = ev
            xsl = ap2(xT, 0, 128, [[1, 512]], mc * T + tb * 512)
            bld.op("dv", lambda e, xsl=xsl, bank=bank: e.tensor_add(xsl, xsl, bank[:]),
                   waits=[ev], inc="dv")
        ev_evac2[mc] = bld.now("dv")
    bld.barrier()

    for mc in range(32):
        xsl = ap2(xT, 0, 128, [[1, T]], mc * T)
        ev_s = bld.op("ac", lambda e, xsl=xsl: e.activation(sqt[:], xsl, AF.Square),
                      waits=[bld.now("pe")], inc="ac")
        for tb in range(2):
            bld.op("pe", lambda e, mc=mc, tb=tb: e.matmul(
                psf[2 + tb][0:1, :], onescol_bf[:],
                ap2(xT, 0, 128, [[1, 512]], mc * T + tb * 512),
                start=(mc == 0), stop=(mc == 31)),
                waits=[], inc="pe" if mc == 31 else None)
            bld.op("pe", lambda e, mc=mc, tb=tb: e.matmul(
                psf[4 + tb][0:1, :], onescol_bf[:], sqt[:, tb * 512:(tb + 1) * 512],
                start=(mc == 0), stop=(mc == 31)),
                waits=[ev_s], inc="pe" if mc == 31 else None)
    ev_stats = bld.now("pe")

    def emit_ln_rows(ev_stats):
        for tb in range(2):
            bld.op("dv", lambda e, tb=tb: e.tensor_scalar(
                rows[0:1, tb * 512:(tb + 1) * 512], psf[2 + tb][0:1, :],
                1.0 / HID, None, ALU.mult), waits=[ev_stats])
            bld.op("dv", lambda e, tb=tb: e.tensor_scalar(
                rows[0:1, T + tb * 512:T + (tb + 1) * 512], psf[4 + tb][0:1, :],
                1.0 / HID, None, ALU.mult), inc="dv")
        ev1 = bld.now("dv")
        bld.op("ac", lambda e: e.activation(rows[0:1, 2 * T:3 * T], rows[0:1, 0:T],
                                            AF.Square), waits=[ev1], inc="ac")
        ev2 = bld.now("ac")
        bld.op("dv", lambda e: e.tensor_sub(rows[0:1, T:2 * T], rows[0:1, T:2 * T],
                                            rows[0:1, 2 * T:3 * T]), waits=[ev2], inc="dv")
        ev3 = bld.now("dv")
        bld.op("ac", lambda e: e.activation(rows[0:1, 2 * T:3 * T], rows[0:1, T:2 * T],
                                            AF.Sqrt, bias=ccol(C_EPS, 1)),
               waits=[ev3], inc="ac")
        ev4 = bld.now("ac")
        bld.op("dv", lambda e: e.reciprocal(rows[0:1, 3 * T:4 * T], rows[0:1, 2 * T:3 * T]),
               waits=[ev4], inc="dv")
        return bld.now("dv")

    ev_rows = emit_ln_rows(ev_stats)

    ev_bc = []
    for tb in range(2):
        bld.op("pe", lambda e, tb=tb: e.matmul(
            psf[2 + tb][:], onesrow[:], rows[0:1, tb * 512:(tb + 1) * 512],
            start=True, stop=True), waits=[ev_rows])
        ev_bc.append(bld.op("pe", lambda e, tb=tb: e.matmul(
            psf[4 + tb][:], onesrow[:], rows[0:1, 3 * T + tb * 512:3 * T + (tb + 1) * 512],
            start=True, stop=True), inc="pe"))

    for mc in range(32):
        for tb in range(2):
            xsl = ap2(xT, 0, 128, [[1, 512]], mc * T + tb * 512)
            bld.op("dv", lambda e, xsl=xsl, tb=tb: e.tensor_sub(ut[:], xsl, psf[2 + tb][:]),
                   waits=[ev_bc[tb]])
            bld.op("dv", lambda e, xsl=xsl, tb=tb: e.tensor_mul(xsl, ut[:], psf[4 + tb][:]),
                   inc="dv")
        bld.dma("sy", xm_s[mc * 128:(mc + 1) * 128, :],
                ap2(xT, 0, 128, [[1, T]], mc * T), waits=[bld.now("dv")], sem="d_xm")
    bld.barrier()
    s3.close()
    dbg_dram("xm_s", xm_s, [HID, T], BF16)

    if stages < 5:
        return finish(nc, bld, es, dbg_tensors)

    s4 = ExitStack()

    def s4b(name, shape, dt=F32):
        return s4.enter_context(nc.sbuf_tensor(name, shape, dt))

    pk1b = [s4b("pk1b0", [128, 4096], I8), s4b("pk1b1", [128, 4096], I8)]
    pk3b = [s4b("pk3b0", [128, 4096], I8), s4b("pk3b1", [128, 4096], I8)]
    sc1b = [s4b("sc1b0", [2, 4096], BF16), s4b("sc1b1", [2, 4096], BF16)]
    sc3b = [s4b("sc3b0", [2, 4096], BF16), s4b("sc3b1", [2, 4096], BF16)]
    W1c = [s4b("W1c0", [128, 4096], BF16), s4b("W1c1", [128, 4096], BF16)]
    W3c = [s4b("W3c0", [128, 4096], BF16), s4b("W3c1", [128, 4096], BF16)]
    asb = s4b("asb", [128, 512], BF16)
    hst = s4b("hst", [128, T], BF16)
    fence4 = s4b("fence4", [1, 4], F32)
    scsb = s4b("scsb", [128, 4096], BF16)

    ev_mm_ob = {}
    ev_stt_ob = {}
    ev_hdma = {}
    ev_ext_prev = {None: None}

    def emit_dequant(ob, par, pk, sc, Wc, pk_h, sc_h, waits_pk, ev_sttprev):
        evp = bld.dma("sy", pk[par].ap(), pk_h[ob * 128:(ob + 1) * 128, :],
                      waits=waits_pk, sem=f"d_s4_{par}")
        evs = bld.dma("gp", sc[par].ap(), sc_h[ob * 2:(ob + 1) * 2, :],
                      waits=[bld.now("pe")], sem=f"d_s4s_{par}")
        eve = bld.op("dv", lambda e, par=par, pk=pk: e.tensor_scalar(
            pk[par][:], pk[par][:], maski[:], xori[:],
            ALU.bitwise_and, ALU.bitwise_xor), waits=[evp], inc="dv")
        ev_evac = {}
        for sl in range(8):
            bank = psf[4 + (sl % 2)]
            evm = bld.op("pe", lambda e, sl=sl, par=par, sc=sc, bank=bank:
                         e.matmul(bank[:], sel2b[:],
                                  sc[par][0:2, sl * 512:(sl + 1) * 512],
                                  start=True, stop=True),
                         waits=[evs, ev_evac.get(sl - 2)], inc="pe")
            ev_evac[sl] = bld.op("ac", lambda e, sl=sl, bank=bank: e.copy(
                scsb[:, sl * 512:(sl + 1) * 512], bank[:]),
                waits=[evm] + ([ev_sttprev] if sl == 0 else []), inc="ac")
        ev = bld.op("dv", lambda e, par=par, pk=pk, Wc=Wc: e.scalar_tensor_tensor(
            Wc[par][:], pk[par][:], off_col, scsb[:], ALU.subtract, ALU.mult),
            waits=[eve, ev_evac[7]] + list(waits_pk), inc="dv")
        return ev

    ev_stt_last = None
    for ob in range(NOB):
        if PARANOID_S4:
            bld.barrier()
        par = ob % 2
        wfree = [ev_mm_ob.get(ob - 2)]
        e1f = emit_dequant(ob, par, pk1b, sc1b, W1c, w1d_h, s1re_h, wfree, ev_stt_last)
        e3f = emit_dequant(ob, par, pk3b, sc3b, W3c, w3d_h, s3re_h, wfree, e1f)
        ev_stt_last = e3f
        e1 = [e1f]
        e3 = [e3f]
        evab = {}
        for tb in range(2):
            for j in range(32):
                w = []
                if j == 0:
                    w = [e1[-1], e3[-1], bld.now("ac"), ev_hdma.get(ob - 1)]
                evab[("a", tb)] = bld.op("pe", lambda e, tb=tb, j=j, par=par: e.matmul(
                    psf[tb][:], ap2(W1c[par], 0, 128, [[1, 128]], j * 128),
                    xT_sl(j, tb * 512, 512),
                    start=(j == 0), stop=(j == 31)),
                    waits=w, inc="pe" if j == 31 else None)
            for j in range(32):
                evab[("b", tb)] = bld.op("pe", lambda e, tb=tb, j=j, par=par: e.matmul(
                    psf[2 + tb][:], ap2(W3c[par], 0, 128, [[1, 128]], j * 128),
                    xT_sl(j, tb * 512, 512),
                    start=(j == 0), stop=(j == 31)),
                    waits=[e3[-1]] if j == 0 else [],
                    inc="pe" if j == 31 else None)
        ev_mm_ob[ob] = evab[("b", 1)]
        for tb in range(2):
            eva = bld.op("ac", lambda e, tb=tb: e.activation(asb[:], psf[tb][:], AF.Silu),
                         waits=[evab[("a", tb)], bld.now("dv")], inc="ac")
            bld.op("dv", lambda e, tb=tb: e.tensor_mul(
                hst[:, tb * 512:(tb + 1) * 512], asb[:], psf[2 + tb][:]),
                waits=[eva, evab[("b", tb)], ev_hdma.get(ob - 1)], inc="dv")
        ev_hdma[ob] = bld.dma("sy", h_s[ob * 128:(ob + 1) * 128, :], hst.ap(),
                              waits=[bld.now("dv")], sem="d_hs")
        if DBG_OB == ob:
            dbg("W1c", W1c[par].ap(), [128, 4096])
            dbg("W3c", W3c[par].ap(), [128, 4096])
            bld.barrier()
    bld.barrier()
    s4.close()
    dbg_dram("h_s", h_s, [NOB * 128, T], BF16)

    if stages < 6:
        return finish(nc, bld, es, dbg_tensors)

    s5 = ExitStack()

    def s5b(name, shape, dt=F32):
        return s5.enter_context(nc.sbuf_tensor(name, shape, dt))

    MG = [2] * 16
    W2c = [s5b(f"W2c{i}", [128, FFH], BF16) for i in range(2)]
    pk2b = s5b("pk2b", [128, FFH], I8)
    ss2 = s5b("ss2", [2, 2560], BF16)
    scsb5 = s5b("scsb5", [128, FFH], BF16)
    hb = [s5b("hb0", [128, T], BF16), s5b("hb1", [128, T], BF16)]
    fence5 = s5b("fence5", [1, 4], F32)
    xmt = [s5b("xmt0", [128, 512], BF16), s5b("xmt1", [128, 512], BF16)]
    r2st = [s5b("r2st0", [128, 512], BF16), s5b("r2st1", [128, 512], BF16)]

    mg_base = 0
    ev_mm_done = None
    ev_r2st = {}
    ev_stt5 = None
    for gi, gsz in enumerate(MG):
        ev_stts = []
        for mb4 in range(gsz):
            mb = mg_base + mb4
            evp = bld.dma("sy", pk2b.ap(), w2d_h[mb * 128:(mb + 1) * 128, :],
                          waits=[bld.now("gp"), bld.now("dv"), ev_mm_done], sem="d_s5")
            eve = bld.op("dv", lambda e: e.tensor_scalar(
                pk2b[:], pk2b[:], maski[:], xori[:],
                ALU.bitwise_and, ALU.bitwise_xor), waits=[evp], inc="dv")
            sl0 = 0
            ev_evac = {}
            si = 0
            for ch in range(5):
                cw = 2560 if ch < 4 else FFH - 4 * 2560
                evs = bld.dma("gp", ss2[0:2, 0:cw],
                              s2re_h[mb * 2:(mb + 1) * 2, sl0:sl0 + cw],
                              waits=[bld.now("ac")], sem="d_s5s")
                nmm = (cw + 511) // 512
                for q2 in range(nmm):
                    qw = min(512, cw - q2 * 512)
                    bank = psf[4 + si % 3]
                    evm = bld.op("pe", lambda e, q2=q2, qw=qw, bank=bank: e.matmul(
                        bank[:, 0:qw], sel2b[:], ss2[0:2, q2 * 512:q2 * 512 + qw],
                        start=True, stop=True),
                        waits=[evs, ev_evac.get(si - 3)], inc="pe")
                    ev_evac[si] = bld.op("ac", lambda e, sl0=sl0, q2=q2, qw=qw, bank=bank:
                                         e.copy(scsb5[:, sl0 + q2 * 512:sl0 + q2 * 512 + qw],
                                                bank[:, 0:qw]),
                                         waits=[evm] + ([ev_stt5] if si == 0 else []),
                                         inc="ac")
                    si += 1
                sl0 += cw
            ev_stt5 = bld.op("dv", lambda e, mb4=mb4: e.scalar_tensor_tensor(
                W2c[mb4][:], pk2b[:], off_col, scsb5[:], ALU.subtract, ALU.mult),
                waits=[eve, ev_evac[si - 1], ev_mm_done], inc="dv")
            ev_stts.append(ev_stt5)
        ev_deq = bld.now("dv")
        ev_h = {}
        ev_last = None
        for j in range(NOB):
            ev_h[j] = bld.dma("sy", hb[j % 2].ap(), h_s[j * 128:(j + 1) * 128, :],
                              waits=([ev_last] if j >= 2 else [bld.now("pe")])
                              + ([("d_hs", 16 * NOB)] if j < 2 else []),
                              sem=f"d_h{j % 2}")
            for mb4 in range(gsz):
                for tb in range(2):
                    ev_last = bld.op("pe", lambda e, j=j, mb4=mb4, tb=tb: e.matmul(
                        psf[mb4 * 2 + tb][:],
                        ap2(W2c[mb4], 0, 128, [[1, 128]], j * 128),
                        apview(hb[j % 2].ap(), 0, 128, tb * 512, [[1, 512]]),
                        start=(j == 0), stop=(j == NOB - 1)),
                        waits=[ev_h[j], ev_deq] if (mb4 == 0 and tb == 0) else [],
                        inc="pe" if (mb4 == gsz - 1 and tb == 1) else None)
        ev_mm_done = ev_last
        for mb4 in range(gsz):
            mb = mg_base + mb4
            for tb in range(2):
                evx = bld.dma("sy", xmt[tb].ap(),
                              xm_s[mb * 128:(mb + 1) * 128, tb * 512:(tb + 1) * 512],
                              waits=[bld.now("dv")]
                              + ([("d_xm", 16 * 32)] if gi == 0 and mb4 == 0 else []),
                              sem=f"d_xt{tb}")
                bld.op("dv", lambda e, mb4=mb4, tb=tb: e.tensor_add(
                    r2st[tb][:], xmt[tb][:], psf[mb4 * 2 + tb][:]),
                    waits=[evx, ev_mm_done, ev_r2st.get(tb)], inc="dv")
                ev_r2st[tb] = bld.dma("sy",
                        r2_s[mb * 128:(mb + 1) * 128, tb * 512:(tb + 1) * 512],
                        r2st[tb].ap(), waits=[bld.now("dv")], sem="d_r2")
        mg_base += gsz
    bld.barrier()
    s5.close()
    dbg_dram("r2_s", r2_s, [HID, T], BF16)

    s6 = ExitStack()

    def s6b(name, shape, dt=F32):
        return s6.enter_context(nc.sbuf_tensor(name, shape, dt))

    rt = [s6b("rt0", [128, T], BF16), s6b("rt1", [128, T], BF16)]
    sq6 = s6b("sq6", [128, T], BF16)
    ut6 = s6b("ut6", [128, 512], F32)
    ost = s6b("ost", [128, T], F32)
    rows6 = s6b("rows6", [1, 4 * T], F32)

    ev_rt = {}
    ev_cons = {}
    for mc in range(32):
        w = [bld.now("pe"), bld.now("ac")] if mc < 2 else [ev_cons[mc - 2]]
        ev_rt[mc] = bld.dma("sy", rt[mc % 2].ap(), r2_s[mc * 128:(mc + 1) * 128, :],
                            waits=w + ([("d_r2", 16 * 64)] if mc < 2 else []),
                            sem=f"d_rt{mc % 2}")
        ev_s = bld.op("ac", lambda e, mc=mc: e.activation(sq6[:], rt[mc % 2][:], AF.Square),
                      waits=[ev_rt[mc], bld.now("pe")], inc="ac")
        for tb in range(2):
            bld.op("pe", lambda e, mc=mc, tb=tb: e.matmul(
                psf[2 + tb][0:1, :], onescol_bf[:],
                apview(rt[mc % 2].ap(), 0, 128, tb * 512, [[1, 512]]),
                start=(mc == 0), stop=(mc == 31)),
                waits=[ev_rt[mc]], inc=None)
            ev_cons[mc] = bld.op("pe", lambda e, mc=mc, tb=tb: e.matmul(
                psf[4 + tb][0:1, :], onescol_bf[:], sq6[:, tb * 512:(tb + 1) * 512],
                start=(mc == 0), stop=(mc == 31)),
                waits=[ev_s], inc="pe")
    ev_stats6 = bld.now("pe")

    for tb in range(2):
        bld.op("dv", lambda e, tb=tb: e.tensor_scalar(
            rows6[0:1, tb * 512:(tb + 1) * 512], psf[2 + tb][0:1, :],
            1.0 / HID, None, ALU.mult), waits=[ev_stats6])
        bld.op("dv", lambda e, tb=tb: e.tensor_scalar(
            rows6[0:1, T + tb * 512:T + (tb + 1) * 512], psf[4 + tb][0:1, :],
            1.0 / HID, None, ALU.mult), inc="dv")
    bld.op("ac", lambda e: e.activation(rows6[0:1, 2 * T:3 * T], rows6[0:1, 0:T],
                                        AF.Square), waits=[bld.now("dv")], inc="ac")
    bld.op("dv", lambda e: e.tensor_sub(rows6[0:1, T:2 * T], rows6[0:1, T:2 * T],
                                        rows6[0:1, 2 * T:3 * T]),
           waits=[bld.now("ac")], inc="dv")
    bld.op("ac", lambda e: e.activation(rows6[0:1, 2 * T:3 * T], rows6[0:1, T:2 * T],
                                        AF.Sqrt, bias=ccol(C_EPS, 1)),
           waits=[bld.now("dv")], inc="ac")
    bld.op("dv", lambda e: e.reciprocal(rows6[0:1, 3 * T:4 * T], rows6[0:1, 2 * T:3 * T]),
           waits=[bld.now("ac")], inc="dv")
    ev_rows6 = bld.now("dv")

    ev_out = None
    ev_bc6 = []
    for tb in range(2):
        bld.op("pe", lambda e, tb=tb: e.matmul(
            psf[2 + tb][:], onesrow[:], rows6[0:1, tb * 512:(tb + 1) * 512],
            start=True, stop=True), waits=[ev_rows6])
        ev_bc6.append(bld.op("pe", lambda e, tb=tb: e.matmul(
            psf[4 + tb][:], onesrow[:], rows6[0:1, 3 * T + tb * 512:3 * T + (tb + 1) * 512],
            start=True, stop=True), inc="pe"))

    for mc in range(32):
        evr = bld.dma("sy", rt[mc % 2].ap(), r2_s[mc * 128:(mc + 1) * 128, :],
                      waits=[bld.now("dv"), ev_bc6[1]], sem=f"d_rt{mc % 2}")
        for tb in range(2):
            bld.op("dv", lambda e, mc=mc, tb=tb: e.tensor_sub(
                ut6[:], apview(rt[mc % 2].ap(), 0, 128, tb * 512, [[1, 512]]),
                psf[2 + tb][:]), waits=[evr, ev_out])
            bld.op("dv", lambda e, mc=mc, tb=tb: e.tensor_mul(
                ost[:, tb * 512:(tb + 1) * 512], ut6[:], psf[4 + tb][:]), inc="dv")
        ev_out = bld.dma("sy", outT_h[mc * 128:(mc + 1) * 128, :], ost[:],
                         waits=[bld.now("dv")], sem="sto")

    dbg("rows6", rows6.ap(), [1, 4 * T])
    s6.close()
    return finish(nc, bld, es, dbg_tensors)


def finish(nc, bld, es, dbg_tensors):
    bld.op("sy", lambda e: e.nop(), waits=bld.dma_sems_now())
    blk = es.enter_context(nc.Block())
    bld.play(blk)
    es.close()
    return nc, dbg_tensors


def rope_tables():
    inv = 1.0 / (10000.0 ** (np.arange(0, D, 2, dtype=np.float64) / D))
    ang = np.arange(2048, dtype=np.float64)[:, None] * inv[None, :]
    cos = np.cos(ang)
    cos_full = np.concatenate([cos, cos], axis=1).T.astype(np.float32)
    sin64 = np.sin(ang).T.astype(np.float32)
    return np.ascontiguousarray(cos_full), np.ascontiguousarray(sin64)


def pack_w(w_packed_i32, out_f, in_f):
    b = w_packed_i32.view(np.int8).reshape(-1, 64, 4)[:, :, 0]
    nob = out_f // 128
    nj = in_f // 128
    t = b.reshape(out_f, nj, 64).reshape(nob, 128, nj, 64)
    t = np.transpose(t, (0, 3, 2, 1))
    dup = np.concatenate([t, t], axis=1)
    return np.ascontiguousarray(dup.reshape(nob * 128, nj * 128))


def pack_s(s_flat, out_f, in_f):
    nob = out_f // 128
    nj = in_f // 128
    t = s_flat.reshape(out_f, nj, 2).reshape(nob, 128, nj, 2)
    t = np.transpose(t, (0, 3, 2, 1))
    return np.ascontiguousarray(t.reshape(nob * 2, nj * 128).astype(np.float32))


def make_shared(inp):
    cos_full, sin64 = rope_tables()
    maskneg = np.where(np.arange(128)[None, :] >= np.arange(128)[:, None],
                       0.0, -1e30).astype(np.float32)
    sel2 = np.zeros((2, 128), dtype=np.float32)
    sel2[0, :64] = 1.0 / 16.0
    sel2[1, 64:] = 1.0
    sh = {
        "wqT": np.ascontiguousarray(inp["L0_attn_Wq"].T),
        "wkT": np.ascontiguousarray(inp["L0_attn_Wk"].T),
        "wvT": np.ascontiguousarray(inp["L0_attn_Wv"].T),
        "wgT": np.ascontiguousarray(inp["L0_attn_Wg"].T),
        "woT": np.ascontiguousarray(inp["L0_attn_Wo"].T),
        "winT": np.ascontiguousarray(inp["L0_attn_Win"].T),
        "dtb": np.ascontiguousarray(inp["L0_attn_dt_bias"].reshape(H, 1)),
        "alog": np.ascontiguousarray(inp["L0_attn_A_log_bias"].reshape(H, 1)),
        "gnorm": np.ascontiguousarray(inp["L0_attn_gnorm_w"].reshape(D, 1)),
        "maskneg": maskneg,
        "ident": np.eye(128, dtype=np.float32),
        "sel2": sel2,
        "w1d": pack_w(inp["L0_ffn_w1"], FFH, HID),
        "w3d": pack_w(inp["L0_ffn_w3"], FFH, HID),
        "w2d": pack_w(inp["L0_ffn_w2"], HID, FFH),
        "s1re": pack_s(inp["L0_ffn_s1"], FFH, HID),
        "s3re": pack_s(inp["L0_ffn_s3"], FFH, HID),
        "s2re": pack_s(inp["L0_ffn_s2"], HID, FFH),
        "_cos": cos_full, "_sin": sin64,
    }
    return sh


def make_core_inputs(inp, sh, core):
    b, half = core // 2, core % 2
    x = inp["L0_attn_hidden_states"]
    xm = np.ascontiguousarray(x[b, half * T:(half + 1) * T, :].T.astype(np.float32))
    if half == 1:
        xp = np.ascontiguousarray(x[b, 0:T, :].T.astype(np.float32))
    else:
        xp = np.zeros((HID, T), dtype=np.float32)
    off = half * T
    d = {k: v for k, v in sh.items() if not k.startswith("_")}
    d.update({
        "xmain": xm, "xpre": xp,
        "cosm": np.ascontiguousarray(sh["_cos"][:, off:off + T]),
        "sinm": np.ascontiguousarray(sh["_sin"][:, off:off + T]),
        "cosp": np.ascontiguousarray(sh["_cos"][:, 0:T]),
        "sinp": np.ascontiguousarray(sh["_sin"][:, 0:T]),
    })
    return d

import concourse.bass_utils as _bass_utils


def kernel(**inputs):
    inputs = {k: np.asarray(v) for k, v in inputs.items()}
    sh = make_shared(inputs)
    in_maps = [make_core_inputs(inputs, sh, c) for c in range(8)]
    nc, _ = build()
    res = _bass_utils.run_bass_kernel_spmd(nc, in_maps, core_ids=list(range(8)))
    out = np.zeros((4, 2048, 4096), np.float32)
    for c in range(8):
        b, half = c // 2, c % 2
        out[b, half * T:(half + 1) * T, :] = np.asarray(res.results[c]["outT"]).T
    return out


# version 10 (final)
# speedup vs baseline: 1.4503x; 1.4457x over previous
import sys
sys.path.insert(0, "/opt/trn_rl_repo")
import numpy as np
import concourse.bass as bass
import concourse.mybir as mybir
from contextlib import ExitStack

F32 = mybir.dt.float32
BF16 = mybir.dt.bfloat16
I32 = mybir.dt.int32
I8 = mybir.dt.int8
AF = mybir.ActivationFunctionType
ALU = mybir.AluOpType
AX = mybir.AxisListType

H, KVH, D, CHUNK = 32, 8, 128, 128
DBG_HN = None
DBG_OB = None
PARANOID_S4 = False
HID = 4096
FFH = 11008
T = 1024
NCH = 8
NOB = FFH // 128

ENGS = ("pe", "dv", "ac", "gp", "sy")
ENG_ATTR = {"pe": "tensor", "dv": "vector", "ac": "scalar", "gp": "gpsimd", "sy": "sync"}


class Bld:

    def __init__(self, nc):
        self.nc = nc
        self.ops = {e: [] for e in ENGS}
        self.cnt = {}
        self.waited = {}
        self.sems = {}

    def _dedup(self, eng, evs):
        out = []
        for ev in evs:
            if ev is None:
                continue
            s, v = ev
            if v <= 0:
                continue
            key = (eng, s)
            if self.waited.get(key, 0) >= v:
                continue
            self.waited[key] = v
            out.append((s, v))
        return out

    def op(self, eng, fn, waits=(), inc=None, amt=1):
        w = self._dedup(eng, list(waits))
        ev = None
        if inc is not None:
            self.cnt[inc] = self.cnt.get(inc, 0) + amt
            ev = (inc, self.cnt[inc])
        self.ops[eng].append((w, fn, inc, amt))
        return ev

    def dma(self, queue, out, in_, waits=(), sem=None, **kw):
        assert sem is not None, "stream semaphore required"
        if sem not in self.cnt:
            self.sems[sem] = self.mksem(sem)
            self.cnt[sem] = 0
        w = list(waits) + [(sem, self.cnt[sem])]
        return self.op(queue, lambda e: e.dma_start(out=out, in_=in_, **kw),
                       waits=w, inc=sem, amt=16)

    def dma_sems_now(self):
        return [self.now(s) for s in self.cnt if s.startswith("d_") or s == "sto"]

    def now(self, sem):
        return (sem, self.cnt.get(sem, 0))

    def barrier(self):
        evs = [self.now(s) for s in self.cnt]
        for e in ENGS:
            w = self._dedup(e, evs)
            if w:
                self.ops[e].append((w, None, None, 0))

    def play(self, blk):
        handles = self.sems
        for eng in ENGS:
            ops = self.ops[eng]
            if not ops:
                continue

            def body(h, ops=ops):
                for (waits, fn, inc, amt) in ops:
                    for (s, v) in waits:
                        h.wait_ge(handles[s], v)
                    if fn is None:
                        continue
                    inst = fn(h)
                    if inc is not None:
                        inst.then_inc(handles[inc], amt)
            getattr(blk, ENG_ATTR[eng])(body)


def ap2(t, part0, nparts, free_ap, off_elems=0):
    a = t.ap()
    return bass.AP(a.tensor, a.offset + part0 * a.ap[0][0] + off_elems,
                   [[a.ap[0][0], nparts]] + [list(x) for x in free_ap])


def apview(ap, part0, nparts, off_elems, free_ap):
    return bass.AP(ap.tensor, ap.offset + part0 * ap.ap[0][0] + off_elems,
                   [[ap.ap[0][0], nparts]] + [list(x) for x in free_ap])


def build(debug_names=(), stages=99):
    nc = bass.Bass("TRN2", target_bir_lowering=False)
    bld = Bld(nc)
    dbg_tensors = {}

    def din(name, shape, dt=F32):
        return nc.dram_tensor(name, shape, dt, kind="ExternalInput").ap()

    xmain_h = din("xmain", [HID, T])
    xpre_h = din("xpre", [HID, T])
    wqT_h = din("wqT", [HID, HID])
    wkT_h = din("wkT", [HID, KVH * D])
    wvT_h = din("wvT", [HID, KVH * D])
    wgT_h = din("wgT", [HID, HID])
    woT_h = din("woT", [HID, HID])
    winT_h = din("winT", [HID, H])
    dtb_h = din("dtb", [H, 1])
    alog_h = din("alog", [H, 1])
    gnorm_h = din("gnorm", [D, 1])
    cosm_h = din("cosm", [D, T])
    sinm_h = din("sinm", [64, T])
    cosp_h = din("cosp", [D, T])
    sinp_h = din("sinp", [64, T])
    maskneg_h = din("maskneg", [CHUNK, CHUNK])
    ident_h = din("ident", [128, 128])
    sel2_h = din("sel2", [2, 128])
    w1d_h = din("w1d", [NOB * 128, 4096], I8)
    w3d_h = din("w3d", [NOB * 128, 4096], I8)
    w2d_h = din("w2d", [32 * 128, FFH], I8)
    s1re_h = din("s1re", [NOB * 2, 4096])
    s3re_h = din("s3re", [NOB * 2, 4096])
    s2re_h = din("s2re", [32 * 2, FFH])

    outT_h = nc.dram_tensor("outT", [HID, T], F32, kind="ExternalOutput").ap()

    y_s = nc.dram_tensor("y_s", [H * D, T], BF16, kind="Internal").ap()
    h_s = nc.dram_tensor("h_s", [NOB * 128, T], BF16, kind="Internal").ap()
    xm_s = nc.dram_tensor("xm_s", [HID, T], BF16, kind="Internal").ap()
    r2_s = nc.dram_tensor("r2_s", [HID, T], BF16, kind="Internal").ap()

    def dbg(name, src_ap, shape, dt=F32):
        if name not in debug_names:
            return
        dt = src_ap.tensor.dtype
        hh = nc.dram_tensor("dbg_" + name, shape, dt, kind="ExternalOutput").ap()
        dbg_tensors[name] = "dbg_" + name
        bld.dma("sy", hh, src_ap, sem="d_dbg",
                waits=[bld.now("dv"), bld.now("ac"), bld.now("pe"), bld.now("gp")]
                + bld.dma_sems_now())

    def dbg_dram(name, src_ap, shape, dt=F32):
        if name not in debug_names:
            return
        hh = nc.dram_tensor("dbg_" + name, shape, dt, kind="ExternalOutput").ap()
        dbg_tensors[name] = "dbg_" + name
        bld.dma("sy", hh, src_ap, sem="d_dbg",
                waits=[bld.now("dv"), bld.now("ac"), bld.now("pe"), bld.now("gp")]
                + bld.dma_sems_now())

    es = ExitStack()

    def sb(name, shape, dt=F32, side=None):
        return es.enter_context(nc.sbuf_tensor(name, shape, dt, side=side))

    psf = [es.enter_context(nc.psum_tensor(f"psf{i}", [128, 512], F32)) for i in range(7)]
    psb = es.enter_context(nc.psum_tensor("psb", [128, 1024], BF16))

    bld.mksem = lambda name: es.enter_context(nc.semaphore(name))
    for s in ("pe", "dv", "ac", "gp"):
        bld.sems[s] = es.enter_context(nc.semaphore(s))
        bld.cnt[s] = 0

    consts = sb("consts", [128, 8], F32)
    C_ONES, C_EPS, C_DTB, C_A, C_GNORM, C_OFF = 0, 1, 2, 3, 4, 5
    NCOL = 8

    def ccol(c, nparts=128, p0=0):
        return ap2(consts, p0, nparts, [[NCOL, 1]], c)

    onesrow = sb("onesrow", [1, 128], F32)
    sel2 = sb("sel2_sb", [2, 128], F32)
    ident = sb("ident_sb", [128, 128], F32)
    identb = sb("identb_sb", [128, 128], BF16)
    maskneg = sb("maskneg_sb", [128, 128], F32)
    onescol_bf = sb("onescol_bf", [128, 1], BF16)
    sel2b = sb("sel2b_sb", [2, 128], BF16)
    maski = sb("maski", [128, 1], I32)
    xori = sb("xori", [128, 1], I32)

    bld.dma("sy", ccol(C_DTB, H), dtb_h, allow_slow_non_contiguous=True, sem="d_c")
    bld.dma("sy", ccol(C_A, H), alog_h, allow_slow_non_contiguous=True, sem="d_c")
    bld.dma("sy", ccol(C_GNORM, D), gnorm_h, allow_slow_non_contiguous=True, sem="d_c")
    bld.dma("sy", maskneg.ap(), maskneg_h, sem="d_c")
    bld.dma("sy", ident.ap(), ident_h, sem="d_c")
    bld.dma("sy", sel2.ap(), sel2_h, sem="d_c")
    bld.op("gp", lambda e: e.memset(ccol(C_ONES), 1.0))
    bld.op("gp", lambda e: e.memset(ccol(C_EPS), 1e-5))
    bld.op("gp", lambda e: e.memset(ccol(C_OFF, 64, 0), 0.0))
    bld.op("gp", lambda e: e.memset(ccol(C_OFF, 64, 64), 8.0))
    bld.op("gp", lambda e: e.memset(onesrow.ap(), 1.0))
    bld.op("gp", lambda e: e.memset(onescol_bf.ap(), 1.0))
    bld.op("gp", lambda e: e.memset(maski[0:64, :], -16))
    bld.op("gp", lambda e: e.memset(maski[64:128, :], 15))
    bld.op("gp", lambda e: e.memset(xori[0:64, :], 0))
    bld.op("gp", lambda e: e.memset(xori[64:128, :], 8), inc="gp")
    bld.barrier()
    bld.op("ac", lambda e: e.activation(ccol(C_A, H), ccol(C_A, H), AF.Exp))
    bld.op("ac", lambda e: e.mul(ccol(C_A, H), ccol(C_A, H), -1.0), inc="ac")
    bld.op("dv", lambda e: e.tensor_copy(identb[:], ident[:]))
    bld.op("dv", lambda e: e.tensor_copy(sel2b[:], sel2[:]), inc="dv")

    ones_col = ccol(C_ONES)
    eps_col = ccol(C_EPS)
    dtb_col = ccol(C_DTB, H)
    A_col = ccol(C_A, H)
    gnorm_col = ccol(C_GNORM, D)
    off_col = ccol(C_OFF)

    scope_m = ExitStack()

    def smb(name, shape, dt=F32):
        return scope_m.enter_context(nc.sbuf_tensor(name, shape, dt))

    kT = smb("kT", [128, KVH * T], BF16)
    ktok = smb("ktok", [128, NCH * KVH * D], BF16)
    vtok = smb("vtok", [128, NCH * KVH * D], BF16)
    G_all = smb("G_all", [H, 2 * T], F32)
    Gcol = smb("Gcol", [128, 16 * H], F32)
    dt_tok = smb("dt_tok", [128, 16 * H], F32)
    states = smb("states", [128, H * D], F32)
    cosm = smb("cosm_sb", [128, T], F32)
    sinm = smb("sinm_sb", [64, T], F32)
    grow = smb("grow", [1, T], F32)
    gl_cols = smb("gl_cols", [128, 2 * NCH], F32)
    xcb = smb("xcb", [128, 128], F32)
    kdb = smb("kdb", [128, 128], F32)

    pre_scope = ExitStack()
    ktok_p = pre_scope.enter_context(nc.sbuf_tensor("ktok_p", [128, NCH * KVH * D], BF16))
    vtok_p = pre_scope.enter_context(nc.sbuf_tensor("vtok_p", [128, NCH * KVH * D], BF16))
    cosp = pre_scope.enter_context(nc.sbuf_tensor("cosp_sb", [128, T], F32))
    sinp = pre_scope.enter_context(nc.sbuf_tensor("sinp_sb", [64, T], F32))

    xT = None

    def xT_sl(kc, t0, nt):
        return ap2(xT, 0, 128, [[1, nt]], kc * T + t0)

    bld.dma("sy", cosm.ap(), cosm_h, sem="d_c")
    bld.dma("sy", sinm.ap(), sinm_h, sem="d_c")
    bld.dma("sy", cosp.ap(), cosp_h, sem="d_c")
    bld.dma("sy", sinp.ap(), sinp_h, sem="d_c")
    bld.op("gp", lambda e: e.memset(states.ap(), 0.0), inc="gp")
    bld.barrier()

    def emit_wcol_load(dst_t, w_h, mb, ncols=128, waits=(), par=0):
        dst = ap2(dst_t, 0, 128, [[ncols, 32], [1, ncols]], 0)
        src = w_h.rearrange("(k p) m -> p k m", p=128)[:, :, mb * ncols:(mb + 1) * ncols]
        return bld.dma("gp", dst, src, waits=list(waits), sem=f"d_w{par}")

    def emit_proj(w_h, nmb, xsl_fn, tlen, bank_fn, evac_fn, wbuf, ncols=128,
                  pre_waits=()):
        ev_mm = {}
        ev_free = {}
        for mb in range(nmb):
            w = [ev_mm[mb - 2]] if mb >= 2 else list(pre_waits)
            evw = emit_wcol_load(wbuf[mb % 2], w_h, mb, ncols, waits=w, par=mb % 2)
            bank = bank_fn(mb)
            for kc in range(32):
                ev_mm[mb] = bld.op("pe", lambda e, mb=mb, kc=kc, bank=bank: e.matmul(
                    bank,
                    ap2(wbuf[mb % 2], 0, 128, [[1, ncols]], kc * ncols),
                    xsl_fn(kc, 0, tlen),
                    start=(kc == 0), stop=(kc == 31)),
                    waits=([evw] + ([ev_free[mb - 2]] if mb >= 2 else []))
                    if kc == 0 else [], inc="pe" if kc == 31 else None)
            evac_fn(mb, bank, ev_mm[mb])
            ev_free[mb] = bld.now("dv")

    def emit_s1_sb(sbi):
        pre = sbi < 2
        t0 = (sbi % 2) * 512
        x_h_src = xpre_h if pre else xmain_h
        cosT, sinT = (cosp, sinp) if pre else (cosm, sinm)
        ktok_dst, vtok_dst = (ktok_p, vtok_p) if pre else (ktok, vtok)
        goff = 0 if pre else T
        nn0 = (0 if pre else 8) + (sbi % 2) * 4
        nl0 = (sbi % 2) * 4

        scope = ExitStack()

        def lsb(name, shape, dt=F32):
            return scope.enter_context(nc.sbuf_tensor(f"{name}_{sbi}", shape, dt))

        if pre:
            xpre_bf = lsb("xpre_bf", [128, 32 * 512], BF16)

            def xsl(kc, tt0, nt):
                return ap2(xpre_bf, 0, 128, [[1, nt]], kc * 512 + tt0)
            dstx = ap2(xpre_bf, 0, 128, [[512, 32], [1, 512]], 0)
        else:
            def xsl(kc, tt0, nt):
                return xT_sl(kc, t0 + tt0, nt)
            dstx = ap2(xT, 0, 128, [[T, 32], [1, 512]], t0)
        srcx = x_h_src.rearrange("(k p) t -> p k t", p=128)[:, :, t0:t0 + 512]
        evx = bld.dma("gp", dstx, srcx, waits=[bld.now("pe"), bld.now("dv")], sem="d_x")

        kT_sb = lsb("kT_sb", [128, KVH * 512], BF16) if pre else None
        vT_sb = lsb("vT_sb", [128, 512], BF16)
        wbuf = [lsb("wbuf0", [128, 32 * 128], BF16), lsb("wbuf1", [128, 32 * 128], BF16)]
        ropeA = lsb("ropeA", [128, 512], F32)
        ropeB = lsb("ropeB", [128, 512], F32)

        def krope_dst(mb, tt0, nt, parts=(0, 128)):
            if pre:
                return apview(kT_sb.ap(), parts[0], parts[1], mb * 512 + tt0, [[1, nt]])
            return apview(kT.ap(), parts[0], parts[1], mb * T + t0 + tt0, [[1, nt]])

        cssl = cosT[:, t0:t0 + 512]
        snsl = sinT[:, t0:t0 + 512]

        def k_evac(mb, bank, ev_mm):
            bld.op("dv", lambda e, bank=bank: e.tensor_mul(ropeA[0:64, :], bank[0:64, :], cssl[0:64, :]),
                   waits=[ev_mm])
            bld.op("dv", lambda e, bank=bank: e.tensor_mul(ropeA[64:128, :], bank[64:128, :], cssl[64:128, :]))
            bld.op("dv", lambda e, bank=bank: e.tensor_mul(ropeB[0:64, :], bank[64:128, :], snsl[:]))
            bld.op("dv", lambda e, bank=bank: e.tensor_mul(ropeB[64:128, :], bank[0:64, :], snsl[:]))
            bld.op("dv", lambda e, mb=mb: e.tensor_sub(krope_dst(mb, 0, 512, (0, 64)),
                                                       ropeA[0:64, :], ropeB[0:64, :]))
            bld.op("dv", lambda e, mb=mb: e.tensor_add(krope_dst(mb, 0, 512, (64, 64)),
                                                       ropeA[64:128, :], ropeB[64:128, :]),
                   inc="dv")
            ev_rope = bld.now("dv")
            for n4 in range(4):
                nl = nl0 + n4
                srck = krope_dst(mb, n4 * 128, 128)
                half = (n4 % 2) * 512
                evt = bld.op("pe", lambda e, srck=srck, half=half: e.transpose(
                    apview(psb.ap(), 0, 128, half, [[1, 128]]), srck, identb[:]),
                    waits=[ev_rope, bld.now("dv")], inc="pe")
                dstk = ap2(ktok_dst, 0, 128, [[1, 128]], (nl * KVH + mb) * D)
                bld.op("dv", lambda e, dstk=dstk, half=half: e.tensor_copy(
                    dstk, apview(psb.ap(), 0, 128, half, [[1, 128]])),
                    waits=[evt], inc="dv")

        emit_proj(wkT_h, KVH, xsl, 512, lambda mb: psf[mb % 2][:],
                  k_evac, wbuf, pre_waits=[evx])

        def v_evac(mb, bank, ev_mm):
            bld.op("dv", lambda e, bank=bank: e.tensor_copy(vT_sb[:], bank), waits=[ev_mm], inc="dv")
            ev_c = bld.now("dv")
            for n4 in range(4):
                nl = nl0 + n4
                half = (n4 % 2) * 512
                evt = bld.op("pe", lambda e, n4=n4, half=half: e.transpose(
                    apview(psb.ap(), 0, 128, half, [[1, 128]]),
                    vT_sb[:, n4 * 128:(n4 + 1) * 128], identb[:]),
                    waits=[ev_c, bld.now("dv")], inc="pe")
                dstv = ap2(vtok_dst, 0, 128, [[1, 128]], (nl * KVH + mb) * D)
                bld.op("dv", lambda e, dstv=dstv, half=half: e.tensor_copy(
                    dstv, apview(psb.ap(), 0, 128, half, [[1, 128]])),
                    waits=[evt], inc="dv")

        emit_proj(wvT_h, KVH, xsl, 512, lambda mb: psf[2 + mb % 2][:],
                  v_evac, wbuf, pre_waits=[bld.now("pe")])

        dt_sl = lsb("dt_sl", [H, 512], F32)
        sp1 = lsb("sp1", [H, 512], F32)
        sp2 = lsb("sp2", [H, 512], F32)
        gsl = lsb("gsl", [H, 512], F32)
        gon = lsb("gon", [H, 128], F32)
        dts = dt_sl[:, :]

        def dt_evac(mb, bank, ev_mm):
            bld.op("dv", lambda e, bank=bank: e.tensor_scalar(sp1[:], bank, dtb_col, None, ALU.add),
                   waits=[ev_mm])
            bld.op("dv", lambda e: e.tensor_relu(sp2[:], sp1[:]), inc="dv")
            bld.op("ac", lambda e: e.activation(sp1[:], sp1[:], AF.Abs), waits=[bld.now("dv")])
            bld.op("ac", lambda e: e.activation(sp1[:], sp1[:], AF.Exp, scale=-1.0))
            bld.op("ac", lambda e: e.activation(sp1[:], sp1[:], AF.Ln, bias=1.0), inc="ac")
            bld.op("dv", lambda e: e.tensor_add(dts, sp1[:], sp2[:]),
                   waits=[bld.now("ac")], inc="dv")

        emit_proj(winT_h, 1, xsl, 512, lambda mb: psf[4][0:H, :],
                  dt_evac, wbuf, ncols=H, pre_waits=[bld.now("pe")])

        ev_sp = bld.now("dv")
        bld.op("gp", lambda e: e.memset(gon[:], 1.0), inc="gp")
        bld.op("dv", lambda e: e.tensor_scalar(gsl[:], dts, A_col, None, ALU.mult),
               waits=[ev_sp, bld.now("gp"), bld.now("ac")])
        Gs = ap2(G_all, 0, H, [[1, 512]], goff + t0)
        for n4 in range(4):
            bld.op("dv", lambda e, n4=n4, Gs=Gs: e.tensor_tensor_scan(
                apview(Gs, 0, H, n4 * 128, [[1, 128]]),
                gon[:], gsl[:, n4 * 128:(n4 + 1) * 128], 0.0, ALU.mult, ALU.add),
                inc="dv" if n4 == 3 else None)
        ev_G = bld.now("dv")

        for n4 in range(4):
            nn = nn0 + n4
            srcd = dt_sl[:, n4 * 128:(n4 + 1) * 128]
            srcg = ap2(G_all, 0, H, [[1, 128]], goff + t0 + n4 * 128)
            ev1 = bld.op("pe", lambda e, srcd=srcd: e.transpose(psf[1][:, 0:H], srcd,
                                                                ident[0:H, 0:H]),
                         waits=[ev_G, bld.now("ac")], inc="pe")
            bld.op("ac", lambda e, nn=nn: e.copy(ap2(dt_tok, 0, 128, [[1, H]], nn * H),
                                                 psf[1][:, 0:H]),
                   waits=[ev1], inc="ac")
            ev2 = bld.op("pe", lambda e, srcg=srcg: e.transpose(psf[2][:, 0:H], srcg,
                                                                ident[0:H, 0:H]),
                         waits=[ev_G, bld.now("ac")], inc="pe")
            bld.op("ac", lambda e, nn=nn: e.copy(ap2(Gcol, 0, 128, [[1, H]], nn * H),
                                                 psf[2][:, 0:H]),
                   waits=[ev2], inc="ac")

        bld.barrier()
        scope.close()

    emit_s1_sb(0)
    emit_s1_sb(1)

    dbg("vtok_p", vtok_p.ap(), [128, NCH * KVH * D])
    dbg("ktok_p", ktok_p.ap(), [128, NCH * KVH * D])

    def emit_head_cols(h, pre):
        goff = 0 if pre else T
        nnb = 0 if pre else 8
        evr = bld.dma("sy", grow.ap(), ap2(G_all, h, 1, [[1, T]], goff),
                      waits=[bld.now("pe"), bld.now("dv"), bld.now("ac")], sem="d_gr")
        gl_row = apview(grow.ap(), 0, 1, 127, [[128, NCH]])
        ev1 = bld.op("pe", lambda e, gl_row=gl_row: e.matmul(
            psf[3][:, 0:NCH], onesrow[:], gl_row, start=True, stop=True),
            waits=[evr, bld.now("dv")], inc="pe")
        gc_sl = ap2(Gcol, 0, 128, [[H, NCH]], nnb * H + h)
        bld.op("dv", lambda e, gc_sl=gc_sl: e.tensor_sub(gl_cols[:, 0:NCH], psf[3][:, 0:NCH], gc_sl),
               waits=[ev1], inc="dv")
        bld.op("ac", lambda e: e.activation(gl_cols[:, 0:NCH], gl_cols[:, 0:NCH], AF.Exp),
               waits=[bld.now("dv")])
        bld.op("ac", lambda e: e.activation(gl_cols[:, NCH:2 * NCH], psf[3][:, 0:NCH], AF.Exp),
               inc="ac")
        return bld.now("ac")

    for h in range(H):
        ev_cols = emit_head_cols(h, pre=True)
        kv = h // 4
        for n in range(NCH):
            bld.op("dv", lambda e, n=n, kv=kv, h=h: e.tensor_scalar(
                xcb[:], ap2(vtok_p, 0, 128, [[1, 128]], (n * KVH + kv) * D),
                ap2(dt_tok, 0, 128, [[H, 1]], n * H + h), None, ALU.mult),
                waits=[ev_cols])
            bld.op("dv", lambda e, n=n, kv=kv: e.tensor_scalar(
                kdb[:], ap2(ktok_p, 0, 128, [[1, 128]], (n * KVH + kv) * D),
                gl_cols[:, n:n + 1], None, ALU.mult), inc="dv")
            evk = bld.now("dv")
            evm = bld.op("pe", lambda e: e.matmul(psf[4][:, 0:128], kdb[:], xcb[:],
                                                  start=True, stop=True),
                         waits=[evk], inc="pe")
            bld.op("dv", lambda e, h=h, n=n: e.scalar_tensor_tensor(
                ap2(states, 0, 128, [[1, 128]], h * D),
                ap2(states, 0, 128, [[1, 128]], h * D),
                gl_cols[:, NCH + n:NCH + n + 1],
                psf[4][:, 0:128], ALU.mult, ALU.add),
                waits=[evm], inc="dv")
    bld.barrier()
    dbg("states", states.ap(), [128, H * D])
    pre_scope.close()

    xT = nc.alloc_sbuf_tensor("xT", [128, 32 * T], BF16, side="right")
    emit_s1_sb(2)
    emit_s1_sb(3)
    dbg("kT", kT.ap(), [128, KVH * T])
    dbg("vtok", vtok.ap(), [128, NCH * KVH * D])
    dbg("ktok", ktok.ap(), [128, NCH * KVH * D])
    dbg("G_all", G_all.ap(), [H, 2 * T])
    dbg("Gcol", Gcol.ap(), [128, 16 * H])
    dbg("dt_tok", dt_tok.ap(), [128, 16 * H])

    if stages < 3:
        return finish(nc, bld, es, dbg_tensors)

    s2 = ExitStack()

    def s2b(name, shape, dt=F32):
        return s2.enter_context(nc.sbuf_tensor(name, shape, dt))

    qT2 = [s2b("qT_a", [128, T], BF16), s2b("qT_b", [128, T], BF16)]
    gs2 = [s2b("gs_a", [128, T], BF16), s2b("gs_b", [128, T], BF16)]
    Y_h = s2b("Y_h", [128, T], BF16)
    wqbuf = [s2b("wqbuf0", [128, 32 * 128], BF16), s2b("wqbuf1", [128, 32 * 128], BF16)]
    ropeA2 = s2b("ropeA2", [128, 512], F32)
    ropeB2 = s2b("ropeB2", [128, 512], F32)
    dfT = s2b("dfT", [128, 128], F32)
    dbgyA = s2b("dbgyA", [128, 128], F32)
    dbgyB = s2b("dbgyB", [128, 128], F32)
    egb = s2b("egb", [128, 128], F32)
    STDb = s2b("STDb", [128, 128], F32)
    qscb = s2b("qscb", [128, 128], F32)
    prb = s2b("prb", [128, 128], F32)
    sqb = s2b("sqb", [128, 128], F32)
    t5b = s2b("t5b", [128, 128], F32)
    rmsr = s2b("rmsr", [1, 128], F32)
    rstdr = s2b("rstdr", [1, 128], F32)
    y_all = s2b("y_all", [128, T], F32)
    sq_all = s2b("sq_all", [128, T], BF16)
    rmsr2 = s2b("rmsr2", [1, T], F32)
    rstdr2 = s2b("rstdr2", [1, T], F32)
    t5w = s2b("t5w", [128, 512], F32)

    ev_w_prev = {}
    ev_free = {"q": None, "g": None}

    def proj_start(h):
        st = {"h": h, "mm": [], "evs": {}, "ev_q": None}
        hp = h % 2

        def emit_rope():
            for tb in range(2):
                bank = psf[5 + tb]
                cssl = cosm[:, tb * 512:(tb + 1) * 512]
                snsl = sinm[:, tb * 512:(tb + 1) * 512]
                bld.op("dv", lambda e, bank=bank, cssl=cssl: e.tensor_mul(
                    ropeA2[0:64, :], bank[0:64, :], cssl[0:64, :]),
                    waits=[st["evs"][("q", tb)]])
                bld.op("dv", lambda e, bank=bank, cssl=cssl: e.tensor_mul(
                    ropeA2[64:128, :], bank[64:128, :], cssl[64:128, :]))
                bld.op("dv", lambda e, bank=bank, snsl=snsl: e.tensor_mul(
                    ropeB2[0:64, :], bank[64:128, :], snsl[:]))
                bld.op("dv", lambda e, bank=bank, snsl=snsl: e.tensor_mul(
                    ropeB2[64:128, :], bank[0:64, :], snsl[:]))
                bld.op("dv", lambda e, tb=tb: e.tensor_sub(
                    apview(qT2[hp].ap(), 0, 64, tb * 512, [[1, 512]]),
                    ropeA2[0:64, :], ropeB2[0:64, :]))
                bld.op("dv", lambda e, tb=tb: e.tensor_add(
                    apview(qT2[hp].ap(), 64, 64, tb * 512, [[1, 512]]),
                    ropeA2[64:128, :], ropeB2[64:128, :]), inc="dv")
            st["ev_q"] = bld.now("dv")
            ev_free["q"] = st["ev_q"]

        for wi, w_h in (("q", wqT_h), ("g", wgT_h)):
            par = proj_start.parity
            proj_start.parity ^= 1
            w = [ev_w_prev.get(par)]
            evw = emit_wcol_load(wqbuf[par], w_h, h, 128, waits=w, par=par)
            for tb in range(2):
                bank = psf[5 + tb]
                for kc in range(32):
                    def mk(wi=wi, par=par, tb=tb, kc=kc, bank=bank, evw=evw, st=st):
                        if wi == "q":
                            w0 = [evw, ev_free["q"]] if kc == 0 else []
                        else:
                            w0 = [evw, ev_free["g"], st["ev_q"]] if kc == 0 else []
                        ev = bld.op("pe", lambda e: e.matmul(
                            bank[:], ap2(wqbuf[par], 0, 128, [[1, 128]], kc * 128),
                            xT_sl(kc, tb * 512, 512),
                            start=(kc == 0), stop=(kc == 31)),
                            waits=w0, inc="pe" if kc == 31 else None)
                        if ev is not None:
                            st["evs"][(wi, tb)] = ev
                            if wi == "g" and tb == 1:
                                ev_w_prev[par] = ev
                                ev_w_prev[1 - par] = ev
                    st["mm"].append(mk)
            if wi == "q":
                st["mm"].append(emit_rope)
        return st
    proj_start.parity = 0

    def proj_drain(st, k):
        for _ in range(k):
            if st["mm"]:
                st["mm"].pop(0)()

    def proj_finish(st):
        h = st["h"]
        hp = h % 2
        proj_drain(st, 999)
        ev_q = st["ev_q"]
        for tb in range(2):
            bank = psf[5 + tb]
            bld.op("ac", lambda e, bank=bank, tb=tb, hp=hp: e.activation(
                gs2[hp][:, tb * 512:(tb + 1) * 512], bank[:], AF.Silu),
                waits=[st["evs"][("g", tb)]], inc="ac")
        ev_g = bld.now("ac")
        ev_free["g"] = ev_g
        return ev_q, ev_g

    pc = {"STD": None, "egb": None, "qsc": None, "sq": None, "rms": None, "Y": None,
          "rstd": None, "epi": None, "epi_rms": None, ("yev", 0): None, ("yev", 1): None}

    st0 = proj_start(0)
    evqg = {0: proj_finish(st0)}
    ev_cols_h = {0: emit_head_cols(0, pre=False)}

    for h in range(H):
        kv = h // 4
        hp = h % 2
        ev_cols = ev_cols_h[h]
        ev_q, ev_g = evqg[h]
        qT_h = qT2[hp]
        gsil_h = gs2[hp]
        stn = proj_start(h + 1) if h + 1 < H else None
        for n in range(NCH):
            col = (8 + n) * H + h
            ev_st = bld.op("pe", lambda e, kv=kv, n=n, qT_h=qT_h: e.matmul(
                psf[0][:, 0:128],
                ap2(kT, 0, 128, [[1, 128]], kv * T + n * 128),
                ap2(qT_h, 0, 128, [[1, 128]], n * 128),
                start=True, stop=True),
                waits=[ev_q, ev_g, pc["STD"]], inc="pe")
            ev_gi = bld.op("pe", lambda e, n=n: e.matmul(
                psf[1][:, 0:128], onesrow[:],
                apview(grow.ap(), 0, 1, n * 128, [[1, 128]]),
                start=True, stop=True),
                waits=[ev_cols, pc["STD"], pc["egb"]], inc="pe")
            ev3 = bld.op("dv", lambda e, col=col: e.scalar_tensor_tensor(
                dfT[:], psf[1][:, 0:128], ap2(Gcol, 0, 128, [[H, 1]], col),
                maskneg[:], ALU.subtract, ALU.add),
                waits=[ev_gi], inc="dv")
            ev4 = bld.op("ac", lambda e: e.activation(dfT[:], dfT[:], AF.Exp),
                         waits=[ev3], inc="ac")
            ev5 = bld.op("ac", lambda e: e.activation(egb[:], psf[1][:, 0:128], AF.Exp),
                         waits=[ev_gi, pc["qsc"]], inc="ac")
            ev6 = bld.op("dv", lambda e: e.tensor_mul(STDb[:], dfT[:], psf[0][:, 0:128]),
                         waits=[ev4, ev_st], inc="dv")
            pc["STD"] = ev6
            ev7 = bld.op("dv", lambda e, n=n, qT_h=qT_h: e.tensor_mul(
                qscb[:], ap2(qT_h, 0, 128, [[1, 128]], n * 128), egb[:]),
                waits=[ev5], inc="dv")
            pc["qsc"] = ev7
            pc["egb"] = ev5
            bld.op("dv", lambda e, n=n, kv=kv, col=col: e.tensor_scalar(
                xcb[:], ap2(vtok, 0, 128, [[1, 128]], (n * KVH + kv) * D),
                ap2(dt_tok, 0, 128, [[H, 1]], col), None, ALU.mult))
            bld.op("dv", lambda e, n=n, kv=kv: e.tensor_scalar(
                kdb[:], ap2(ktok, 0, 128, [[1, 128]], (n * KVH + kv) * D),
                gl_cols[:, n:n + 1], None, ALU.mult))
            ev10 = bld.op("dv", lambda e, h=h: e.tensor_copy(
                prb[:], ap2(states, 0, 128, [[1, 128]], h * D)), inc="dv")
            ybank = psf[2 + 2 * (n % 2)]
            bld.op("pe", lambda e, ybank=ybank: e.matmul(ybank[:, 0:128], xcb[:], STDb[:],
                                            start=True, stop=False),
                   waits=[ev10, pc.get(("yev", n % 2)), ev6]
                   + ([pc.get("epi")] if n % 2 == 1 else []), inc=None)
            ev_y = bld.op("pe", lambda e, ybank=ybank: e.matmul(ybank[:, 0:128], prb[:], qscb[:],
                                                   start=False, stop=True),
                          waits=[ev7], inc="pe")
            ev_cs = bld.op("pe", lambda e: e.matmul(psf[3][:, 0:128], kdb[:], xcb[:],
                                                    start=True, stop=True),
                           waits=[ev10], inc="pe")
            bld.op("dv", lambda e, h=h, n=n: e.scalar_tensor_tensor(
                ap2(states, 0, 128, [[1, 128]], h * D),
                ap2(states, 0, 128, [[1, 128]], h * D),
                gl_cols[:, NCH + n:NCH + n + 1],
                psf[3][:, 0:128], ALU.mult, ALU.add),
                waits=[ev_cs], inc="dv")
            pc[("yev", n % 2)] = bld.op("ac", lambda e, n=n, ybank=ybank: e.copy(
                y_all[:, n * 128:(n + 1) * 128], ybank[:, 0:128]),
                waits=[ev_y, pc.get("epi")], inc="ac")
            if stn is not None:
                proj_drain(stn, 16)
        ev_sqa = bld.op("ac", lambda e: e.activation(sq_all[:], y_all[:], AF.Square),
                        waits=[pc[("yev", 0)], pc[("yev", 1)]], inc="ac")
        ev_epi_last = None
        for tb in range(2):
            ev_ssq = bld.op("pe", lambda e, tb=tb: e.matmul(
                psf[4][0:1, :], onescol_bf[:], sq_all[:, tb * 512:(tb + 1) * 512],
                start=True, stop=True),
                waits=[ev_sqa, pc.get("epi_rms"), ev_epi_last], inc="pe")
            ev_rms = bld.op("ac", lambda e, tb=tb: e.activation(
                rmsr2[0:1, tb * 512:(tb + 1) * 512], psf[4][0:1, :], AF.Sqrt,
                scale=1.0 / 128.0, bias=ccol(C_EPS, 1)),
                waits=[ev_ssq], inc="ac")
            pc["epi_rms"] = ev_rms
            ev_rcp = bld.op("dv", lambda e, tb=tb: e.reciprocal(
                rstdr2[0:1, tb * 512:(tb + 1) * 512],
                rmsr2[0:1, tb * 512:(tb + 1) * 512]), waits=[ev_rms], inc="dv")
            ev_rb = bld.op("pe", lambda e, tb=tb: e.matmul(
                psf[4][:], onesrow[:], rstdr2[0:1, tb * 512:(tb + 1) * 512],
                start=True, stop=True), waits=[ev_rcp, ev_rms], inc="pe")
            bld.op("dv", lambda e, tb=tb, gsil_h=gsil_h: e.scalar_tensor_tensor(
                t5w[:], y_all[:, tb * 512:(tb + 1) * 512], gnorm_col,
                ap2(gsil_h, 0, 128, [[1, 512]], tb * 512), ALU.mult, ALU.mult),
                waits=[ev_g, ev_rb])
            ev_epi_last = bld.op("dv", lambda e, tb=tb: e.tensor_mul(
                Y_h[:, tb * 512:(tb + 1) * 512], t5w[:], psf[4][:]),
                waits=[pc["Y"]] if tb == 0 else [], inc="dv")
        pc["epi"] = ev_epi_last
        ev_ys = bld.dma("sy", y_s[h * 128:(h + 1) * 128, :], Y_h.ap(), waits=[ev_epi_last],
                        sem="d_y")
        pc["Y"] = ev_ys
        if stn is not None:
            evqg[h + 1] = proj_finish(stn)
            ev_cols_h[h + 1] = emit_head_cols(h + 1, pre=False)
    bld.barrier()
    s2.close()
    scope_m.close()
    dbg_dram("y_s", y_s, [H * D, T], BF16)

    if stages < 4:
        return finish(nc, bld, es, dbg_tensors)

    s3 = ExitStack()

    def s3b(name, shape, dt=F32):
        return s3.enter_context(nc.sbuf_tensor(name, shape, dt))

    Y_all = s3b("Y_all", [128, H * T], BF16)
    wobuf = [s3b("wobuf0", [128, 32 * 128], BF16), s3b("wobuf1", [128, 32 * 128], BF16)]
    sqt = s3b("sqt", [128, T], BF16)
    rows = s3b("rows", [1, 4 * T], F32)
    ut = s3b("ut", [128, 512], F32)

    ev_yl = bld.dma("sy", ap2(Y_all, 0, 128, [[T, H], [1, T]], 0),
                    y_s.rearrange("(h p) t -> p h t", p=128), sem="d_y")

    def emit_ln(xsrc_sl, dma_out, out_dt_bf=True):
        pass

    ev_w_prev2 = {}
    ev_evac2 = {}
    for mc in range(32):
        par = mc % 2
        w = [ev_w_prev2[par]] if par in ev_w_prev2 else [bld.now("pe")]
        evw = emit_wcol_load(wobuf[par], woT_h, mc, 128, waits=w, par=par)
        for tb in range(2):
            bank = psf[tb]
            for hc in range(32):
                ev = bld.op("pe", lambda e, par=par, hc=hc, tb=tb, bank=bank: e.matmul(
                    bank[:], ap2(wobuf[par], 0, 128, [[1, 128]], hc * 128),
                    ap2(Y_all, 0, 128, [[1, 512]], hc * T + tb * 512),
                    start=(hc == 0), stop=(hc == 31)),
                    waits=([evw, ev_yl] + ([ev_evac2.get(mc - 2)] if mc >= 2 else []))
                    if hc == 0 else [],
                    inc="pe" if hc == 31 else None)
            ev_w_prev2[par] = ev
            xsl = ap2(xT, 0, 128, [[1, 512]], mc * T + tb * 512)
            bld.op("dv", lambda e, xsl=xsl, bank=bank: e.tensor_add(xsl, xsl, bank[:]),
                   waits=[ev], inc="dv")
        ev_evac2[mc] = bld.now("dv")
    bld.barrier()

    for mc in range(32):
        xsl = ap2(xT, 0, 128, [[1, T]], mc * T)
        ev_s = bld.op("ac", lambda e, xsl=xsl: e.activation(sqt[:], xsl, AF.Square),
                      waits=[bld.now("pe")], inc="ac")
        for tb in range(2):
            bld.op("pe", lambda e, mc=mc, tb=tb: e.matmul(
                psf[2 + tb][0:1, :], onescol_bf[:],
                ap2(xT, 0, 128, [[1, 512]], mc * T + tb * 512),
                start=(mc == 0), stop=(mc == 31)),
                waits=[], inc="pe" if mc == 31 else None)
            bld.op("pe", lambda e, mc=mc, tb=tb: e.matmul(
                psf[4 + tb][0:1, :], onescol_bf[:], sqt[:, tb * 512:(tb + 1) * 512],
                start=(mc == 0), stop=(mc == 31)),
                waits=[ev_s], inc="pe" if mc == 31 else None)
    ev_stats = bld.now("pe")

    def emit_ln_rows(ev_stats):
        for tb in range(2):
            bld.op("dv", lambda e, tb=tb: e.tensor_scalar(
                rows[0:1, tb * 512:(tb + 1) * 512], psf[2 + tb][0:1, :],
                1.0 / HID, None, ALU.mult), waits=[ev_stats])
            bld.op("dv", lambda e, tb=tb: e.tensor_scalar(
                rows[0:1, T + tb * 512:T + (tb + 1) * 512], psf[4 + tb][0:1, :],
                1.0 / HID, None, ALU.mult), inc="dv")
        ev1 = bld.now("dv")
        bld.op("ac", lambda e: e.activation(rows[0:1, 2 * T:3 * T], rows[0:1, 0:T],
                                            AF.Square), waits=[ev1], inc="ac")
        ev2 = bld.now("ac")
        bld.op("dv", lambda e: e.tensor_sub(rows[0:1, T:2 * T], rows[0:1, T:2 * T],
                                            rows[0:1, 2 * T:3 * T]), waits=[ev2], inc="dv")
        ev3 = bld.now("dv")
        bld.op("ac", lambda e: e.activation(rows[0:1, 2 * T:3 * T], rows[0:1, T:2 * T],
                                            AF.Sqrt, bias=ccol(C_EPS, 1)),
               waits=[ev3], inc="ac")
        ev4 = bld.now("ac")
        bld.op("dv", lambda e: e.reciprocal(rows[0:1, 3 * T:4 * T], rows[0:1, 2 * T:3 * T]),
               waits=[ev4], inc="dv")
        return bld.now("dv")

    ev_rows = emit_ln_rows(ev_stats)

    ev_bc = []
    for tb in range(2):
        bld.op("pe", lambda e, tb=tb: e.matmul(
            psf[2 + tb][:], onesrow[:], rows[0:1, tb * 512:(tb + 1) * 512],
            start=True, stop=True), waits=[ev_rows])
        ev_bc.append(bld.op("pe", lambda e, tb=tb: e.matmul(
            psf[4 + tb][:], onesrow[:], rows[0:1, 3 * T + tb * 512:3 * T + (tb + 1) * 512],
            start=True, stop=True), inc="pe"))

    for mc in range(32):
        for tb in range(2):
            xsl = ap2(xT, 0, 128, [[1, 512]], mc * T + tb * 512)
            bld.op("dv", lambda e, xsl=xsl, tb=tb: e.tensor_sub(ut[:], xsl, psf[2 + tb][:]),
                   waits=[ev_bc[tb]])
            bld.op("dv", lambda e, xsl=xsl, tb=tb: e.tensor_mul(xsl, ut[:], psf[4 + tb][:]),
                   inc="dv")
        bld.dma("sy", xm_s[mc * 128:(mc + 1) * 128, :],
                ap2(xT, 0, 128, [[1, T]], mc * T), waits=[bld.now("dv")], sem="d_xm")
    bld.barrier()
    s3.close()
    dbg_dram("xm_s", xm_s, [HID, T], BF16)

    if stages < 5:
        return finish(nc, bld, es, dbg_tensors)

    s4 = ExitStack()

    def s4b(name, shape, dt=F32):
        return s4.enter_context(nc.sbuf_tensor(name, shape, dt))

    pk1b = [s4b("pk1b0", [128, 4096], I8), s4b("pk1b1", [128, 4096], I8)]
    pk3b = [s4b("pk3b0", [128, 4096], I8), s4b("pk3b1", [128, 4096], I8)]
    sc1b = [s4b("sc1b0", [2, 4096], BF16), s4b("sc1b1", [2, 4096], BF16)]
    sc3b = [s4b("sc3b0", [2, 4096], BF16), s4b("sc3b1", [2, 4096], BF16)]
    W1c = [s4b("W1c0", [128, 4096], BF16), s4b("W1c1", [128, 4096], BF16)]
    W3c = [s4b("W3c0", [128, 4096], BF16), s4b("W3c1", [128, 4096], BF16)]
    asb = s4b("asb", [128, 512], BF16)
    hst = s4b("hst", [128, T], BF16)
    fence4 = s4b("fence4", [1, 4], F32)
    scsb = s4b("scsb", [128, 8192], BF16)

    ev_mm_ob = {}
    ev_stt_ob = {}
    ev_hdma = {}
    ev_ext_prev = {None: None}

    def dequant_start(ob):
        par = ob % 2
        wfree = ev_mm_ob.get(ob - 2)
        st = {"q": [], "stt": {}, "evac": {}, "si": 0}
        for wi, pk, sc, Wc, pk_h2, sc_h2 in (("w1", pk1b, sc1b, W1c, w1d_h, s1re_h),
                                             ("w3", pk3b, sc3b, W3c, w3d_h, s3re_h)):
            evp = bld.dma("sy", pk[par].ap(), pk_h2[ob * 128:(ob + 1) * 128, :],
                          waits=[wfree], sem=f"d_s4_{par}")
            evs = bld.dma("gp", sc[par].ap(), sc_h2[ob * 2:(ob + 1) * 2, :],
                          waits=[bld.now("pe")], sem=f"d_s4s_{par}")
            eve = bld.op("dv", lambda e, par=par, pk=pk: e.tensor_scalar(
                pk[par][:], pk[par][:], maski[:], xori[:],
                ALU.bitwise_and, ALU.bitwise_xor), waits=[evp], inc="dv")
            half = 0 if wi == "w1" else 1
            for sl in range(8):
                def mk(sl=sl, par=par, sc=sc, wi=wi, half=half, evs=evs, st=st):
                    si = st["si"]
                    st["si"] += 1
                    bank = psf[4 + si % 3]
                    evm = bld.op("pe", lambda e, bank=bank: e.matmul(
                        bank[:], sel2b[:], sc[par][0:2, sl * 512:(sl + 1) * 512],
                        start=True, stop=True),
                        waits=[evs, st["evac"].get(si - 3)], inc="pe")
                    off2 = half * 4096
                    st["evac"][si] = bld.op("ac", lambda e, bank=bank: e.copy(
                        scsb[:, off2 + sl * 512:off2 + (sl + 1) * 512], bank[:]),
                        waits=[evm] + ([ev_free_sc.get(wi)] if sl == 0 else []),
                        inc="ac")
                st["q"].append(mk)

            def mk_stt(wi=wi, par=par, pk=pk, Wc=Wc, eve=eve,
                       wfree=wfree, half=half, st=st):
                off2 = half * 4096
                ev = bld.op("dv", lambda e: e.scalar_tensor_tensor(
                    Wc[par][:], pk[par][:], off_col,
                    scsb[:, off2:off2 + 4096], ALU.subtract, ALU.mult),
                    waits=[eve, st["evac"][st["si"] - 1], wfree], inc="dv")
                st["stt"][wi] = ev
                ev_free_sc[wi] = ev
            st["q"].append(mk_stt)
        return st

    def dequant_drain(st, k):
        for _ in range(k):
            if st["q"]:
                st["q"].pop(0)()

    ev_free_sc = {}
    ev_stt_last = None
    dst = {0: dequant_start(0)}
    dequant_drain(dst[0], 999)
    dst[1] = dequant_start(1)
    for ob in range(NOB):
        par = ob % 2
        dequant_drain(dst[ob], 999)
        e1f = dst[ob]["stt"]["w1"]
        e3f = dst[ob]["stt"]["w3"]
        nxt = dst.get(ob + 1)
        evab = {}
        mmi = 0
        for tb in range(2):
            for j in range(32):
                w = []
                if j == 0:
                    w = [e1f, e3f, bld.now("ac"), ev_hdma.get(ob - 1)]
                evab[("a", tb)] = bld.op("pe", lambda e, tb=tb, j=j, par=par: e.matmul(
                    psf[tb][:], ap2(W1c[par], 0, 128, [[1, 128]], j * 128),
                    xT_sl(j, tb * 512, 512),
                    start=(j == 0), stop=(j == 31)),
                    waits=w, inc="pe" if j == 31 else None)
                mmi += 1
                if nxt is not None and mmi % 7 == 0:
                    dequant_drain(nxt, 1)
            for j in range(32):
                evab[("b", tb)] = bld.op("pe", lambda e, tb=tb, j=j, par=par: e.matmul(
                    psf[2 + tb][:], ap2(W3c[par], 0, 128, [[1, 128]], j * 128),
                    xT_sl(j, tb * 512, 512),
                    start=(j == 0), stop=(j == 31)),
                    waits=[], inc="pe" if j == 31 else None)
                mmi += 1
                if nxt is not None and mmi % 7 == 0:
                    dequant_drain(nxt, 1)
        ev_mm_ob[ob] = evab[("b", 1)]
        if ob + 2 < NOB:
            dst[ob + 2] = dequant_start(ob + 2)
        for tb in range(2):
            eva = bld.op("ac", lambda e, tb=tb: e.activation(asb[:], psf[tb][:], AF.Silu),
                         waits=[evab[("a", tb)], bld.now("dv")], inc="ac")
            bld.op("dv", lambda e, tb=tb: e.tensor_mul(
                hst[:, tb * 512:(tb + 1) * 512], asb[:], psf[2 + tb][:]),
                waits=[eva, evab[("b", tb)], ev_hdma.get(ob - 1)], inc="dv")
        ev_hdma[ob] = bld.dma("sy", h_s[ob * 128:(ob + 1) * 128, :], hst.ap(),
                              waits=[bld.now("dv")], sem="d_hs")
    bld.barrier()
    s4.close()
    dbg_dram("h_s", h_s, [NOB * 128, T], BF16)

    if stages < 6:
        return finish(nc, bld, es, dbg_tensors)

    s5 = ExitStack()

    def s5b(name, shape, dt=F32):
        return s5.enter_context(nc.sbuf_tensor(name, shape, dt))

    MG = [2] * 16
    W2c = [s5b(f"W2c{i}", [128, FFH], BF16) for i in range(2)]
    pk2b = s5b("pk2b", [128, FFH], I8)
    ss2 = s5b("ss2", [2, 2560], BF16)
    scsb5 = s5b("scsb5", [128, FFH], BF16)
    hb = [s5b(f"hb{i}", [128, 2 * T], BF16) for i in range(4)]
    fence5 = s5b("fence5", [1, 4], F32)
    xmt = [s5b("xmt0", [128, 512], BF16), s5b("xmt1", [128, 512], BF16)]
    r2st = [s5b("r2st0", [128, 512], BF16), s5b("r2st1", [128, 512], BF16)]

    mg_base = 0
    ev_mm_done = None
    ev_r2st = {}
    ev_stt5 = None
    for gi, gsz in enumerate(MG):
        ev_stts = []
        for mb4 in range(gsz):
            mb = mg_base + mb4
            evp = bld.dma("sy", pk2b.ap(), w2d_h[mb * 128:(mb + 1) * 128, :],
                          waits=[bld.now("gp"), bld.now("dv"), ev_mm_done], sem="d_s5")
            eve = bld.op("dv", lambda e: e.tensor_scalar(
                pk2b[:], pk2b[:], maski[:], xori[:],
                ALU.bitwise_and, ALU.bitwise_xor), waits=[evp], inc="dv")
            sl0 = 0
            ev_evac = {}
            si = 0
            for ch in range(5):
                cw = 2560 if ch < 4 else FFH - 4 * 2560
                evs = bld.dma("gp", ss2[0:2, 0:cw],
                              s2re_h[mb * 2:(mb + 1) * 2, sl0:sl0 + cw],
                              waits=[bld.now("ac")], sem="d_s5s")
                nmm = (cw + 511) // 512
                for q2 in range(nmm):
                    qw = min(512, cw - q2 * 512)
                    bank = psf[4 + si % 3]
                    evm = bld.op("pe", lambda e, q2=q2, qw=qw, bank=bank: e.matmul(
                        bank[:, 0:qw], sel2b[:], ss2[0:2, q2 * 512:q2 * 512 + qw],
                        start=True, stop=True),
                        waits=[evs, ev_evac.get(si - 3)], inc="pe")
                    ev_evac[si] = bld.op("ac", lambda e, sl0=sl0, q2=q2, qw=qw, bank=bank:
                                         e.copy(scsb5[:, sl0 + q2 * 512:sl0 + q2 * 512 + qw],
                                                bank[:, 0:qw]),
                                         waits=[evm] + ([ev_stt5] if si == 0 else []),
                                         inc="ac")
                    si += 1
                sl0 += cw
            ev_stt5 = bld.op("dv", lambda e, mb4=mb4: e.scalar_tensor_tensor(
                W2c[mb4][:], pk2b[:], off_col, scsb5[:], ALU.subtract, ALU.mult),
                waits=[eve, ev_evac[si - 1], ev_mm_done], inc="dv")
            ev_stts.append(ev_stt5)
        ev_deq = bld.now("dv")
        ev_h = {}
        ev_j = {}
        ev_last = None
        for jp in range(NOB // 2):
            buf = jp % 4
            ev_h[jp] = bld.dma("sy", ap2(hb[buf], 0, 128, [[T, 2], [1, T]], 0),
                               h_s.rearrange("(a p) t -> p a t", p=128)[:, jp * 2:(jp + 1) * 2, :],
                               waits=([ev_j.get(2 * (jp - 4) + 1)] if jp >= 4
                                      else [bld.now("pe")])
                               + ([("d_hs", 16 * NOB)] if jp < 4 else []),
                               sem=f"d_h{buf}")
            for sub in range(2):
                j = jp * 2 + sub
                for mb4 in range(gsz):
                    for tb in range(2):
                        ev_last = bld.op("pe", lambda e, j=j, sub=sub, mb4=mb4, tb=tb,
                                         buf=buf: e.matmul(
                            psf[mb4 * 2 + tb][:],
                            ap2(W2c[mb4], 0, 128, [[1, 128]], j * 128),
                            apview(hb[buf].ap(), 0, 128, sub * T + tb * 512, [[1, 512]]),
                            start=(j == 0), stop=(j == NOB - 1)),
                            waits=[ev_h[jp], ev_deq] if (sub == 0 and mb4 == 0 and tb == 0)
                            else [],
                            inc="pe" if (mb4 == gsz - 1 and tb == 1) else None)
                ev_j[j] = ev_last
        ev_mm_done = ev_last
        for mb4 in range(gsz):
            mb = mg_base + mb4
            for tb in range(2):
                evx = bld.dma("sy", xmt[tb].ap(),
                              xm_s[mb * 128:(mb + 1) * 128, tb * 512:(tb + 1) * 512],
                              waits=[bld.now("dv")]
                              + ([("d_xm", 16 * 32)] if gi == 0 and mb4 == 0 else []),
                              sem=f"d_xt{tb}")
                bld.op("dv", lambda e, mb4=mb4, tb=tb: e.tensor_add(
                    r2st[tb][:], xmt[tb][:], psf[mb4 * 2 + tb][:]),
                    waits=[evx, ev_mm_done, ev_r2st.get(tb)], inc="dv")
                ev_r2st[tb] = bld.dma("sy",
                        r2_s[mb * 128:(mb + 1) * 128, tb * 512:(tb + 1) * 512],
                        r2st[tb].ap(), waits=[bld.now("dv")], sem="d_r2")
        mg_base += gsz
    bld.barrier()
    s5.close()
    dbg_dram("r2_s", r2_s, [HID, T], BF16)

    s6 = ExitStack()

    def s6b(name, shape, dt=F32):
        return s6.enter_context(nc.sbuf_tensor(name, shape, dt))

    rt = [s6b("rt0", [128, T], BF16), s6b("rt1", [128, T], BF16)]
    sq6 = s6b("sq6", [128, T], BF16)
    ut6 = s6b("ut6", [128, 512], F32)
    ost = s6b("ost", [128, T], F32)
    rows6 = s6b("rows6", [1, 4 * T], F32)

    ev_rt = {}
    ev_cons = {}
    for mc in range(32):
        w = [bld.now("pe"), bld.now("ac")] if mc < 2 else [ev_cons[mc - 2]]
        ev_rt[mc] = bld.dma("sy", rt[mc % 2].ap(), r2_s[mc * 128:(mc + 1) * 128, :],
                            waits=w + ([("d_r2", 16 * 64)] if mc < 2 else []),
                            sem=f"d_rt{mc % 2}")
        ev_s = bld.op("ac", lambda e, mc=mc: e.activation(sq6[:], rt[mc % 2][:], AF.Square),
                      waits=[ev_rt[mc], bld.now("pe")], inc="ac")
        for tb in range(2):
            bld.op("pe", lambda e, mc=mc, tb=tb: e.matmul(
                psf[2 + tb][0:1, :], onescol_bf[:],
                apview(rt[mc % 2].ap(), 0, 128, tb * 512, [[1, 512]]),
                start=(mc == 0), stop=(mc == 31)),
                waits=[ev_rt[mc]], inc=None)
            ev_cons[mc] = bld.op("pe", lambda e, mc=mc, tb=tb: e.matmul(
                psf[4 + tb][0:1, :], onescol_bf[:], sq6[:, tb * 512:(tb + 1) * 512],
                start=(mc == 0), stop=(mc == 31)),
                waits=[ev_s], inc="pe")
    ev_stats6 = bld.now("pe")

    for tb in range(2):
        bld.op("dv", lambda e, tb=tb: e.tensor_scalar(
            rows6[0:1, tb * 512:(tb + 1) * 512], psf[2 + tb][0:1, :],
            1.0 / HID, None, ALU.mult), waits=[ev_stats6])
        bld.op("dv", lambda e, tb=tb: e.tensor_scalar(
            rows6[0:1, T + tb * 512:T + (tb + 1) * 512], psf[4 + tb][0:1, :],
            1.0 / HID, None, ALU.mult), inc="dv")
    bld.op("ac", lambda e: e.activation(rows6[0:1, 2 * T:3 * T], rows6[0:1, 0:T],
                                        AF.Square), waits=[bld.now("dv")], inc="ac")
    bld.op("dv", lambda e: e.tensor_sub(rows6[0:1, T:2 * T], rows6[0:1, T:2 * T],
                                        rows6[0:1, 2 * T:3 * T]),
           waits=[bld.now("ac")], inc="dv")
    bld.op("ac", lambda e: e.activation(rows6[0:1, 2 * T:3 * T], rows6[0:1, T:2 * T],
                                        AF.Sqrt, bias=ccol(C_EPS, 1)),
           waits=[bld.now("dv")], inc="ac")
    bld.op("dv", lambda e: e.reciprocal(rows6[0:1, 3 * T:4 * T], rows6[0:1, 2 * T:3 * T]),
           waits=[bld.now("ac")], inc="dv")
    ev_rows6 = bld.now("dv")

    ev_out = None
    ev_bc6 = []
    for tb in range(2):
        bld.op("pe", lambda e, tb=tb: e.matmul(
            psf[2 + tb][:], onesrow[:], rows6[0:1, tb * 512:(tb + 1) * 512],
            start=True, stop=True), waits=[ev_rows6])
        ev_bc6.append(bld.op("pe", lambda e, tb=tb: e.matmul(
            psf[4 + tb][:], onesrow[:], rows6[0:1, 3 * T + tb * 512:3 * T + (tb + 1) * 512],
            start=True, stop=True), inc="pe"))

    for mc in range(32):
        evr = bld.dma("sy", rt[mc % 2].ap(), r2_s[mc * 128:(mc + 1) * 128, :],
                      waits=[bld.now("dv"), ev_bc6[1]], sem=f"d_rt{mc % 2}")
        for tb in range(2):
            bld.op("dv", lambda e, mc=mc, tb=tb: e.tensor_sub(
                ut6[:], apview(rt[mc % 2].ap(), 0, 128, tb * 512, [[1, 512]]),
                psf[2 + tb][:]), waits=[evr, ev_out])
            bld.op("dv", lambda e, mc=mc, tb=tb: e.tensor_mul(
                ost[:, tb * 512:(tb + 1) * 512], ut6[:], psf[4 + tb][:]), inc="dv")
        ev_out = bld.dma("sy", outT_h[mc * 128:(mc + 1) * 128, :], ost[:],
                         waits=[bld.now("dv")], sem="sto")

    dbg("rows6", rows6.ap(), [1, 4 * T])
    s6.close()
    return finish(nc, bld, es, dbg_tensors)


def finish(nc, bld, es, dbg_tensors):
    bld.op("sy", lambda e: e.nop(), waits=bld.dma_sems_now())
    blk = es.enter_context(nc.Block())
    bld.play(blk)
    es.close()
    return nc, dbg_tensors


def rope_tables():
    inv = 1.0 / (10000.0 ** (np.arange(0, D, 2, dtype=np.float64) / D))
    ang = np.arange(2048, dtype=np.float64)[:, None] * inv[None, :]
    cos = np.cos(ang)
    cos_full = np.concatenate([cos, cos], axis=1).T.astype(np.float32)
    sin64 = np.sin(ang).T.astype(np.float32)
    return np.ascontiguousarray(cos_full), np.ascontiguousarray(sin64)


def pack_w(w_packed_i32, out_f, in_f):
    b = w_packed_i32.view(np.int8).reshape(-1, 64, 4)[:, :, 0]
    nob = out_f // 128
    nj = in_f // 128
    t = b.reshape(out_f, nj, 64).reshape(nob, 128, nj, 64)
    t = np.transpose(t, (0, 3, 2, 1))
    dup = np.concatenate([t, t], axis=1)
    return np.ascontiguousarray(dup.reshape(nob * 128, nj * 128))


def pack_s(s_flat, out_f, in_f):
    nob = out_f // 128
    nj = in_f // 128
    t = s_flat.reshape(out_f, nj, 2).reshape(nob, 128, nj, 2)
    t = np.transpose(t, (0, 3, 2, 1))
    return np.ascontiguousarray(t.reshape(nob * 2, nj * 128).astype(np.float32))


def make_shared(inp):
    cos_full, sin64 = rope_tables()
    maskneg = np.where(np.arange(128)[None, :] >= np.arange(128)[:, None],
                       0.0, -1e30).astype(np.float32)
    sel2 = np.zeros((2, 128), dtype=np.float32)
    sel2[0, :64] = 1.0 / 16.0
    sel2[1, 64:] = 1.0
    sh = {
        "wqT": np.ascontiguousarray(inp["L0_attn_Wq"].T),
        "wkT": np.ascontiguousarray(inp["L0_attn_Wk"].T),
        "wvT": np.ascontiguousarray(inp["L0_attn_Wv"].T),
        "wgT": np.ascontiguousarray(inp["L0_attn_Wg"].T),
        "woT": np.ascontiguousarray(inp["L0_attn_Wo"].T),
        "winT": np.ascontiguousarray(inp["L0_attn_Win"].T),
        "dtb": np.ascontiguousarray(inp["L0_attn_dt_bias"].reshape(H, 1)),
        "alog": np.ascontiguousarray(inp["L0_attn_A_log_bias"].reshape(H, 1)),
        "gnorm": np.ascontiguousarray(inp["L0_attn_gnorm_w"].reshape(D, 1)),
        "maskneg": maskneg,
        "ident": np.eye(128, dtype=np.float32),
        "sel2": sel2,
        "w1d": pack_w(inp["L0_ffn_w1"], FFH, HID),
        "w3d": pack_w(inp["L0_ffn_w3"], FFH, HID),
        "w2d": pack_w(inp["L0_ffn_w2"], HID, FFH),
        "s1re": pack_s(inp["L0_ffn_s1"], FFH, HID),
        "s3re": pack_s(inp["L0_ffn_s3"], FFH, HID),
        "s2re": pack_s(inp["L0_ffn_s2"], HID, FFH),
        "_cos": cos_full, "_sin": sin64,
    }
    return sh


def make_core_inputs(inp, sh, core):
    b, half = core // 2, core % 2
    x = inp["L0_attn_hidden_states"]
    xm = np.ascontiguousarray(x[b, half * T:(half + 1) * T, :].T.astype(np.float32))
    if half == 1:
        xp = np.ascontiguousarray(x[b, 0:T, :].T.astype(np.float32))
    else:
        xp = np.zeros((HID, T), dtype=np.float32)
    off = half * T
    d = {k: v for k, v in sh.items() if not k.startswith("_")}
    d.update({
        "xmain": xm, "xpre": xp,
        "cosm": np.ascontiguousarray(sh["_cos"][:, off:off + T]),
        "sinm": np.ascontiguousarray(sh["_sin"][:, off:off + T]),
        "cosp": np.ascontiguousarray(sh["_cos"][:, 0:T]),
        "sinp": np.ascontiguousarray(sh["_sin"][:, 0:T]),
    })
    return d

import concourse.bass_utils as _bass_utils


def kernel(**inputs):
    inputs = {k: np.asarray(v) for k, v in inputs.items()}
    sh = make_shared(inputs)
    in_maps = [make_core_inputs(inputs, sh, c) for c in range(8)]
    nc, _ = build()
    res = _bass_utils.run_bass_kernel_spmd(nc, in_maps, core_ids=list(range(8)))
    out = np.zeros((4, 2048, 4096), np.float32)
    for c in range(8):
        b, half = c // 2, c % 2
        out[b, half * T:(half + 1) * T, :] = np.asarray(res.results[c]["outT"]).T
    return out


# version 13
# speedup vs baseline: 1.4653x; 1.0103x over previous
import sys
sys.path.insert(0, "/opt/trn_rl_repo")
import numpy as np
import concourse.bass as bass
import concourse.mybir as mybir
from contextlib import ExitStack

F32 = mybir.dt.float32
BF16 = mybir.dt.bfloat16
I32 = mybir.dt.int32
I8 = mybir.dt.int8
AF = mybir.ActivationFunctionType
ALU = mybir.AluOpType
AX = mybir.AxisListType

H, KVH, D, CHUNK = 32, 8, 128, 128
DBG_HN = None
DBG_OB = None
PARANOID_S4 = False
HID = 4096
FFH = 11008
T = 1024
NCH = 8
NOB = FFH // 128

ENGS = ("pe", "dv", "ac", "gp", "sy")
ENG_ATTR = {"pe": "tensor", "dv": "vector", "ac": "scalar", "gp": "gpsimd", "sy": "sync"}


class Bld:

    def __init__(self, nc):
        self.nc = nc
        self.ops = {e: [] for e in ENGS}
        self.cnt = {}
        self.waited = {}
        self.sems = {}

    def _dedup(self, eng, evs):
        out = []
        for ev in evs:
            if ev is None:
                continue
            s, v = ev
            if v <= 0:
                continue
            key = (eng, s)
            if self.waited.get(key, 0) >= v:
                continue
            self.waited[key] = v
            out.append((s, v))
        return out

    def op(self, eng, fn, waits=(), inc=None, amt=1):
        w = self._dedup(eng, list(waits))
        ev = None
        if inc is not None:
            self.cnt[inc] = self.cnt.get(inc, 0) + amt
            ev = (inc, self.cnt[inc])
        self.ops[eng].append((w, fn, inc, amt))
        return ev

    def dma(self, queue, out, in_, waits=(), sem=None, **kw):
        assert sem is not None, "stream semaphore required"
        if sem not in self.cnt:
            self.sems[sem] = self.mksem(sem)
            self.cnt[sem] = 0
        w = list(waits) + [(sem, self.cnt[sem])]
        return self.op(queue, lambda e: e.dma_start(out=out, in_=in_, **kw),
                       waits=w, inc=sem, amt=16)

    def dma_sems_now(self):
        return [self.now(s) for s in self.cnt if s.startswith("d_") or s == "sto"]

    def now(self, sem):
        return (sem, self.cnt.get(sem, 0))

    def barrier(self):
        evs = [self.now(s) for s in self.cnt]
        for e in ENGS:
            w = self._dedup(e, evs)
            if w:
                self.ops[e].append((w, None, None, 0))

    def play(self, blk):
        handles = self.sems
        for eng in ENGS:
            ops = self.ops[eng]
            if not ops:
                continue

            def body(h, ops=ops):
                for (waits, fn, inc, amt) in ops:
                    for (s, v) in waits:
                        h.wait_ge(handles[s], v)
                    if fn is None:
                        continue
                    inst = fn(h)
                    if inc is not None:
                        inst.then_inc(handles[inc], amt)
            getattr(blk, ENG_ATTR[eng])(body)


def ap2(t, part0, nparts, free_ap, off_elems=0):
    a = t.ap()
    return bass.AP(a.tensor, a.offset + part0 * a.ap[0][0] + off_elems,
                   [[a.ap[0][0], nparts]] + [list(x) for x in free_ap])


def apview(ap, part0, nparts, off_elems, free_ap):
    return bass.AP(ap.tensor, ap.offset + part0 * ap.ap[0][0] + off_elems,
                   [[ap.ap[0][0], nparts]] + [list(x) for x in free_ap])


def build(debug_names=(), stages=99):
    nc = bass.Bass("TRN2", target_bir_lowering=False)
    bld = Bld(nc)
    dbg_tensors = {}

    def din(name, shape, dt=F32):
        return nc.dram_tensor(name, shape, dt, kind="ExternalInput").ap()

    xmain_h = din("xmain", [HID, T])
    xpre_h = din("xpre", [HID, T])
    wqT_h = din("wqT", [HID, HID])
    wkT_h = din("wkT", [HID, KVH * D])
    wvT_h = din("wvT", [HID, KVH * D])
    wgT_h = din("wgT", [HID, HID])
    woT_h = din("woT", [HID, HID])
    winT_h = din("winT", [HID, H])
    dtb_h = din("dtb", [H, 1])
    alog_h = din("alog", [H, 1])
    gnorm_h = din("gnorm", [D, 1])
    cosm_h = din("cosm", [D, T])
    sinm_h = din("sinm", [64, T])
    cosp_h = din("cosp", [D, T])
    sinp_h = din("sinp", [64, T])
    maskneg_h = din("maskneg", [CHUNK, CHUNK])
    ident_h = din("ident", [128, 128])
    sel2_h = din("sel2", [2, 128])
    w1d_h = din("w1d", [NOB * 128, 4096], I8)
    w3d_h = din("w3d", [NOB * 128, 4096], I8)
    w2d_h = din("w2d", [32 * 128, FFH], I8)
    s1re_h = din("s1re", [NOB * 2, 4096])
    s3re_h = din("s3re", [NOB * 2, 4096])
    s2re_h = din("s2re", [32 * 2, FFH])

    outT_h = nc.dram_tensor("outT", [HID, T], F32, kind="ExternalOutput").ap()

    y_s = nc.dram_tensor("y_s", [H * D, T], BF16, kind="Internal").ap()
    h_s = nc.dram_tensor("h_s", [NOB * 128, T], BF16, kind="Internal").ap()
    xm_s = nc.dram_tensor("xm_s", [HID, T], BF16, kind="Internal").ap()
    r2_s = nc.dram_tensor("r2_s", [HID, T], BF16, kind="Internal").ap()

    def dbg(name, src_ap, shape, dt=F32):
        if name not in debug_names:
            return
        dt = src_ap.tensor.dtype
        hh = nc.dram_tensor("dbg_" + name, shape, dt, kind="ExternalOutput").ap()
        dbg_tensors[name] = "dbg_" + name
        bld.dma("sy", hh, src_ap, sem="d_dbg",
                waits=[bld.now("dv"), bld.now("ac"), bld.now("pe"), bld.now("gp")]
                + bld.dma_sems_now())

    def dbg_dram(name, src_ap, shape, dt=F32):
        if name not in debug_names:
            return
        hh = nc.dram_tensor("dbg_" + name, shape, dt, kind="ExternalOutput").ap()
        dbg_tensors[name] = "dbg_" + name
        bld.dma("sy", hh, src_ap, sem="d_dbg",
                waits=[bld.now("dv"), bld.now("ac"), bld.now("pe"), bld.now("gp")]
                + bld.dma_sems_now())

    es = ExitStack()

    def sb(name, shape, dt=F32, side=None):
        return es.enter_context(nc.sbuf_tensor(name, shape, dt, side=side))

    psf = [es.enter_context(nc.psum_tensor(f"psf{i}", [128, 512], F32)) for i in range(7)]
    psb = es.enter_context(nc.psum_tensor("psb", [128, 1024], BF16))

    bld.mksem = lambda name: es.enter_context(nc.semaphore(name))
    for s in ("pe", "dv", "ac", "gp"):
        bld.sems[s] = es.enter_context(nc.semaphore(s))
        bld.cnt[s] = 0

    consts = sb("consts", [128, 8], F32)
    C_ONES, C_EPS, C_DTB, C_A, C_GNORM, C_OFF = 0, 1, 2, 3, 4, 5
    NCOL = 8

    def ccol(c, nparts=128, p0=0):
        return ap2(consts, p0, nparts, [[NCOL, 1]], c)

    onesrow = sb("onesrow", [1, 128], F32)
    sel2 = sb("sel2_sb", [2, 128], F32)
    ident = sb("ident_sb", [128, 128], F32)
    identb = sb("identb_sb", [128, 128], BF16)
    maskneg = sb("maskneg_sb", [128, 128], F32)
    onescol_bf = sb("onescol_bf", [128, 1], BF16)
    sel2b = sb("sel2b_sb", [2, 128], BF16)
    maski = sb("maski", [128, 1], I32)
    xori = sb("xori", [128, 1], I32)

    bld.dma("sy", ccol(C_DTB, H), dtb_h, allow_slow_non_contiguous=True, sem="d_c")
    bld.dma("sy", ccol(C_A, H), alog_h, allow_slow_non_contiguous=True, sem="d_c")
    bld.dma("sy", ccol(C_GNORM, D), gnorm_h, allow_slow_non_contiguous=True, sem="d_c")
    bld.dma("sy", maskneg.ap(), maskneg_h, sem="d_c")
    bld.dma("sy", ident.ap(), ident_h, sem="d_c")
    bld.dma("sy", sel2.ap(), sel2_h, sem="d_c")
    bld.op("gp", lambda e: e.memset(ccol(C_ONES), 1.0))
    bld.op("gp", lambda e: e.memset(ccol(C_EPS), 1e-5))
    bld.op("gp", lambda e: e.memset(ccol(C_OFF, 64, 0), 0.0))
    bld.op("gp", lambda e: e.memset(ccol(C_OFF, 64, 64), 8.0))
    bld.op("gp", lambda e: e.memset(onesrow.ap(), 1.0))
    bld.op("gp", lambda e: e.memset(onescol_bf.ap(), 1.0))
    bld.op("gp", lambda e: e.memset(maski[0:64, :], -16))
    bld.op("gp", lambda e: e.memset(maski[64:128, :], 15))
    bld.op("gp", lambda e: e.memset(xori[0:64, :], 0))
    bld.op("gp", lambda e: e.memset(xori[64:128, :], 8), inc="gp")
    bld.barrier()
    bld.op("ac", lambda e: e.activation(ccol(C_A, H), ccol(C_A, H), AF.Exp))
    bld.op("ac", lambda e: e.mul(ccol(C_A, H), ccol(C_A, H), -1.0), inc="ac")
    bld.op("dv", lambda e: e.tensor_copy(identb[:], ident[:]))
    bld.op("dv", lambda e: e.tensor_copy(sel2b[:], sel2[:]), inc="dv")

    ones_col = ccol(C_ONES)
    eps_col = ccol(C_EPS)
    dtb_col = ccol(C_DTB, H)
    A_col = ccol(C_A, H)
    gnorm_col = ccol(C_GNORM, D)
    off_col = ccol(C_OFF)

    scope_m = ExitStack()

    def smb(name, shape, dt=F32):
        return scope_m.enter_context(nc.sbuf_tensor(name, shape, dt))

    kT = smb("kT", [128, KVH * T], BF16)
    ktok = smb("ktok", [128, NCH * KVH * D], BF16)
    vtok = smb("vtok", [128, NCH * KVH * D], BF16)
    G_all = smb("G_all", [H, 2 * T], F32)
    Gcol = smb("Gcol", [128, 16 * H], F32)
    dt_tok = smb("dt_tok", [128, 16 * H], F32)
    states = smb("states", [128, H * D], F32)
    cosm = smb("cosm_sb", [128, T], F32)
    sinm = smb("sinm_sb", [64, T], F32)
    grow = smb("grow", [1, T], F32)
    gl_cols = smb("gl_cols", [128, 2 * NCH], F32)
    xcb = smb("xcb", [128, 128], F32)
    kdb = smb("kdb", [128, 128], F32)

    pre_scope = ExitStack()
    ktok_p = pre_scope.enter_context(nc.sbuf_tensor("ktok_p", [128, NCH * KVH * D], BF16))
    vtok_p = pre_scope.enter_context(nc.sbuf_tensor("vtok_p", [128, NCH * KVH * D], BF16))
    cosp = pre_scope.enter_context(nc.sbuf_tensor("cosp_sb", [128, T], F32))
    sinp = pre_scope.enter_context(nc.sbuf_tensor("sinp_sb", [64, T], F32))

    xT = None

    def xT_sl(kc, t0, nt):
        return ap2(xT, 0, 128, [[1, nt]], kc * T + t0)

    bld.dma("sy", cosm.ap(), cosm_h, sem="d_c")
    bld.dma("sy", sinm.ap(), sinm_h, sem="d_c")
    bld.dma("sy", cosp.ap(), cosp_h, sem="d_c")
    bld.dma("sy", sinp.ap(), sinp_h, sem="d_c")
    bld.op("gp", lambda e: e.memset(states.ap(), 0.0), inc="gp")
    bld.barrier()

    def emit_wcol_load(dst_t, w_h, mb, ncols=128, waits=(), par=0):
        dst = ap2(dst_t, 0, 128, [[ncols, 32], [1, ncols]], 0)
        src = w_h.rearrange("(k p) m -> p k m", p=128)[:, :, mb * ncols:(mb + 1) * ncols]
        return bld.dma("gp", dst, src, waits=list(waits), sem=f"d_w{par}")

    def emit_proj(w_h, nmb, xsl_fn, tlen, bank_fn, evac_fn, wbuf, ncols=128,
                  pre_waits=()):
        ev_mm = {}
        ev_free = {}
        for mb in range(nmb):
            w = [ev_mm[mb - 2]] if mb >= 2 else list(pre_waits)
            evw = emit_wcol_load(wbuf[mb % 2], w_h, mb, ncols, waits=w, par=mb % 2)
            bank = bank_fn(mb)
            for kc in range(32):
                ev_mm[mb] = bld.op("pe", lambda e, mb=mb, kc=kc, bank=bank: e.matmul(
                    bank,
                    ap2(wbuf[mb % 2], 0, 128, [[1, ncols]], kc * ncols),
                    xsl_fn(kc, 0, tlen),
                    start=(kc == 0), stop=(kc == 31)),
                    waits=([evw] + ([ev_free[mb - 2]] if mb >= 2 else []))
                    if kc == 0 else [], inc="pe" if kc == 31 else None)
            evac_fn(mb, bank, ev_mm[mb])
            ev_free[mb] = bld.now("dv")

    def emit_s1_sb(sbi):
        pre = sbi < 2
        t0 = (sbi % 2) * 512
        x_h_src = xpre_h if pre else xmain_h
        cosT, sinT = (cosp, sinp) if pre else (cosm, sinm)
        ktok_dst, vtok_dst = (ktok_p, vtok_p) if pre else (ktok, vtok)
        goff = 0 if pre else T
        nn0 = (0 if pre else 8) + (sbi % 2) * 4
        nl0 = (sbi % 2) * 4

        scope = ExitStack()

        def lsb(name, shape, dt=F32):
            return scope.enter_context(nc.sbuf_tensor(f"{name}_{sbi}", shape, dt))

        if pre:
            xpre_bf = lsb("xpre_bf", [128, 32 * 512], BF16)

            def xsl(kc, tt0, nt):
                return ap2(xpre_bf, 0, 128, [[1, nt]], kc * 512 + tt0)
            dstx = ap2(xpre_bf, 0, 128, [[512, 32], [1, 512]], 0)
        else:
            def xsl(kc, tt0, nt):
                return xT_sl(kc, t0 + tt0, nt)
            dstx = ap2(xT, 0, 128, [[T, 32], [1, 512]], t0)
        srcx = x_h_src.rearrange("(k p) t -> p k t", p=128)[:, :, t0:t0 + 512]
        evx = bld.dma("gp", dstx, srcx, waits=[bld.now("pe"), bld.now("dv")], sem="d_x")

        kT_sb = lsb("kT_sb", [128, KVH * 512], BF16) if pre else None
        vT_sb = lsb("vT_sb", [128, 512], BF16)
        wbuf = [lsb("wbuf0", [128, 32 * 128], BF16), lsb("wbuf1", [128, 32 * 128], BF16)]
        ropeA = lsb("ropeA", [128, 512], F32)
        ropeB = lsb("ropeB", [128, 512], F32)

        def krope_dst(mb, tt0, nt, parts=(0, 128)):
            if pre:
                return apview(kT_sb.ap(), parts[0], parts[1], mb * 512 + tt0, [[1, nt]])
            return apview(kT.ap(), parts[0], parts[1], mb * T + t0 + tt0, [[1, nt]])

        cssl = cosT[:, t0:t0 + 512]
        snsl = sinT[:, t0:t0 + 512]

        def k_evac(mb, bank, ev_mm):
            bld.op("dv", lambda e, bank=bank: e.tensor_mul(ropeA[0:64, :], bank[0:64, :], cssl[0:64, :]),
                   waits=[ev_mm])
            bld.op("dv", lambda e, bank=bank: e.tensor_mul(ropeA[64:128, :], bank[64:128, :], cssl[64:128, :]))
            bld.op("dv", lambda e, bank=bank: e.tensor_mul(ropeB[0:64, :], bank[64:128, :], snsl[:]))
            bld.op("dv", lambda e, bank=bank: e.tensor_mul(ropeB[64:128, :], bank[0:64, :], snsl[:]))
            bld.op("dv", lambda e, mb=mb: e.tensor_sub(krope_dst(mb, 0, 512, (0, 64)),
                                                       ropeA[0:64, :], ropeB[0:64, :]))
            bld.op("dv", lambda e, mb=mb: e.tensor_add(krope_dst(mb, 0, 512, (64, 64)),
                                                       ropeA[64:128, :], ropeB[64:128, :]),
                   inc="dv")
            ev_rope = bld.now("dv")
            for n4 in range(4):
                nl = nl0 + n4
                srck = krope_dst(mb, n4 * 128, 128)
                half = (n4 % 2) * 512
                evt = bld.op("pe", lambda e, srck=srck, half=half: e.transpose(
                    apview(psb.ap(), 0, 128, half, [[1, 128]]), srck, identb[:]),
                    waits=[ev_rope, bld.now("dv")], inc="pe")
                dstk = ap2(ktok_dst, 0, 128, [[1, 128]], (nl * KVH + mb) * D)
                bld.op("dv", lambda e, dstk=dstk, half=half: e.tensor_copy(
                    dstk, apview(psb.ap(), 0, 128, half, [[1, 128]])),
                    waits=[evt], inc="dv")

        emit_proj(wkT_h, KVH, xsl, 512, lambda mb: psf[mb % 2][:],
                  k_evac, wbuf, pre_waits=[evx])

        def v_evac(mb, bank, ev_mm):
            bld.op("dv", lambda e, bank=bank: e.tensor_copy(vT_sb[:], bank), waits=[ev_mm], inc="dv")
            ev_c = bld.now("dv")
            for n4 in range(4):
                nl = nl0 + n4
                half = (n4 % 2) * 512
                evt = bld.op("pe", lambda e, n4=n4, half=half: e.transpose(
                    apview(psb.ap(), 0, 128, half, [[1, 128]]),
                    vT_sb[:, n4 * 128:(n4 + 1) * 128], identb[:]),
                    waits=[ev_c, bld.now("dv")], inc="pe")
                dstv = ap2(vtok_dst, 0, 128, [[1, 128]], (nl * KVH + mb) * D)
                bld.op("dv", lambda e, dstv=dstv, half=half: e.tensor_copy(
                    dstv, apview(psb.ap(), 0, 128, half, [[1, 128]])),
                    waits=[evt], inc="dv")

        emit_proj(wvT_h, KVH, xsl, 512, lambda mb: psf[2 + mb % 2][:],
                  v_evac, wbuf, pre_waits=[bld.now("pe")])

        dt_sl = lsb("dt_sl", [H, 512], F32)
        sp1 = lsb("sp1", [H, 512], F32)
        sp2 = lsb("sp2", [H, 512], F32)
        gsl = lsb("gsl", [H, 512], F32)
        gon = lsb("gon", [H, 128], F32)
        dts = dt_sl[:, :]

        def dt_evac(mb, bank, ev_mm):
            bld.op("dv", lambda e, bank=bank: e.tensor_scalar(sp1[:], bank, dtb_col, None, ALU.add),
                   waits=[ev_mm])
            bld.op("dv", lambda e: e.tensor_relu(sp2[:], sp1[:]), inc="dv")
            bld.op("ac", lambda e: e.activation(sp1[:], sp1[:], AF.Abs), waits=[bld.now("dv")])
            bld.op("ac", lambda e: e.activation(sp1[:], sp1[:], AF.Exp, scale=-1.0))
            bld.op("ac", lambda e: e.activation(sp1[:], sp1[:], AF.Ln, bias=1.0), inc="ac")
            bld.op("dv", lambda e: e.tensor_add(dts, sp1[:], sp2[:]),
                   waits=[bld.now("ac")], inc="dv")

        emit_proj(winT_h, 1, xsl, 512, lambda mb: psf[4][0:H, :],
                  dt_evac, wbuf, ncols=H, pre_waits=[bld.now("pe")])

        ev_sp = bld.now("dv")
        bld.op("gp", lambda e: e.memset(gon[:], 1.0), inc="gp")
        bld.op("dv", lambda e: e.tensor_scalar(gsl[:], dts, A_col, None, ALU.mult),
               waits=[ev_sp, bld.now("gp"), bld.now("ac")])
        Gs = ap2(G_all, 0, H, [[1, 512]], goff + t0)
        for n4 in range(4):
            bld.op("dv", lambda e, n4=n4, Gs=Gs: e.tensor_tensor_scan(
                apview(Gs, 0, H, n4 * 128, [[1, 128]]),
                gon[:], gsl[:, n4 * 128:(n4 + 1) * 128], 0.0, ALU.mult, ALU.add),
                inc="dv" if n4 == 3 else None)
        ev_G = bld.now("dv")

        for n4 in range(4):
            nn = nn0 + n4
            srcd = dt_sl[:, n4 * 128:(n4 + 1) * 128]
            srcg = ap2(G_all, 0, H, [[1, 128]], goff + t0 + n4 * 128)
            ev1 = bld.op("pe", lambda e, srcd=srcd: e.transpose(psf[1][:, 0:H], srcd,
                                                                ident[0:H, 0:H]),
                         waits=[ev_G, bld.now("ac")], inc="pe")
            bld.op("ac", lambda e, nn=nn: e.copy(ap2(dt_tok, 0, 128, [[1, H]], nn * H),
                                                 psf[1][:, 0:H]),
                   waits=[ev1], inc="ac")
            ev2 = bld.op("pe", lambda e, srcg=srcg: e.transpose(psf[2][:, 0:H], srcg,
                                                                ident[0:H, 0:H]),
                         waits=[ev_G, bld.now("ac")], inc="pe")
            bld.op("ac", lambda e, nn=nn: e.copy(ap2(Gcol, 0, 128, [[1, H]], nn * H),
                                                 psf[2][:, 0:H]),
                   waits=[ev2], inc="ac")

        bld.barrier()
        scope.close()

    emit_s1_sb(0)
    emit_s1_sb(1)

    dbg("vtok_p", vtok_p.ap(), [128, NCH * KVH * D])
    dbg("ktok_p", ktok_p.ap(), [128, NCH * KVH * D])

    def emit_head_cols(h, pre):
        goff = 0 if pre else T
        nnb = 0 if pre else 8
        evr = bld.dma("sy", grow.ap(), ap2(G_all, h, 1, [[1, T]], goff),
                      waits=[bld.now("pe"), bld.now("dv"), bld.now("ac")], sem="d_gr")
        gl_row = apview(grow.ap(), 0, 1, 127, [[128, NCH]])
        ev1 = bld.op("pe", lambda e, gl_row=gl_row: e.matmul(
            psf[3][:, 0:NCH], onesrow[:], gl_row, start=True, stop=True),
            waits=[evr, bld.now("dv")], inc="pe")
        gc_sl = ap2(Gcol, 0, 128, [[H, NCH]], nnb * H + h)
        bld.op("dv", lambda e, gc_sl=gc_sl: e.tensor_sub(gl_cols[:, 0:NCH], psf[3][:, 0:NCH], gc_sl),
               waits=[ev1], inc="dv")
        bld.op("ac", lambda e: e.activation(gl_cols[:, 0:NCH], gl_cols[:, 0:NCH], AF.Exp),
               waits=[bld.now("dv")])
        bld.op("ac", lambda e: e.activation(gl_cols[:, NCH:2 * NCH], psf[3][:, 0:NCH], AF.Exp),
               inc="ac")
        return bld.now("ac")

    for h in range(H):
        ev_cols = emit_head_cols(h, pre=True)
        kv = h // 4
        for n in range(NCH):
            bld.op("dv", lambda e, n=n, kv=kv, h=h: e.tensor_scalar(
                xcb[:], ap2(vtok_p, 0, 128, [[1, 128]], (n * KVH + kv) * D),
                ap2(dt_tok, 0, 128, [[H, 1]], n * H + h), None, ALU.mult),
                waits=[ev_cols])
            bld.op("dv", lambda e, n=n, kv=kv: e.tensor_scalar(
                kdb[:], ap2(ktok_p, 0, 128, [[1, 128]], (n * KVH + kv) * D),
                gl_cols[:, n:n + 1], None, ALU.mult), inc="dv")
            evk = bld.now("dv")
            evm = bld.op("pe", lambda e: e.matmul(psf[4][:, 0:128], kdb[:], xcb[:],
                                                  start=True, stop=True),
                         waits=[evk], inc="pe")
            bld.op("dv", lambda e, h=h, n=n: e.scalar_tensor_tensor(
                ap2(states, 0, 128, [[1, 128]], h * D),
                ap2(states, 0, 128, [[1, 128]], h * D),
                gl_cols[:, NCH + n:NCH + n + 1],
                psf[4][:, 0:128], ALU.mult, ALU.add),
                waits=[evm], inc="dv")
    bld.barrier()
    dbg("states", states.ap(), [128, H * D])
    pre_scope.close()

    xT = nc.alloc_sbuf_tensor("xT", [128, 32 * T], BF16, side="right")
    emit_s1_sb(2)
    emit_s1_sb(3)
    dbg("kT", kT.ap(), [128, KVH * T])
    dbg("vtok", vtok.ap(), [128, NCH * KVH * D])
    dbg("ktok", ktok.ap(), [128, NCH * KVH * D])
    dbg("G_all", G_all.ap(), [H, 2 * T])
    dbg("Gcol", Gcol.ap(), [128, 16 * H])
    dbg("dt_tok", dt_tok.ap(), [128, 16 * H])

    if stages < 3:
        return finish(nc, bld, es, dbg_tensors)

    s2 = ExitStack()

    def s2b(name, shape, dt=F32):
        return s2.enter_context(nc.sbuf_tensor(name, shape, dt))

    qT2 = [s2b("qT_a", [128, T], BF16), s2b("qT_b", [128, T], BF16)]
    gs2 = [s2b("gs_a", [128, T], BF16), s2b("gs_b", [128, T], BF16)]
    Y_h = s2b("Y_h", [128, T], BF16)
    wqbuf = [s2b("wqbuf0", [128, 32 * 128], BF16), s2b("wqbuf1", [128, 32 * 128], BF16)]
    ropeA2 = s2b("ropeA2", [128, 512], F32)
    ropeB2 = s2b("ropeB2", [128, 512], F32)
    dfT = s2b("dfT", [128, 128], F32)
    dbgyA = s2b("dbgyA", [128, 128], F32)
    dbgyB = s2b("dbgyB", [128, 128], F32)
    egb = s2b("egb", [128, 128], F32)
    STDb = s2b("STDb", [128, 128], F32)
    qscb = s2b("qscb", [128, 128], F32)
    prb = s2b("prb", [128, 128], F32)
    sqb = s2b("sqb", [128, 128], F32)
    t5b = s2b("t5b", [128, 128], F32)
    rmsr = s2b("rmsr", [1, 128], F32)
    rstdr = s2b("rstdr", [1, 128], F32)
    y_all = s2b("y_all", [128, T], F32)
    sq_all = s2b("sq_all", [128, T], BF16)
    rmsr2 = s2b("rmsr2", [1, T], F32)
    rstdr2 = s2b("rstdr2", [1, T], F32)
    t5w = s2b("t5w", [128, 512], F32)

    ev_w_prev = {}
    ev_free = {"q": None, "g": None}

    def proj_start(h):
        st = {"h": h, "mm": [], "evs": {}, "ev_q": None}
        hp = h % 2

        def emit_rope():
            for tb in range(2):
                bank = psf[5 + tb]
                cssl = cosm[:, tb * 512:(tb + 1) * 512]
                snsl = sinm[:, tb * 512:(tb + 1) * 512]
                bld.op("dv", lambda e, bank=bank, cssl=cssl: e.tensor_mul(
                    ropeA2[0:64, :], bank[0:64, :], cssl[0:64, :]),
                    waits=[st["evs"][("q", tb)]])
                bld.op("dv", lambda e, bank=bank, cssl=cssl: e.tensor_mul(
                    ropeA2[64:128, :], bank[64:128, :], cssl[64:128, :]))
                bld.op("dv", lambda e, bank=bank, snsl=snsl: e.tensor_mul(
                    ropeB2[0:64, :], bank[64:128, :], snsl[:]))
                bld.op("dv", lambda e, bank=bank, snsl=snsl: e.tensor_mul(
                    ropeB2[64:128, :], bank[0:64, :], snsl[:]))
                bld.op("dv", lambda e, tb=tb: e.tensor_sub(
                    apview(qT2[hp].ap(), 0, 64, tb * 512, [[1, 512]]),
                    ropeA2[0:64, :], ropeB2[0:64, :]))
                bld.op("dv", lambda e, tb=tb: e.tensor_add(
                    apview(qT2[hp].ap(), 64, 64, tb * 512, [[1, 512]]),
                    ropeA2[64:128, :], ropeB2[64:128, :]), inc="dv")
            st["ev_q"] = bld.now("dv")
            ev_free["q"] = st["ev_q"]

        for wi, w_h in (("q", wqT_h), ("g", wgT_h)):
            par = proj_start.parity
            proj_start.parity ^= 1
            w = [ev_w_prev.get(par)]
            evw = emit_wcol_load(wqbuf[par], w_h, h, 128, waits=w, par=par)
            for tb in range(2):
                bank = psf[5 + tb]
                for kc in range(32):
                    def mk(wi=wi, par=par, tb=tb, kc=kc, bank=bank, evw=evw, st=st):
                        if wi == "q":
                            w0 = [evw, ev_free["q"]] if kc == 0 else []
                        else:
                            w0 = [evw, ev_free["g"], st["ev_q"]] if kc == 0 else []
                        ev = bld.op("pe", lambda e: e.matmul(
                            bank[:], ap2(wqbuf[par], 0, 128, [[1, 128]], kc * 128),
                            xT_sl(kc, tb * 512, 512),
                            start=(kc == 0), stop=(kc == 31)),
                            waits=w0, inc="pe" if kc == 31 else None)
                        if ev is not None:
                            st["evs"][(wi, tb)] = ev
                            if wi == "g" and tb == 1:
                                ev_w_prev[par] = ev
                                ev_w_prev[1 - par] = ev
                    st["mm"].append(mk)
            if wi == "q":
                st["mm"].append(emit_rope)
        return st
    proj_start.parity = 0

    def proj_drain(st, k):
        for _ in range(k):
            if st["mm"]:
                st["mm"].pop(0)()

    def proj_finish(st):
        h = st["h"]
        hp = h % 2
        proj_drain(st, 999)
        ev_q = st["ev_q"]
        for tb in range(2):
            bank = psf[5 + tb]
            bld.op("ac", lambda e, bank=bank, tb=tb, hp=hp: e.activation(
                gs2[hp][:, tb * 512:(tb + 1) * 512], bank[:], AF.Silu),
                waits=[st["evs"][("g", tb)]], inc="ac")
        ev_g = bld.now("ac")
        ev_free["g"] = ev_g
        return ev_q, ev_g

    pc = {"STD": None, "egb": None, "qsc": None, "sq": None, "rms": None, "Y": None,
          "rstd": None, "epi": None, "epi_rms": None, ("yev", 0): None, ("yev", 1): None}

    st0 = proj_start(0)
    evqg = {0: proj_finish(st0)}
    ev_cols_h = {0: emit_head_cols(0, pre=False)}

    for h in range(H):
        kv = h // 4
        hp = h % 2
        ev_cols = ev_cols_h[h]
        ev_q, ev_g = evqg[h]
        qT_h = qT2[hp]
        gsil_h = gs2[hp]
        stn = proj_start(h + 1) if h + 1 < H else None
        for n in range(NCH):
            col = (8 + n) * H + h
            ev_st = bld.op("pe", lambda e, kv=kv, n=n, qT_h=qT_h: e.matmul(
                psf[0][:, 0:128],
                ap2(kT, 0, 128, [[1, 128]], kv * T + n * 128),
                ap2(qT_h, 0, 128, [[1, 128]], n * 128),
                start=True, stop=True),
                waits=[ev_q, ev_g, pc["STD"]], inc="pe")
            ev_gi = bld.op("pe", lambda e, n=n: e.matmul(
                psf[1][:, 0:128], onesrow[:],
                apview(grow.ap(), 0, 1, n * 128, [[1, 128]]),
                start=True, stop=True),
                waits=[ev_cols, pc["STD"], pc["egb"]], inc="pe")
            ev3 = bld.op("dv", lambda e, col=col: e.scalar_tensor_tensor(
                dfT[:], psf[1][:, 0:128], ap2(Gcol, 0, 128, [[H, 1]], col),
                maskneg[:], ALU.subtract, ALU.add),
                waits=[ev_gi], inc="dv")
            ev4 = bld.op("ac", lambda e: e.activation(dfT[:], dfT[:], AF.Exp),
                         waits=[ev3], inc="ac")
            ev5 = bld.op("ac", lambda e: e.activation(egb[:], psf[1][:, 0:128], AF.Exp),
                         waits=[ev_gi, pc["qsc"]], inc="ac")
            ev6 = bld.op("dv", lambda e: e.tensor_mul(STDb[:], dfT[:], psf[0][:, 0:128]),
                         waits=[ev4, ev_st], inc="dv")
            pc["STD"] = ev6
            ev7 = bld.op("dv", lambda e, n=n, qT_h=qT_h: e.tensor_mul(
                qscb[:], ap2(qT_h, 0, 128, [[1, 128]], n * 128), egb[:]),
                waits=[ev5], inc="dv")
            pc["qsc"] = ev7
            pc["egb"] = ev5
            bld.op("dv", lambda e, n=n, kv=kv, col=col: e.tensor_scalar(
                xcb[:], ap2(vtok, 0, 128, [[1, 128]], (n * KVH + kv) * D),
                ap2(dt_tok, 0, 128, [[H, 1]], col), None, ALU.mult))
            bld.op("dv", lambda e, n=n, kv=kv: e.tensor_scalar(
                kdb[:], ap2(ktok, 0, 128, [[1, 128]], (n * KVH + kv) * D),
                gl_cols[:, n:n + 1], None, ALU.mult))
            ev10 = bld.op("dv", lambda e, h=h: e.tensor_copy(
                prb[:], ap2(states, 0, 128, [[1, 128]], h * D)), inc="dv")
            ybank = psf[2 + 2 * (n % 2)]
            bld.op("pe", lambda e, ybank=ybank: e.matmul(ybank[:, 0:128], xcb[:], STDb[:],
                                            start=True, stop=False),
                   waits=[ev10, pc.get(("yev", n % 2)), ev6]
                   + ([pc.get("epi")] if n % 2 == 1 else []), inc=None)
            ev_y = bld.op("pe", lambda e, ybank=ybank: e.matmul(ybank[:, 0:128], prb[:], qscb[:],
                                                   start=False, stop=True),
                          waits=[ev7], inc="pe")
            ev_cs = bld.op("pe", lambda e: e.matmul(psf[3][:, 0:128], kdb[:], xcb[:],
                                                    start=True, stop=True),
                           waits=[ev10], inc="pe")
            bld.op("dv", lambda e, h=h, n=n: e.scalar_tensor_tensor(
                ap2(states, 0, 128, [[1, 128]], h * D),
                ap2(states, 0, 128, [[1, 128]], h * D),
                gl_cols[:, NCH + n:NCH + n + 1],
                psf[3][:, 0:128], ALU.mult, ALU.add),
                waits=[ev_cs], inc="dv")
            pc[("yev", n % 2)] = bld.op("ac", lambda e, n=n, ybank=ybank: e.copy(
                y_all[:, n * 128:(n + 1) * 128], ybank[:, 0:128]),
                waits=[ev_y, pc.get("epi")], inc="ac")
            if stn is not None:
                proj_drain(stn, 16)
        ev_sqa = bld.op("ac", lambda e: e.activation(sq_all[:], y_all[:], AF.Square),
                        waits=[pc[("yev", 0)], pc[("yev", 1)]], inc="ac")
        ev_epi_last = None
        for tb in range(2):
            ev_ssq = bld.op("pe", lambda e, tb=tb: e.matmul(
                psf[4][0:1, :], onescol_bf[:], sq_all[:, tb * 512:(tb + 1) * 512],
                start=True, stop=True),
                waits=[ev_sqa, pc.get("epi_rms"), ev_epi_last], inc="pe")
            ev_rms = bld.op("ac", lambda e, tb=tb: e.activation(
                rmsr2[0:1, tb * 512:(tb + 1) * 512], psf[4][0:1, :], AF.Sqrt,
                scale=1.0 / 128.0, bias=ccol(C_EPS, 1)),
                waits=[ev_ssq], inc="ac")
            pc["epi_rms"] = ev_rms
            ev_rcp = bld.op("dv", lambda e, tb=tb: e.reciprocal(
                rstdr2[0:1, tb * 512:(tb + 1) * 512],
                rmsr2[0:1, tb * 512:(tb + 1) * 512]), waits=[ev_rms], inc="dv")
            ev_rb = bld.op("pe", lambda e, tb=tb: e.matmul(
                psf[4][:], onesrow[:], rstdr2[0:1, tb * 512:(tb + 1) * 512],
                start=True, stop=True), waits=[ev_rcp, ev_rms], inc="pe")
            bld.op("dv", lambda e, tb=tb, gsil_h=gsil_h: e.scalar_tensor_tensor(
                t5w[:], y_all[:, tb * 512:(tb + 1) * 512], gnorm_col,
                ap2(gsil_h, 0, 128, [[1, 512]], tb * 512), ALU.mult, ALU.mult),
                waits=[ev_g, ev_rb])
            ev_epi_last = bld.op("dv", lambda e, tb=tb: e.tensor_mul(
                Y_h[:, tb * 512:(tb + 1) * 512], t5w[:], psf[4][:]),
                waits=[pc["Y"]] if tb == 0 else [], inc="dv")
        pc["epi"] = ev_epi_last
        ev_ys = bld.dma("sy", y_s[h * 128:(h + 1) * 128, :], Y_h.ap(), waits=[ev_epi_last],
                        sem="d_y")
        pc["Y"] = ev_ys
        if stn is not None:
            evqg[h + 1] = proj_finish(stn)
            ev_cols_h[h + 1] = emit_head_cols(h + 1, pre=False)
    bld.barrier()
    s2.close()
    scope_m.close()
    dbg_dram("y_s", y_s, [H * D, T], BF16)

    if stages < 4:
        return finish(nc, bld, es, dbg_tensors)

    s3 = ExitStack()

    def s3b(name, shape, dt=F32):
        return s3.enter_context(nc.sbuf_tensor(name, shape, dt))

    Y_all = s3b("Y_all", [128, H * T], BF16)
    wobuf = [s3b("wobuf0", [128, 32 * 128], BF16), s3b("wobuf1", [128, 32 * 128], BF16)]
    sqt = s3b("sqt", [128, T], BF16)
    rows = s3b("rows", [1, 4 * T], F32)
    ut = s3b("ut", [128, 512], F32)

    ev_yl = bld.dma("sy", ap2(Y_all, 0, 128, [[T, H], [1, T]], 0),
                    y_s.rearrange("(h p) t -> p h t", p=128), sem="d_y")

    def emit_ln(xsrc_sl, dma_out, out_dt_bf=True):
        pass

    ev_w_prev2 = {}
    ev_evac2 = {}
    for mc in range(32):
        par = mc % 2
        w = [ev_w_prev2[par]] if par in ev_w_prev2 else [bld.now("pe")]
        evw = emit_wcol_load(wobuf[par], woT_h, mc, 128, waits=w, par=par)
        for tb in range(2):
            bank = psf[tb]
            for hc in range(32):
                ev = bld.op("pe", lambda e, par=par, hc=hc, tb=tb, bank=bank: e.matmul(
                    bank[:], ap2(wobuf[par], 0, 128, [[1, 128]], hc * 128),
                    ap2(Y_all, 0, 128, [[1, 512]], hc * T + tb * 512),
                    start=(hc == 0), stop=(hc == 31)),
                    waits=([evw, ev_yl] + ([ev_evac2.get(mc - 2)] if mc >= 2 else []))
                    if hc == 0 else [],
                    inc="pe" if hc == 31 else None)
            ev_w_prev2[par] = ev
            xsl = ap2(xT, 0, 128, [[1, 512]], mc * T + tb * 512)
            bld.op("dv", lambda e, xsl=xsl, bank=bank: e.tensor_add(xsl, xsl, bank[:]),
                   waits=[ev], inc="dv")
        ev_evac2[mc] = bld.now("dv")
    bld.barrier()

    for mc in range(32):
        xsl = ap2(xT, 0, 128, [[1, T]], mc * T)
        ev_s = bld.op("ac", lambda e, xsl=xsl: e.activation(sqt[:], xsl, AF.Square),
                      waits=[bld.now("pe")], inc="ac")
        for tb in range(2):
            bld.op("pe", lambda e, mc=mc, tb=tb: e.matmul(
                psf[2 + tb][0:1, :], onescol_bf[:],
                ap2(xT, 0, 128, [[1, 512]], mc * T + tb * 512),
                start=(mc == 0), stop=(mc == 31)),
                waits=[], inc="pe" if mc == 31 else None)
            bld.op("pe", lambda e, mc=mc, tb=tb: e.matmul(
                psf[4 + tb][0:1, :], onescol_bf[:], sqt[:, tb * 512:(tb + 1) * 512],
                start=(mc == 0), stop=(mc == 31)),
                waits=[ev_s], inc="pe" if mc == 31 else None)
    ev_stats = bld.now("pe")

    def emit_ln_rows(ev_stats):
        for tb in range(2):
            bld.op("dv", lambda e, tb=tb: e.tensor_scalar(
                rows[0:1, tb * 512:(tb + 1) * 512], psf[2 + tb][0:1, :],
                1.0 / HID, None, ALU.mult), waits=[ev_stats])
            bld.op("dv", lambda e, tb=tb: e.tensor_scalar(
                rows[0:1, T + tb * 512:T + (tb + 1) * 512], psf[4 + tb][0:1, :],
                1.0 / HID, None, ALU.mult), inc="dv")
        ev1 = bld.now("dv")
        bld.op("ac", lambda e: e.activation(rows[0:1, 2 * T:3 * T], rows[0:1, 0:T],
                                            AF.Square), waits=[ev1], inc="ac")
        ev2 = bld.now("ac")
        bld.op("dv", lambda e: e.tensor_sub(rows[0:1, T:2 * T], rows[0:1, T:2 * T],
                                            rows[0:1, 2 * T:3 * T]), waits=[ev2], inc="dv")
        ev3 = bld.now("dv")
        bld.op("ac", lambda e: e.activation(rows[0:1, 2 * T:3 * T], rows[0:1, T:2 * T],
                                            AF.Sqrt, bias=ccol(C_EPS, 1)),
               waits=[ev3], inc="ac")
        ev4 = bld.now("ac")
        bld.op("dv", lambda e: e.reciprocal(rows[0:1, 3 * T:4 * T], rows[0:1, 2 * T:3 * T]),
               waits=[ev4], inc="dv")
        return bld.now("dv")

    ev_rows = emit_ln_rows(ev_stats)

    ev_bc = []
    for tb in range(2):
        bld.op("pe", lambda e, tb=tb: e.matmul(
            psf[2 + tb][:], onesrow[:], rows[0:1, tb * 512:(tb + 1) * 512],
            start=True, stop=True), waits=[ev_rows])
        ev_bc.append(bld.op("pe", lambda e, tb=tb: e.matmul(
            psf[4 + tb][:], onesrow[:], rows[0:1, 3 * T + tb * 512:3 * T + (tb + 1) * 512],
            start=True, stop=True), inc="pe"))

    for mc in range(32):
        for tb in range(2):
            xsl = ap2(xT, 0, 128, [[1, 512]], mc * T + tb * 512)
            bld.op("dv", lambda e, xsl=xsl, tb=tb: e.tensor_sub(ut[:], xsl, psf[2 + tb][:]),
                   waits=[ev_bc[tb]])
            bld.op("dv", lambda e, xsl=xsl, tb=tb: e.tensor_mul(xsl, ut[:], psf[4 + tb][:]),
                   inc="dv")
        bld.dma("sy", xm_s[mc * 128:(mc + 1) * 128, :],
                ap2(xT, 0, 128, [[1, T]], mc * T), waits=[bld.now("dv")], sem="d_xm")
    bld.barrier()
    s3.close()
    dbg_dram("xm_s", xm_s, [HID, T], BF16)

    if stages < 5:
        return finish(nc, bld, es, dbg_tensors)

    s4 = ExitStack()

    def s4b(name, shape, dt=F32):
        return s4.enter_context(nc.sbuf_tensor(name, shape, dt))

    pk1b = [s4b("pk1b0", [128, 4096], I8), s4b("pk1b1", [128, 4096], I8)]
    pk3b = [s4b("pk3b0", [128, 4096], I8), s4b("pk3b1", [128, 4096], I8)]
    sc1b = [s4b("sc1b0", [2, 4096], BF16), s4b("sc1b1", [2, 4096], BF16)]
    sc3b = [s4b("sc3b0", [2, 4096], BF16), s4b("sc3b1", [2, 4096], BF16)]
    W1c = [s4b("W1c0", [128, 4096], BF16), s4b("W1c1", [128, 4096], BF16)]
    W3c = [s4b("W3c0", [128, 4096], BF16), s4b("W3c1", [128, 4096], BF16)]
    asb = s4b("asb", [128, 512], BF16)
    hst = s4b("hst", [128, T], BF16)
    fence4 = s4b("fence4", [1, 4], F32)
    scsb = s4b("scsb", [128, 8192], BF16)

    ev_mm_ob = {}
    ev_stt_ob = {}
    ev_hdma = {}
    ev_ext_prev = {None: None}

    def dequant_start(ob):
        par = ob % 2
        wfree = ev_mm_ob.get(ob - 2)
        st = {"q": [], "stt": {}, "evac": {}, "si": 0}
        for wi, pk, sc, Wc, pk_h2, sc_h2 in (("w1", pk1b, sc1b, W1c, w1d_h, s1re_h),
                                             ("w3", pk3b, sc3b, W3c, w3d_h, s3re_h)):
            evp = bld.dma("sy", pk[par].ap(), pk_h2[ob * 128:(ob + 1) * 128, :],
                          waits=[wfree], sem=f"d_s4_{par}")
            evs = bld.dma("gp", sc[par].ap(), sc_h2[ob * 2:(ob + 1) * 2, :],
                          waits=[bld.now("pe")], sem=f"d_s4s_{par}")
            eve = bld.op("dv", lambda e, par=par, pk=pk: e.tensor_scalar(
                pk[par][:], pk[par][:], maski[:], xori[:],
                ALU.bitwise_and, ALU.bitwise_xor), waits=[evp], inc="dv")
            half = 0 if wi == "w1" else 1
            for sl in range(8):
                def mk(sl=sl, par=par, sc=sc, wi=wi, half=half, evs=evs, st=st):
                    si = st["si"]
                    st["si"] += 1
                    bank = psf[4 + si % 3]
                    evm = bld.op("pe", lambda e, bank=bank: e.matmul(
                        bank[:], sel2b[:], sc[par][0:2, sl * 512:(sl + 1) * 512],
                        start=True, stop=True),
                        waits=[evs, st["evac"].get(si - 3)], inc="pe")
                    off2 = half * 4096
                    st["evac"][si] = bld.op("ac", lambda e, bank=bank: e.copy(
                        scsb[:, off2 + sl * 512:off2 + (sl + 1) * 512], bank[:]),
                        waits=[evm] + ([ev_free_sc.get(wi)] if sl == 0 else []),
                        inc="ac")
                st["q"].append(mk)

            def mk_stt(wi=wi, par=par, pk=pk, Wc=Wc, eve=eve,
                       wfree=wfree, half=half, st=st):
                off2 = half * 4096
                ev = bld.op("dv", lambda e: e.scalar_tensor_tensor(
                    Wc[par][:], pk[par][:], off_col,
                    scsb[:, off2:off2 + 4096], ALU.subtract, ALU.mult),
                    waits=[eve, st["evac"][st["si"] - 1], wfree], inc="dv")
                st["stt"][wi] = ev
                ev_free_sc[wi] = ev
            st["q"].append(mk_stt)
        return st

    def dequant_drain(st, k):
        for _ in range(k):
            if st["q"]:
                st["q"].pop(0)()

    ev_free_sc = {}
    ev_stt_last = None
    dst = {0: dequant_start(0)}
    dequant_drain(dst[0], 999)
    dst[1] = dequant_start(1)
    for ob in range(NOB):
        par = ob % 2
        dequant_drain(dst[ob], 999)
        e1f = dst[ob]["stt"]["w1"]
        e3f = dst[ob]["stt"]["w3"]
        nxt = dst.get(ob + 1)
        evab = {}
        mmi = 0
        for tb in range(2):
            for j in range(32):
                w = []
                if j == 0:
                    w = [e1f, e3f, bld.now("ac"), ev_hdma.get(ob - 1)]
                evab[("a", tb)] = bld.op("pe", lambda e, tb=tb, j=j, par=par: e.matmul(
                    psf[tb][:], ap2(W1c[par], 0, 128, [[1, 128]], j * 128),
                    xT_sl(j, tb * 512, 512),
                    start=(j == 0), stop=(j == 31)),
                    waits=w, inc="pe" if j == 31 else None)
                mmi += 1
                if nxt is not None and mmi % 4 == 0:
                    dequant_drain(nxt, 1)
            for j in range(32):
                evab[("b", tb)] = bld.op("pe", lambda e, tb=tb, j=j, par=par: e.matmul(
                    psf[2 + tb][:], ap2(W3c[par], 0, 128, [[1, 128]], j * 128),
                    xT_sl(j, tb * 512, 512),
                    start=(j == 0), stop=(j == 31)),
                    waits=[], inc="pe" if j == 31 else None)
                mmi += 1
                if nxt is not None and mmi % 4 == 0:
                    dequant_drain(nxt, 1)
        ev_mm_ob[ob] = evab[("b", 1)]
        if ob + 2 < NOB:
            dst[ob + 2] = dequant_start(ob + 2)
        for tb in range(2):
            eva = bld.op("ac", lambda e, tb=tb: e.activation(asb[:], psf[tb][:], AF.Silu),
                         waits=[evab[("a", tb)], bld.now("dv")], inc="ac")
            bld.op("dv", lambda e, tb=tb: e.tensor_mul(
                hst[:, tb * 512:(tb + 1) * 512], asb[:], psf[2 + tb][:]),
                waits=[eva, evab[("b", tb)], ev_hdma.get(ob - 1)], inc="dv")
        ev_hdma[ob] = bld.dma("sy", h_s[ob * 128:(ob + 1) * 128, :], hst.ap(),
                              waits=[bld.now("dv")], sem="d_hs")
    bld.barrier()
    s4.close()
    dbg_dram("h_s", h_s, [NOB * 128, T], BF16)

    if stages < 6:
        return finish(nc, bld, es, dbg_tensors)

    s5 = ExitStack()

    def s5b(name, shape, dt=F32):
        return s5.enter_context(nc.sbuf_tensor(name, shape, dt))

    MG = [2] * 16
    W2c = [s5b(f"W2c{i}", [128, FFH], BF16) for i in range(2)]
    pk2b = s5b("pk2b", [128, FFH], I8)
    ss2 = s5b("ss2", [2, 2560], BF16)
    scsb5 = s5b("scsb5", [128, FFH], BF16)
    hb = [s5b(f"hb{i}", [128, 2 * T], BF16) for i in range(4)]
    fence5 = s5b("fence5", [1, 4], F32)
    xmt = [s5b("xmt0", [128, 512], BF16), s5b("xmt1", [128, 512], BF16)]
    r2st = [s5b("r2st0", [128, 512], BF16), s5b("r2st1", [128, 512], BF16)]

    mg_base = 0
    ev_mm_done = None
    ev_r2st = {}
    ev_stt5 = None
    for gi, gsz in enumerate(MG):
        ev_stts = []
        for mb4 in range(gsz):
            mb = mg_base + mb4
            evp = bld.dma("sy", pk2b.ap(), w2d_h[mb * 128:(mb + 1) * 128, :],
                          waits=[bld.now("gp"), bld.now("dv"), ev_mm_done], sem="d_s5")
            eve = bld.op("dv", lambda e: e.tensor_scalar(
                pk2b[:], pk2b[:], maski[:], xori[:],
                ALU.bitwise_and, ALU.bitwise_xor), waits=[evp], inc="dv")
            sl0 = 0
            ev_evac = {}
            si = 0
            for ch in range(5):
                cw = 2560 if ch < 4 else FFH - 4 * 2560
                evs = bld.dma("gp", ss2[0:2, 0:cw],
                              s2re_h[mb * 2:(mb + 1) * 2, sl0:sl0 + cw],
                              waits=[bld.now("ac")], sem="d_s5s")
                nmm = (cw + 511) // 512
                for q2 in range(nmm):
                    qw = min(512, cw - q2 * 512)
                    bank = psf[4 + si % 3]
                    evm = bld.op("pe", lambda e, q2=q2, qw=qw, bank=bank: e.matmul(
                        bank[:, 0:qw], sel2b[:], ss2[0:2, q2 * 512:q2 * 512 + qw],
                        start=True, stop=True),
                        waits=[evs, ev_evac.get(si - 3)], inc="pe")
                    ev_evac[si] = bld.op("ac", lambda e, sl0=sl0, q2=q2, qw=qw, bank=bank:
                                         e.copy(scsb5[:, sl0 + q2 * 512:sl0 + q2 * 512 + qw],
                                                bank[:, 0:qw]),
                                         waits=[evm] + ([ev_stt5] if si == 0 else []),
                                         inc="ac")
                    si += 1
                sl0 += cw
            ev_stt5 = bld.op("dv", lambda e, mb4=mb4: e.scalar_tensor_tensor(
                W2c[mb4][:], pk2b[:], off_col, scsb5[:], ALU.subtract, ALU.mult),
                waits=[eve, ev_evac[si - 1], ev_mm_done], inc="dv")
            ev_stts.append(ev_stt5)
        ev_deq = bld.now("dv")
        ev_h = {}
        ev_j = {}
        ev_last = None
        for jp in range(NOB // 2):
            buf = jp % 4
            ev_h[jp] = bld.dma("sy", ap2(hb[buf], 0, 128, [[T, 2], [1, T]], 0),
                               h_s.rearrange("(a p) t -> p a t", p=128)[:, jp * 2:(jp + 1) * 2, :],
                               waits=([ev_j.get(2 * (jp - 4) + 1)] if jp >= 4
                                      else [bld.now("pe")])
                               + ([("d_hs", 16 * NOB)] if jp < 4 else []),
                               sem=f"d_h{buf}")
            for sub in range(2):
                j = jp * 2 + sub
                for mb4 in range(gsz):
                    for tb in range(2):
                        ev_last = bld.op("pe", lambda e, j=j, sub=sub, mb4=mb4, tb=tb,
                                         buf=buf: e.matmul(
                            psf[mb4 * 2 + tb][:],
                            ap2(W2c[mb4], 0, 128, [[1, 128]], j * 128),
                            apview(hb[buf].ap(), 0, 128, sub * T + tb * 512, [[1, 512]]),
                            start=(j == 0), stop=(j == NOB - 1)),
                            waits=[ev_h[jp], ev_deq] if (sub == 0 and mb4 == 0 and tb == 0)
                            else [],
                            inc="pe" if (mb4 == gsz - 1 and tb == 1) else None)
                ev_j[j] = ev_last
        ev_mm_done = ev_last
        for mb4 in range(gsz):
            mb = mg_base + mb4
            for tb in range(2):
                evx = bld.dma("sy", xmt[tb].ap(),
                              xm_s[mb * 128:(mb + 1) * 128, tb * 512:(tb + 1) * 512],
                              waits=[bld.now("dv")]
                              + ([("d_xm", 16 * 32)] if gi == 0 and mb4 == 0 else []),
                              sem=f"d_xt{tb}")
                bld.op("dv", lambda e, mb4=mb4, tb=tb: e.tensor_add(
                    r2st[tb][:], xmt[tb][:], psf[mb4 * 2 + tb][:]),
                    waits=[evx, ev_mm_done, ev_r2st.get(tb)], inc="dv")
                ev_r2st[tb] = bld.dma("sy",
                        r2_s[mb * 128:(mb + 1) * 128, tb * 512:(tb + 1) * 512],
                        r2st[tb].ap(), waits=[bld.now("dv")], sem="d_r2")
        mg_base += gsz
    bld.barrier()
    s5.close()
    dbg_dram("r2_s", r2_s, [HID, T], BF16)

    s6 = ExitStack()

    def s6b(name, shape, dt=F32):
        return s6.enter_context(nc.sbuf_tensor(name, shape, dt))

    rt = [s6b("rt0", [128, T], BF16), s6b("rt1", [128, T], BF16)]
    sq6 = s6b("sq6", [128, T], BF16)
    ut6 = s6b("ut6", [128, 512], F32)
    ost = s6b("ost", [128, T], F32)
    rows6 = s6b("rows6", [1, 4 * T], F32)

    ev_rt = {}
    ev_cons = {}
    for mc in range(32):
        w = [bld.now("pe"), bld.now("ac")] if mc < 2 else [ev_cons[mc - 2]]
        ev_rt[mc] = bld.dma("sy", rt[mc % 2].ap(), r2_s[mc * 128:(mc + 1) * 128, :],
                            waits=w + ([("d_r2", 16 * 64)] if mc < 2 else []),
                            sem=f"d_rt{mc % 2}")
        ev_s = bld.op("ac", lambda e, mc=mc: e.activation(sq6[:], rt[mc % 2][:], AF.Square),
                      waits=[ev_rt[mc], bld.now("pe")], inc="ac")
        for tb in range(2):
            bld.op("pe", lambda e, mc=mc, tb=tb: e.matmul(
                psf[2 + tb][0:1, :], onescol_bf[:],
                apview(rt[mc % 2].ap(), 0, 128, tb * 512, [[1, 512]]),
                start=(mc == 0), stop=(mc == 31)),
                waits=[ev_rt[mc]], inc=None)
            ev_cons[mc] = bld.op("pe", lambda e, mc=mc, tb=tb: e.matmul(
                psf[4 + tb][0:1, :], onescol_bf[:], sq6[:, tb * 512:(tb + 1) * 512],
                start=(mc == 0), stop=(mc == 31)),
                waits=[ev_s], inc="pe")
    ev_stats6 = bld.now("pe")

    for tb in range(2):
        bld.op("dv", lambda e, tb=tb: e.tensor_scalar(
            rows6[0:1, tb * 512:(tb + 1) * 512], psf[2 + tb][0:1, :],
            1.0 / HID, None, ALU.mult), waits=[ev_stats6])
        bld.op("dv", lambda e, tb=tb: e.tensor_scalar(
            rows6[0:1, T + tb * 512:T + (tb + 1) * 512], psf[4 + tb][0:1, :],
            1.0 / HID, None, ALU.mult), inc="dv")
    bld.op("ac", lambda e: e.activation(rows6[0:1, 2 * T:3 * T], rows6[0:1, 0:T],
                                        AF.Square), waits=[bld.now("dv")], inc="ac")
    bld.op("dv", lambda e: e.tensor_sub(rows6[0:1, T:2 * T], rows6[0:1, T:2 * T],
                                        rows6[0:1, 2 * T:3 * T]),
           waits=[bld.now("ac")], inc="dv")
    bld.op("ac", lambda e: e.activation(rows6[0:1, 2 * T:3 * T], rows6[0:1, T:2 * T],
                                        AF.Sqrt, bias=ccol(C_EPS, 1)),
           waits=[bld.now("dv")], inc="ac")
    bld.op("dv", lambda e: e.reciprocal(rows6[0:1, 3 * T:4 * T], rows6[0:1, 2 * T:3 * T]),
           waits=[bld.now("ac")], inc="dv")
    ev_rows6 = bld.now("dv")

    ev_out = None
    ev_bc6 = []
    for tb in range(2):
        bld.op("pe", lambda e, tb=tb: e.matmul(
            psf[2 + tb][:], onesrow[:], rows6[0:1, tb * 512:(tb + 1) * 512],
            start=True, stop=True), waits=[ev_rows6])
        ev_bc6.append(bld.op("pe", lambda e, tb=tb: e.matmul(
            psf[4 + tb][:], onesrow[:], rows6[0:1, 3 * T + tb * 512:3 * T + (tb + 1) * 512],
            start=True, stop=True), inc="pe"))

    for mc in range(32):
        evr = bld.dma("sy", rt[mc % 2].ap(), r2_s[mc * 128:(mc + 1) * 128, :],
                      waits=[bld.now("dv"), ev_bc6[1]], sem=f"d_rt{mc % 2}")
        for tb in range(2):
            bld.op("dv", lambda e, mc=mc, tb=tb: e.tensor_sub(
                ut6[:], apview(rt[mc % 2].ap(), 0, 128, tb * 512, [[1, 512]]),
                psf[2 + tb][:]), waits=[evr, ev_out])
            bld.op("dv", lambda e, mc=mc, tb=tb: e.tensor_mul(
                ost[:, tb * 512:(tb + 1) * 512], ut6[:], psf[4 + tb][:]), inc="dv")
        ev_out = bld.dma("sy", outT_h[mc * 128:(mc + 1) * 128, :], ost[:],
                         waits=[bld.now("dv")], sem="sto")

    dbg("rows6", rows6.ap(), [1, 4 * T])
    s6.close()
    return finish(nc, bld, es, dbg_tensors)


def finish(nc, bld, es, dbg_tensors):
    bld.op("sy", lambda e: e.nop(), waits=bld.dma_sems_now())
    blk = es.enter_context(nc.Block())
    bld.play(blk)
    es.close()
    return nc, dbg_tensors


def rope_tables():
    inv = 1.0 / (10000.0 ** (np.arange(0, D, 2, dtype=np.float64) / D))
    ang = np.arange(2048, dtype=np.float64)[:, None] * inv[None, :]
    cos = np.cos(ang)
    cos_full = np.concatenate([cos, cos], axis=1).T.astype(np.float32)
    sin64 = np.sin(ang).T.astype(np.float32)
    return np.ascontiguousarray(cos_full), np.ascontiguousarray(sin64)


def pack_w(w_packed_i32, out_f, in_f):
    b = w_packed_i32.view(np.int8).reshape(-1, 64, 4)[:, :, 0]
    nob = out_f // 128
    nj = in_f // 128
    t = b.reshape(out_f, nj, 64).reshape(nob, 128, nj, 64)
    t = np.transpose(t, (0, 3, 2, 1))
    dup = np.concatenate([t, t], axis=1)
    return np.ascontiguousarray(dup.reshape(nob * 128, nj * 128))


def pack_s(s_flat, out_f, in_f):
    nob = out_f // 128
    nj = in_f // 128
    t = s_flat.reshape(out_f, nj, 2).reshape(nob, 128, nj, 2)
    t = np.transpose(t, (0, 3, 2, 1))
    return np.ascontiguousarray(t.reshape(nob * 2, nj * 128).astype(np.float32))


def make_shared(inp):
    cos_full, sin64 = rope_tables()
    maskneg = np.where(np.arange(128)[None, :] >= np.arange(128)[:, None],
                       0.0, -1e30).astype(np.float32)
    sel2 = np.zeros((2, 128), dtype=np.float32)
    sel2[0, :64] = 1.0 / 16.0
    sel2[1, 64:] = 1.0
    sh = {
        "wqT": np.ascontiguousarray(inp["L0_attn_Wq"].T),
        "wkT": np.ascontiguousarray(inp["L0_attn_Wk"].T),
        "wvT": np.ascontiguousarray(inp["L0_attn_Wv"].T),
        "wgT": np.ascontiguousarray(inp["L0_attn_Wg"].T),
        "woT": np.ascontiguousarray(inp["L0_attn_Wo"].T),
        "winT": np.ascontiguousarray(inp["L0_attn_Win"].T),
        "dtb": np.ascontiguousarray(inp["L0_attn_dt_bias"].reshape(H, 1)),
        "alog": np.ascontiguousarray(inp["L0_attn_A_log_bias"].reshape(H, 1)),
        "gnorm": np.ascontiguousarray(inp["L0_attn_gnorm_w"].reshape(D, 1)),
        "maskneg": maskneg,
        "ident": np.eye(128, dtype=np.float32),
        "sel2": sel2,
        "w1d": pack_w(inp["L0_ffn_w1"], FFH, HID),
        "w3d": pack_w(inp["L0_ffn_w3"], FFH, HID),
        "w2d": pack_w(inp["L0_ffn_w2"], HID, FFH),
        "s1re": pack_s(inp["L0_ffn_s1"], FFH, HID),
        "s3re": pack_s(inp["L0_ffn_s3"], FFH, HID),
        "s2re": pack_s(inp["L0_ffn_s2"], HID, FFH),
        "_cos": cos_full, "_sin": sin64,
    }
    return sh


def make_core_inputs(inp, sh, core):
    b, half = core // 2, core % 2
    x = inp["L0_attn_hidden_states"]
    xm = np.ascontiguousarray(x[b, half * T:(half + 1) * T, :].T.astype(np.float32))
    if half == 1:
        xp = np.ascontiguousarray(x[b, 0:T, :].T.astype(np.float32))
    else:
        xp = np.zeros((HID, T), dtype=np.float32)
    off = half * T
    d = {k: v for k, v in sh.items() if not k.startswith("_")}
    d.update({
        "xmain": xm, "xpre": xp,
        "cosm": np.ascontiguousarray(sh["_cos"][:, off:off + T]),
        "sinm": np.ascontiguousarray(sh["_sin"][:, off:off + T]),
        "cosp": np.ascontiguousarray(sh["_cos"][:, 0:T]),
        "sinp": np.ascontiguousarray(sh["_sin"][:, 0:T]),
    })
    return d

import concourse.bass_utils as _bass_utils


def kernel(**inputs):
    inputs = {k: np.asarray(v) for k, v in inputs.items()}
    sh = make_shared(inputs)
    in_maps = [make_core_inputs(inputs, sh, c) for c in range(8)]
    nc, _ = build()
    res = _bass_utils.run_bass_kernel_spmd(nc, in_maps, core_ids=list(range(8)))
    out = np.zeros((4, 2048, 4096), np.float32)
    for c in range(8):
        b, half = c // 2, c % 2
        out[b, half * T:(half + 1) * T, :] = np.asarray(res.results[c]["outT"]).T
    return out
